# version 08
# baseline (speedup 1.0000x reference)
import numpy as np
import concourse.bass as bass
import concourse.mybir as mybir
from concourse.bass_utils import run_bass_kernel_spmd

F32 = mybir.dt.float32
BF16 = mybir.dt.bfloat16
AF = mybir.ActivationFunctionType
ALU = mybir.AluOpType
AX = mybir.AxisListType

DEBUG = False
B, S, D = 4, 4096, 1024
H, HD = 16, 64
NCORES = 8
TOK = 2048
NIC = 8
MAGIC = 12582912.0

ENGINES = ("tensor", "vector", "scalar", "gpsimd", "sync")

QUEUES = {
    "gd": ("gpsimd", 10),
    "pk": ("gpsimd", 1),
    "sd": ("sync", 10),
    "vt": ("sync", 4),
    "sp": ("sync", 1),
    "cd": ("scalar", 6),
}


class Builder:
    def __init__(self, nc):
        self.nc = nc
        self.ops = {e: [] for e in ENGINES}
        self.cnt = {}
        self.sems = {}
        for name in ("tm", "vv", "aa", "gg"):
            self.sems[name] = nc.alloc_semaphore(name=name)
            self.cnt[name] = 0
        self.qpool = {}
        self.qidx = {}
        for q, (eng, n) in QUEUES.items():
            self.qpool[q] = []
            self.qidx[q] = 0
            for i in range(n):
                nm = f"{q}{i}"
                self.sems[nm] = nc.alloc_semaphore(name=nm)
                self.cnt[nm] = 0
                self.qpool[q].append(nm)

    def emit(self, eng, fn=None, waits=(), inc=None):
        sems = self.sems
        merged = {}
        for w in waits:
            if w is None:
                continue
            s, v = w
            if v is None:
                continue
            merged[s] = max(merged.get(s, 0), v)
        wl = sorted(merged.items())
        if inc is not None:
            self.cnt[inc[0]] += inc[1]
        post = self.cnt[inc[0]] if inc is not None else None

        def run(e):
            for s, v in wl:
                e.wait_ge(sems[s], v)
            if fn is not None:
                instr = fn(e)
                if inc is not None:
                    instr.then_inc(sems[inc[0]], inc[1])
        self.ops[eng].append(run)
        return (inc[0], post) if inc is not None else None

    def dma(self, queue, out, in_, waits=(), transpose=False):
        eng = QUEUES[queue][0]
        pool = self.qpool[queue]
        nm = pool[self.qidx[queue] % len(pool)]
        self.qidx[queue] += 1
        w = list(waits)
        if self.cnt[nm] > 0:
            w.append((nm, self.cnt[nm]))
        return self.emit(eng, lambda e: e.dma_start(out=out, in_=in_, transpose=transpose),
                         waits=w, inc=(nm, 16))

    def qall(self, queue):
        return [(nm, self.cnt[nm]) for nm in self.qpool[queue] if self.cnt[nm] > 0]

    def finalize(self):
        nc = self.nc
        block_cm = nc.Block()
        block = block_cm.__enter__()
        for eng in ENGINES:
            opslist = self.ops[eng]
            getattr(block, eng)(lambda e, _l=opslist: [f(e) for f in _l])
        block_cm.__exit__(None, None, None)


class Guard:
    def __init__(self, n):
        self.t = [None] * n

    def wait(self, i):
        return self.t[i % len(self.t)]

    def all(self):
        return [t for t in self.t if t is not None]

    def set(self, i, tick):
        self.t[i % len(self.t)] = tick


def build_program():
    nc = bass.Bass("TRN2", target_bir_lowering=False, num_devices=NCORES)

    def inp(name, shape, dtype=F32):
        return nc.dram_tensor(name, shape, dtype, kind="ExternalInput").ap()

    xkv = inp("xkv", [NIC, 128, S])
    wqT = inp("wqT", [NIC, 128, D])
    wkT = inp("wkT", [NIC, 128, D])
    wvT = inp("wvT", [NIC, 128, D])
    woT = inp("woT", [NIC, 128, D])
    bq = inp("bq", [128, NIC])
    bk = inp("bk", [128, NIC])
    bv = inp("bv", [128, NIC])
    bo = inp("bo", [128, NIC])
    wqsT = inp("wqsT", [64, 3])
    wksT = inp("wksT", [64, 3])
    wvsT = inp("wvsT", [64, 3])
    bqs = inp("bqs", [3])
    bks = inp("bks", [3])
    bvs = inp("bvs", [3])
    temp = inp("temp", [1])
    w1T = inp("w1T", [32, NIC, 128, 128])
    w2T = inp("w2T", [32, NIC, 128, 128])
    h1T = inp("h1T", [32, 32, 128, 128])
    h2T = inp("h2T", [32, 32, 128, 128])
    w3T = inp("w3T", [NIC, 32, 128, 128])
    s1 = inp("s1", [128, 32])
    s2 = inp("s2", [128, 32])
    h1s = inp("h1s", [128, 32])
    h2s = inp("h2s", [128, 32])
    s3 = inp("s3", [128, NIC])
    l1wqT = inp("l1wqT", [NIC, 128, 64])
    l1wkT = inp("l1wkT", [NIC, 128, 64])
    l1wvT = inp("l1wvT", [NIC, 128, 64])
    l1woT = inp("l1woT", [64, D])
    wgate = inp("wgate", [24, 128, D])
    wup = inp("wup", [24, 128, D])
    wdown = inp("wdown", [NIC, 128, 3072])
    pmask = inp("pmask", [128, 2])

    out = nc.dram_tensor("out", [D, TOK], F32, kind="ExternalOutput").ap()
    dbg = {}
    dbg_ticks = []
    if DEBUG:
        for nm, shape, dt in (("d_kpair", [128, S], BF16), ("d_vpair", [128, S], BF16),
                              ("d_qpair", [128, TOK], BF16), ("d_stpK", [128, S], BF16),
                              ("d_stpQ", [128, TOK], BF16), ("d_attn", [H, 64, TOK], BF16),
                              ("d_y1", [NIC, 128, TOK], F32), ("d_y2", [NIC, 128, TOK], F32),
                              ("d_qf", [128, TOK], BF16), ("d_kf", [128, TOK], BF16),
                              ("d_vf", [128, TOK], BF16), ("d_num", [128, TOK], F32),
                              ("d_st", [128, 80], F32), ("d_outn", [128, TOK], BF16),
                              ("d_y3", [NIC, 128, TOK], F32),
                              ("d_y3p", [NIC, 128, 512], F32), ("d_sc", [128, 16], F32),
                              ("d_xq", [NIC, 128, TOK], BF16), ("d_gu", [2, 128, TOK], BF16),
                              ("d_wq", [NIC, 128, 128], BF16),
                              ("d_oq", [24, 128, 512], BF16), ("d_y4p", [NIC, 128, 512], F32),
                              ("d_ps", [128, 512], F32), ("d_y3ld", [NIC, 128, 512], F32)):
            dbg[nm] = nc.dram_tensor(nm, shape, dt, kind="ExternalOutput").ap()

    scr_row = nc.dram_tensor("scr_row", [8192], F32, kind="Internal").ap()
    scr_l = nc.dram_tensor("scr_l", [TOK], F32, kind="Internal").ap()
    y_park = nc.dram_tensor("y_park", [3, NIC, 128, TOK], F32, kind="Internal").ap()
    attn_park = nc.dram_tensor("attn_park", [H, 64, TOK], BF16, kind="Internal").ap()
    gu_park = nc.dram_tensor("gu_park", [2, 24, 128, TOK], BF16, kind="Internal").ap()
    hc_w1 = nc.dram_tensor("hc_w1", [32, NIC, 128, 128], BF16, kind="Internal").ap()
    hc_w2 = nc.dram_tensor("hc_w2", [32, NIC, 128, 128], BF16, kind="Internal").ap()
    hc_h1 = nc.dram_tensor("hc_h1", [32, 32, 128, 128], BF16, kind="Internal").ap()
    hc_h2 = nc.dram_tensor("hc_h2", [32, 32, 128, 128], BF16, kind="Internal").ap()
    hc_w3 = nc.dram_tensor("hc_w3", [NIC, 32, 128, 128], BF16, kind="Internal").ap()
    wq_park = nc.dram_tensor("wq_park", [3, 24, NIC * 3, 128, 128], BF16, kind="Internal").ap()
    wqn_park = nc.dram_tensor("wqn_park", [3, 24, 128, 3072], BF16, kind="Internal").ap()
    cc_in = [nc.dram_tensor(f"cc_in{i}", [4], F32, kind="Internal").ap() for i in range(4)]
    cc_out = [nc.dram_tensor(f"cc_out{i}", [4], F32, kind="Internal").ap() for i in range(4)]
    st_in = nc.dram_tensor("st_in", [64 * 80], F32, kind="Internal").ap()
    st_out = nc.dram_tensor("st_out", [64 * 80], F32, kind="Internal").ap()
    scr_pm = nc.dram_tensor("scr_pm", [128], F32, kind="Internal").ap()

    b = Builder(nc)
    A = nc.alloc_sbuf_tensor
    P = nc.alloc_psum_tensor

    AR0 = A("AR0", [128, 32768], BF16)
    STR = A("STR", [128, 16384], BF16)
    AR2 = A("AR2", [128, 16384], BF16)
    AR3 = A("AR3", [128, 8192], BF16)
    AR4 = A("AR4", [128, 16384], BF16)
    ARF = A("ARF", [128, 6144], F32)
    mcols = A("mcols", [128, 16], F32)
    mcb = A("mcb", [128, 128], BF16)
    mtr = A("mtr", [128, 128], BF16)
    smalls = A("smalls", [128, 256], F32)
    smallw = A("smallw", [128, 64], BF16)
    ident = A("ident", [128, 128], BF16)
    tri32 = A("tri32", [128, 128], F32)
    pmsk = A("pmsk", [128, 2], F32)
    onescol = A("onescol", [128, 1], BF16)

    res_h = nc.alloc_sbuf_tensor_at("res_f", [128, NIC, TOK], F32,
                                    offset=nc.lookup_mloc(AR0).addr)
    res_f = res_h.ap()

    psA = P("psA", [128, 2048], F32)
    psB = P("psB", [128, 2048], F32)

    xkv_b = AR0[:].rearrange("p (c t) -> p c t", c=NIC)
    stream_b = STR[:].rearrange("p (c t) -> p c t", c=NIC)
    wq_s = AR2[:, 0:8192].rearrange("p (c o) -> p c o", c=NIC)
    wk_s = AR2[:, 8192:16384].rearrange("p (c o) -> p c o", c=NIC)
    wv_s = AR3[:, 0:8192].rearrange("p (c o) -> p c o", c=NIC)
    kpair = AR4[:, 0:4096]
    vpair = AR4[:, 4096:8192]
    qpair = AR4[:, 8192:10240]
    stpK = AR4[:, 10240:14336]
    stpQ = AR4[:, 14336:16384]
    at_buf = STR[:, 0:2048].rearrange("p (h j t) -> p h j t", h=2, j=2)
    vtm = STR[:, 2048:7168].rearrange("p (k c) -> p k c", k=32)
    VB = 80
    kk6 = ARF[:, 0:2048]
    lrow = ARF[:, 2048:4096]
    avstage = ARF[:, 0:2048]
    statw = ARF[:]

    c_bq, c_bk, c_bv, c_bo = 0, 8, 16, 24
    c_tmp, c_bks6, c_bvs6, c_bqs6 = 32, 33, 34, 35
    c_wks, c_wvs, c_wqs = 40, 43, 46
    c_s1, c_s2, c_h1s, c_h2s, c_s3 = 64, 96, 128, 160, 192

    g_io = b.emit("gpsimd", lambda g: g.iota(ARF[:, 0:128], [[1, 128]], base=0,
                                             channel_multiplier=-1,
                                             allow_small_or_imprecise_dtypes=True),
                  inc=("gg", 1))
    id_done = b.emit("vector", lambda v: v.tensor_scalar(ident[:], ARF[:, 0:128],
                                                         0.0, None, ALU.is_equal),
                     waits=[g_io], inc=("vv", 1))
    tri_done = b.emit("vector", lambda v: v.tensor_scalar(tri32[:], ARF[:, 0:128],
                                                          0.0, None, ALU.is_ge),
                      waits=[g_io], inc=("vv", 1))
    pm_ld = b.dma("sp", pmsk[:], pmask[:])
    ones_done = b.emit("vector", lambda v: v.memset(onescol[:], 1.0), inc=("vv", 1))
    mcb_init = b.emit("vector", lambda v: v.memset(mcb[:], 0.0), inc=("vv", 1))

    ld_xkv = b.dma("gd", xkv_b[:, :, :], xkv.rearrange("c p t -> p c t"))
    ld_wq = b.dma("gd", wq_s, wqT.rearrange("c p o -> p c o"))
    ld_wk = b.dma("gd", wk_s, wkT.rearrange("c p o -> p c o"))
    ld_wv = b.dma("gd", wv_s, wvT.rearrange("c p o -> p c o"))

    ld_b = None
    for col, bsrc in ((c_bq, bq), (c_bk, bk), (c_bv, bv), (c_bo, bo)):
        ld_b = b.dma("sp", smalls[:, col:col + NIC], bsrc[:])
    ld_tmp = b.dma("sp", smalls[:, c_tmp:c_tmp + 1][:, None, :],
                   temp[None, :].partition_broadcast(128))
    ld_t = None
    for col, bsrc in ((c_wks, wksT), (c_wvs, wvsT), (c_wqs, wqsT)):
        ld_t = b.dma("sp", smalls[0:64, col:col + 3], bsrc[:])
        ld_t = b.dma("sp", smalls[64:128, col:col + 3], bsrc[:])
    for base in (0, 64):
        for cc, bsrc, p1 in ((c_bks6, bks, True), (c_bvs6, bvs, False)):
            pa, pb_ = ((1, 2, 0), (2, 0, 1)) if p1 else ((2, 0, 1), (1, 2, 0))
            for j in range(3):
                ld_t = b.dma("sp", smalls[base + j:base + j + 1, cc:cc + 1],
                             bsrc[pa[j]:pa[j] + 1][:, None])
                ld_t = b.dma("sp", smalls[base + 3 + j:base + 4 + j, cc:cc + 1],
                             bsrc[pb_[j]:pb_[j] + 1][:, None])
        ld_t = b.dma("sp", smalls[base + 0:base + 3, c_bqs6:c_bqs6 + 1], bqs[:, None])
        ld_t = b.dma("sp", smalls[base + 3:base + 6, c_bqs6:c_bqs6 + 1], bqs[:, None])
    c_stg = 60
    ld_t = b.dma("sp", smalls[32:35, c_stg:c_stg + 1], bqs[:, None])
    for col, ssrc in ((c_s1, s1), (c_s2, s2), (c_h1s, h1s), (c_h2s, h2s), (c_s3, s3)):
        ld_t = b.dma("sp", smalls[:, col:col + (NIC if col == c_s3 else 32)], ssrc[:])
    wait_tiny = ld_t
    negs = b.emit("vector", lambda v: v.tensor_scalar(smalls[32:35, c_stg:c_stg + 1],
                                                      smalls[32:35, c_stg:c_stg + 1],
                                                      -1.0, None, ALU.mult),
                  waits=[wait_tiny], inc=("vv", 1))
    neg1 = b.dma("sp", smalls[3:6, c_bqs6:c_bqs6 + 1], smalls[32:35, c_stg:c_stg + 1],
                 waits=[negs])
    neg2 = b.dma("sp", smalls[67:70, c_bqs6:c_bqs6 + 1], smalls[32:35, c_stg:c_stg + 1],
                 waits=[negs])

    extk = smallw[:, 0:12]
    extv = smallw[:, 12:24]
    extq = smallw[:, 24:36]
    zext = b.emit("vector", lambda v: v.memset(smallw[:, 0:40], 0.0), inc=("vv", 1))

    def build_ext(v):
        kp1 = (1, 2, 0)
        kp2 = (2, 0, 1)
        last = None
        for half, base in ((0, 0), (1, 64)):
            r = slice(base, base + 64)
            for j in range(3):
                co = half * 6
                last = v.tensor_copy(extk[r, co + j:co + j + 1],
                                     smalls[r, c_wks + kp1[j]:c_wks + kp1[j] + 1])
                last = v.tensor_copy(extk[r, co + 3 + j:co + 4 + j],
                                     smalls[r, c_wks + kp2[j]:c_wks + kp2[j] + 1])
                last = v.tensor_copy(extv[r, co + j:co + j + 1],
                                     smalls[r, c_wvs + kp2[j]:c_wvs + kp2[j] + 1])
                last = v.tensor_copy(extv[r, co + 3 + j:co + 4 + j],
                                     smalls[r, c_wvs + kp1[j]:c_wvs + kp1[j] + 1])
                last = v.tensor_copy(extq[r, co + j:co + j + 1],
                                     smalls[r, c_wqs + j:c_wqs + j + 1])
                last = v.tensor_scalar(extq[r, co + 3 + j:co + 4 + j],
                                       smalls[r, c_wqs + j:c_wqs + j + 1],
                                       -1.0, None, ALU.mult)
        return last
    ext_done = b.emit("vector", build_ext, waits=[wait_tiny, zext], inc=("vv", 1))

    stka = b.emit("vector", lambda v: v.memset(stpK[:], 0.0), inc=("vv", 1))
    stk0 = b.emit("vector", lambda v: v.memset(stpK[32:33, :], -1.0),
                  waits=[stka], inc=("vv", 1))
    b.dma("sp", stpK[6:7, :], stpK[32:33, :], waits=[stk0])
    stk_init = b.dma("sp", stpK[70:71, :], stpK[32:33, :], waits=[stk0])
    stq_init = b.emit("vector", lambda v: v.memset(stpQ[:], 0.0), inc=("vv", 1))
    vt0 = b.emit("vector", lambda v: v.memset(vtm[:], 0.0), inc=("vv", 1))
    vt1 = b.emit("vector", lambda v: v.memset(vtm[:, :, 64:65], 1.0),
                 waits=[vt0], inc=("vv", 1))
    vtm_init = b.emit("vector", lambda v: v.memset(vtm[:, :, VB + 64:VB + 65], 1.0),
                      waits=[vt1], inc=("vv", 1))

    pc_ticks = []
    pc_jobs = []
    for dst_t, src_t, n_oc in ((hc_w1, w1T, 32), (hc_w2, w2T, 32), (hc_h1, h1T, 32),
                               (hc_h2, h2T, 32), (hc_w3, w3T, NIC)):
        for o0 in range(0, n_oc, 2):
            pc_jobs.append((dst_t, src_t, o0))

    def emit_precast(n):
        while n > 0 and pc_jobs:
            dst_t, src_t, o0 = pc_jobs.pop(0)
            pc_ticks.append(b.dma("gd", dst_t[o0:o0 + 2], src_t[o0:o0 + 2]))
            n -= 1

    NT_S = S // 512
    NT_Q = TOK // 512
    gA = Guard(4)
    gB = Guard(4)
    av_guard = None
    vst_guard = None
    vtm_guard = None
    pair_sb_guard = None
    mc_guard = None
    c_done = None
    last_proj_tick = None

    for hp in range(8):
        emit_precast(11)
        hA, hB = 2 * hp, 2 * hp + 1
        kv_evac = q_evac = None
        for which, wsl, dst, bcol in ((0, wk_s, kpair, c_bk), (1, wv_s, vpair, c_bv),
                                      (2, wq_s, qpair, c_bq)):
            ntc = NT_S if which < 2 else NT_Q
            for tc in range(ntc):
                qi = tc % 4
                first_w = []
                if hp == 0 and which == 0 and tc == 0:
                    first_w = [ld_xkv, ld_wq, ld_wk, ld_wv]
                if hp > 0 and which == 0 and tc == 0:
                    first_w = [pair_sb_guard] + (dbg_ticks if hp == 1 else [])
                tick = None
                for ic in range(NIC):
                    tick = b.emit(
                        "tensor",
                        lambda t, wsl=wsl, ic=ic, hp=hp, tc=tc, qi=qi:
                        t.matmul(psA[:, 512 * qi:512 * qi + 512],
                                 wsl[:, ic, 128 * hp:128 * hp + 128],
                                 xkv_b[:, ic, 512 * tc:512 * tc + 512],
                                 start=(ic == 0), stop=(ic == NIC - 1)),
                        waits=(first_w + [gA.wait(qi)]) if ic == 0 else [],
                        inc=("tm", 1) if ic == NIC - 1 else None)
                last_proj_tick = tick
                if which < 2:
                    ev = b.emit("scalar",
                                lambda s, dst=dst, qi=qi, hp=hp, tc=tc, bcol=bcol:
                                s.activation(dst[:, 512 * tc:512 * tc + 512],
                                             psA[:, 512 * qi:512 * qi + 512],
                                             AF.Identity, bias=smalls[:, bcol + hp:bcol + hp + 1],
                                             scale=1.0),
                                waits=[tick, ld_b], inc=("aa", 1))
                    kv_evac = ev
                else:
                    ev = b.emit("vector",
                                lambda v, qi=qi, hp=hp, tc=tc, bcol=bcol:
                                v.tensor_scalar(qpair[:, 512 * tc:512 * tc + 512],
                                                psA[:, 512 * qi:512 * qi + 512],
                                                smalls[:, bcol + hp:bcol + hp + 1],
                                                smalls[:, c_tmp:c_tmp + 1],
                                                ALU.add, ALU.mult),
                                waits=[tick, ld_b, ld_tmp], inc=("vv", 1))
                    q_evac = ev
                gA.set(qi, ev)

        vt_done = None
        for kt in range(32):
            w0 = [kv_evac, vtm_init] if kt == 0 else []
            if kt == 0 and vtm_guard is not None:
                w0.append(vtm_guard)
            b.dma("vt", vtm[:, kt, 0:64], vpair[0:64, 128 * kt:128 * kt + 128],
                  waits=w0, transpose=True)
            vt_done = b.dma("vt", vtm[:, kt, VB:VB + 64], vpair[64:128, 128 * kt:128 * kt + 128],
                            transpose=True)

        stp_mul = None
        for sh in range(2):
            k_ev = {}
            vmul = {}
            for half, pbase in ((0, 0), (1, 64)):
                tp = (0, 64) if half == 1 else None
                for t4 in range(4):
                    tc = 4 * sh + t4
                    qi = t4
                    tick = b.emit(
                        "tensor",
                        lambda t, half=half, tc=tc, pbase=pbase, tp=tp, qi=qi:
                        t.matmul(psB[pbase:pbase + 6, 512 * qi:512 * qi + 512],
                                 extk[:, half * 6:half * 6 + 6],
                                 kpair[:, 512 * tc:512 * tc + 512],
                                 start=True, stop=True, tile_position=tp),
                        waits=([ext_done, kv_evac] if tc == 0 and sh == 0 else []) + [gB.wait(qi)]
                              + ([k_ev[(0, t4)]] if half == 1 else []),
                        inc=("tm", 1))
                    k_ev[(half, t4)] = b.emit(
                        "scalar",
                        lambda s, pbase=pbase, t4=t4, qi=qi:
                        s.activation(kk6[pbase:pbase + 6, 512 * t4:512 * t4 + 512],
                                     psB[pbase:pbase + 6, 512 * qi:512 * qi + 512],
                                     AF.Identity, bias=smalls[pbase:pbase + 6, c_bks6:c_bks6 + 1],
                                     scale=1.0),
                        waits=[tick, wait_tiny] +
                              ([stp_mul, vst_guard] if t4 == 0 and half == 0 else []),
                        inc=("aa", 1))
                    gB.set(qi, k_ev[(half, t4)])
            for half, pbase in ((0, 0), (1, 64)):
                tp = (0, 64) if half == 1 else None
                for t4 in range(4):
                    tc = 4 * sh + t4
                    qi = t4
                    tick = b.emit(
                        "tensor",
                        lambda t, half=half, tc=tc, pbase=pbase, tp=tp, qi=qi:
                        t.matmul(psB[pbase:pbase + 6, 512 * qi:512 * qi + 512],
                                 extv[:, half * 6:half * 6 + 6],
                                 vpair[:, 512 * tc:512 * tc + 512],
                                 start=True, stop=True, tile_position=tp),
                        waits=[gB.wait(qi)] + ([vmul[(0, t4)]] if half == 1 and (0, t4) in vmul else []),
                        inc=("tm", 1))
                    stp_mul = b.emit(
                        "vector",
                        lambda v, pbase=pbase, tc=tc, t4=t4, qi=qi:
                        v.scalar_tensor_tensor(stpK[pbase:pbase + 6, 512 * tc:512 * tc + 512],
                                               psB[pbase:pbase + 6, 512 * qi:512 * qi + 512],
                                               smalls[pbase:pbase + 6, c_bvs6:c_bvs6 + 1],
                                               kk6[pbase:pbase + 6, 512 * t4:512 * t4 + 512],
                                               ALU.add, ALU.mult),
                        waits=[tick, k_ev[(half, t4)], stk_init], inc=("vv", 1))
                    vmul[(half, t4)] = stp_mul
                    gB.set(qi, stp_mul)
        stq_ev = None
        qev = {}
        for half, pbase in ((0, 0), (1, 64)):
            tp = (0, 64) if half == 1 else None
            for tc in range(NT_Q):
                qi = tc % 4
                tick = b.emit(
                    "tensor",
                    lambda t, half=half, tc=tc, pbase=pbase, tp=tp, qi=qi:
                    t.matmul(psB[pbase:pbase + 6, 512 * qi:512 * qi + 512],
                             extq[:, half * 6:half * 6 + 6],
                             qpair[:, 512 * tc:512 * tc + 512],
                             start=True, stop=True, tile_position=tp),
                    waits=([q_evac, ext_done, stq_init] if tc == 0 and half == 0 else []) +
                          [gB.wait(qi)] + ([qev[(0, tc)]] if half == 1 else []),
                    inc=("tm", 1))
                stq_ev = b.emit(
                    "vector",
                    lambda v, pbase=pbase, tc=tc, qi=qi:
                    v.tensor_scalar(stpQ[pbase:pbase + 6, 512 * tc:512 * tc + 512],
                                    psB[pbase:pbase + 6, 512 * qi:512 * qi + 512],
                                    smalls[pbase:pbase + 6, c_bqs6:c_bqs6 + 1],
                                    smalls[pbase:pbase + 6, c_tmp:c_tmp + 1],
                                    ALU.add, ALU.mult),
                    waits=[tick, neg1, neg2, ld_tmp], inc=("vv", 1))
                qev[(half, tc)] = stq_ev
                gB.set(qi, stq_ev)

        ksub = STR[:, 7168:7424]
        ssub = STR[:, 7424:7680]
        kc1 = b.emit("vector", lambda v: v.tensor_copy(ksub[:, :], kpair[:, ::16]),
                     waits=[kv_evac] + ([c_done] if c_done else []), inc=("vv", 1))
        kc2 = b.emit("vector", lambda v: v.tensor_copy(ssub[:, :], stpK[:, ::16]),
                     waits=[stp_mul, kc1], inc=("vv", 1))
        for half, pbase in ((0, 0), (1, 64)):
            h = hA if half == 0 else hB
            red = None
            for qt in range(16):
                qi = qt % 4
                b.emit("tensor",
                       lambda t, pbase=pbase, qt=qt, qi=qi:
                       t.matmul(psB[:, 512 * qi:512 * qi + 256],
                                qpair[pbase:pbase + 64, 128 * qt:128 * qt + 128],
                                ksub[pbase:pbase + 64, :],
                                start=True, stop=False),
                       waits=([kc2, stq_ev] if qt == 0 and half == 0 else []) +
                             [gB.wait(qi)])
                tick = b.emit("tensor",
                              lambda t, pbase=pbase, qt=qt, qi=qi:
                              t.matmul(psB[:, 512 * qi:512 * qi + 256],
                                       stpQ[pbase:pbase + 6, 128 * qt:128 * qt + 128],
                                       ssub[pbase:pbase + 6, :],
                                       start=False, stop=True),
                              inc=("tm", 1))
                red = b.emit("vector",
                             lambda v, qt=qt, qi=qi:
                             v.tensor_reduce(mcols[:, qt:qt + 1],
                                             psB[:, 512 * qi:512 * qi + 256],
                                             AX.X, ALU.max),
                             waits=[tick] + ([mc_guard] if qt == 0 else []),
                             inc=("vv", 1))
                gB.set(qi, red)
            cst = b.emit("vector", lambda v: v.tensor_copy(mcb[:, 0:16], mcols[:, :]),
                         waits=[red, mcb_init] + ([mc_guard] if mc_guard else []),
                         inc=("vv", 1))
            tr = b.dma("vt", mtr[:, :], mcb[:, :],
                       waits=[cst] + ([c_done] if c_done else []), transpose=True)
            mc_guard = tr
            c_done = b.dma("sd", stpQ[pbase + 6:pbase + 7, :], mtr[0:16, :],
                           waits=[tr])

        if DEBUG and hp == 0:
            for nm, tens in (("d_kpair", kpair), ("d_vpair", vpair), ("d_qpair", qpair),
                             ("d_stpK", stpK), ("d_stpQ", stpQ)):
                dbg_ticks.append(b.dma("sd", dbg[nm][:], tens[:],
                                       waits=[c_done, stp_mul, stq_ev]))
        av_mm_last = {0: None, 1: None}
        exp_ticks = {0: None, 1: None}
        exp_hist = {}
        av_hist = {}
        first_qq = True
        for qq in range(4):
            q0 = 512 * qq
            for kt in range(33):
                if kt < 32:
                    jj = kt % 2
                    for half, pbase, sc0 in ((0, 0, 0), (1, 64, 1024)):
                        scol = sc0 + 512 * jj
                        b.emit("tensor",
                               lambda t, pbase=pbase, scol=scol, kt=kt, q0=q0:
                               t.matmul(psA[:, scol:scol + 512],
                                        kpair[pbase:pbase + 64, 128 * kt:128 * kt + 128],
                                        qpair[pbase:pbase + 64, q0:q0 + 512],
                                        start=True, stop=False),
                               waits=(([c_done] + b.qall("vt") + gA.all())
                                      if first_qq and kt == 0 and half == 0 else []) +
                                     ([exp_hist.get((half, kt - 2))] if kt >= 2 else
                                      ([exp_hist.get((half, 30 + kt)), exp_hist.get((half, 31 + kt))]
                                       if not first_qq else [])))
                    tick_h = {}
                    for half, pbase, sc0 in ((0, 0, 0), (1, 64, 1024)):
                        scol = sc0 + 512 * jj
                        tick_h[half] = b.emit(
                            "tensor",
                            lambda t, pbase=pbase, scol=scol, kt=kt, q0=q0:
                            t.matmul(psA[:, scol:scol + 512],
                                     stpK[pbase:pbase + 7, 128 * kt:128 * kt + 128],
                                     stpQ[pbase:pbase + 7, q0:q0 + 512],
                                     start=False, stop=True),
                            inc=("tm", 1))
                if kt >= 1:
                    for half in (0, 1):
                        av_mm_last[half] = b.emit(
                            "tensor",
                            lambda t, half=half, kt=kt:
                            t.matmul(psB[0:65, 512 * half:512 * half + 512],
                                     vtm[:, kt - 1, VB * half:VB * half + 65],
                                     at_buf[:, half, (kt - 1) % 2, :],
                                     start=(kt - 1 == 0), stop=(kt - 1 == 31)),
                            waits=[exp_hist[(half, kt - 1)]] +
                                  ([av_guard] if kt == 1 and av_guard else []),
                            inc=("tm", 1))
                        av_hist[(half, kt - 1)] = av_mm_last[half]
                if kt < 32:
                    for half, pbase, sc0 in ((0, 0, 0), (1, 64, 1024)):
                        scol = sc0 + 512 * jj
                        exp_hist[(half, kt)] = exp_ticks[half] = b.emit(
                            "scalar",
                            lambda s, half=half, scol=scol, kt=kt:
                            s.activation(at_buf[:, half, kt % 2, :],
                                         psA[:, scol:scol + 512], AF.Exp),
                            waits=[tick_h[half]] +
                                  ([av_hist.get((half, kt - 2))] if kt >= 2 else []),
                            inc=("aa", 1))
            first_qq = False
            av_done = av_mm_last[1]
            rc = b.emit("vector",
                        lambda v: v.reciprocal(lrow[64:65, 0:1024], psB[64:65, 0:1024]),
                        waits=[av_done, av_mm_last[0]], inc=("vv", 1))
            s1_ = b.dma("sd", scr_l[None, 0:1024], lrow[64:65, 0:1024], waits=[rc])
            s2_ = b.dma("sd", lrow[0:64, None, 0:1024],
                        scr_l[None, 0:1024].partition_broadcast(64),
                        waits=[s1_])
            e1 = b.emit("vector",
                        lambda v:
                        v.tensor_mul(avstage[0:64, 0:512],
                                     psB[0:64, 0:512], lrow[0:64, 0:512]),
                        waits=[s2_] + ([vst_guard] if vst_guard else []), inc=("vv", 1))
            e2 = b.emit("vector",
                        lambda v:
                        v.tensor_mul(avstage[0:64, 512:1024],
                                     psB[0:64, 512:1024], lrow[0:64, 512:1024]),
                        inc=("vv", 1))
            av_guard = e2
            b.dma("pk", attn_park[hA, :, q0:q0 + 512], avstage[0:64, 0:512],
                  waits=[e1])
            vst_guard = b.dma("pk", attn_park[hB, :, q0:q0 + 512],
                              avstage[0:64, 512:1024], waits=[e2])
        vtm_guard = av_done
        pair_sb_guard = av_done
        for i in range(4):
            gA.set(i, exp_ticks[1])
            gB.set(i, e2)

    emit_precast(99)
    ld_wo = b.dma("gd", wq_s, woT.rearrange("c p o -> p c o"), waits=[pair_sb_guard])
    ld_attn = b.dma("sd", stream_b[:, :, :],
                    attn_park.rearrange("(c w) p t -> (w p) c t", w=2),
                    waits=[vst_guard])
    ld_xf = b.dma("sd", res_f[:, :, :],
                  xkv.rearrange("c p t -> p c t")[:, :, 0:TOK],
                  waits=[last_proj_tick])
    y1_done = None
    for oc in range(NIC):
        for tc in range(NT_Q):
            qi = (oc * NT_Q + tc) % 4
            tick = None
            for ic in range(NIC):
                tick = b.emit(
                    "tensor",
                    lambda t, oc=oc, tc=tc, ic=ic, qi=qi:
                    t.matmul(psA[:, 512 * qi:512 * qi + 512],
                             wq_s[:, ic, 128 * oc:128 * oc + 128],
                             stream_b[:, ic, 512 * tc:512 * tc + 512],
                             start=(ic == 0), stop=(ic == NIC - 1)),
                    waits=(([ld_wo, ld_attn] if oc == 0 and tc == 0 else []) +
                           [gA.wait(qi)]) if ic == 0 else [],
                    inc=("tm", 1) if ic == NIC - 1 else None)
            y1_done = b.emit(
                "vector",
                lambda v, oc=oc, tc=tc, qi=qi:
                v.scalar_tensor_tensor(res_f[:, oc, 512 * tc:512 * tc + 512],
                                       psA[:, 512 * qi:512 * qi + 512],
                                       smalls[:, c_bo + oc:c_bo + oc + 1],
                                       res_f[:, oc, 512 * tc:512 * tc + 512],
                                       ALU.add, ALU.add),
                waits=[tick, ld_xf, ld_b], inc=("vv", 1))
            gA.set(qi, y1_done)

    def layernorm(src_f32, wait_prev, park_idx):
        sb = stream_b
        sqv = AR2[:].rearrange("p (c t) -> p c t", c=NIC)
        cp = b.emit("vector", lambda v: v.tensor_copy(sb[:, :, :], src_f32[:, :, :]),
                    waits=[wait_prev], inc=("vv", 1))
        sq = b.emit("vector", lambda v: v.tensor_mul(sqv[:, :, :], sb[:, :, :], sb[:, :, :]),
                    waits=[cp], inc=("vv", 1))
        st_done = None
        for stat_i, (srcb, ps) in enumerate(((sb, psA), (sqv, psB))):
            for nt in range(4):
                for ic in range(NIC):
                    st_done = b.emit(
                        "tensor",
                        lambda t, srcb=srcb, ps=ps, ic=ic, nt=nt:
                        t.matmul(ps[0:1, 512 * nt:512 * nt + 512], onescol[:],
                                 srcb[:, ic, 512 * nt:512 * nt + 512],
                                 start=(ic == 0), stop=(ic == NIC - 1)),
                        waits=[sq, ones_done] if ic == 0 and nt == 0 and stat_i == 0 else [],
                        inc=("tm", 1) if ic == NIC - 1 else None)
        Pv, Qv, Rv = statw[0:1, 0:2048], statw[0:1, 2048:4096], statw[0:1, 4096:6144]
        w = b.emit("vector", lambda v: v.tensor_scalar(Pv, psA[0:1, 0:2048],
                                                       1.0 / 1024, None, ALU.mult),
                   waits=[st_done], inc=("vv", 1))
        w = b.emit("vector", lambda v: v.tensor_scalar(Qv, psB[0:1, 0:2048],
                                                       1.0 / 1024, None, ALU.mult),
                   waits=[w], inc=("vv", 1))
        w = b.emit("vector", lambda v: v.tensor_mul(Rv, Pv, Pv), waits=[w], inc=("vv", 1))
        w = b.emit("vector", lambda v: v.tensor_sub(Qv, Qv, Rv), waits=[w], inc=("vv", 1))
        c4 = b.emit("vector", lambda v: v.tensor_scalar(Qv, Qv, 1e-5, None, ALU.add),
                    waits=[w], inc=("vv", 1))
        c5 = b.emit("scalar", lambda s: s.activation(Rv, Qv, AF.Sqrt),
                    waits=[c4], inc=("aa", 1))
        w = b.emit("vector", lambda v: v.reciprocal(Qv, Rv), waits=[c5], inc=("vv", 1))
        w = b.emit("vector", lambda v: v.tensor_mul(Pv, Pv, Qv), waits=[w], inc=("vv", 1))
        c8 = b.emit("vector", lambda v: v.tensor_scalar(Pv, Pv, -1.0, None, ALU.mult),
                    waits=[w], inc=("vv", 1))
        s1_ = b.dma("sd", scr_row[None, 0:2048], Qv, waits=[c8])
        s2_ = b.dma("sd", scr_row[None, 2048:4096], Pv)
        s3_ = b.dma("sd", statw[:, None, 0:4096],
                    scr_row[None, 0:4096].partition_broadcast(128), waits=[s1_, s2_])
        last = None
        for ic in range(NIC):
            t1 = b.emit("vector", lambda v, ic=ic: v.tensor_mul(
                src_f32[:, ic, :], src_f32[:, ic, :], statw[:, 0:2048]),
                waits=[s3_] if ic == 0 else [], inc=("vv", 1))
            t2 = b.emit("vector", lambda v, ic=ic: v.tensor_add(
                src_f32[:, ic, :], src_f32[:, ic, :], statw[:, 2048:4096]),
                waits=[t1], inc=("vv", 1))
            last = b.emit("vector", lambda v, ic=ic: v.tensor_copy(
                sb[:, ic, :], src_f32[:, ic, :]),
                waits=[t2], inc=("vv", 1))
        pk = b.dma("cd", y_park[park_idx].rearrange("c p t -> p c t"), src_f32[:, :, :],
                   waits=[last])
        return last, pk

    def ln_tile(ybuf, dst_b16, t0, w, wait_list, park_idx, tag):
        sb = dst_b16
        cpw = b.emit("vector", lambda v: v.tensor_copy(sb[:, :, t0:t0 + w], ybuf[:, :, :]),
                     waits=wait_list, inc=("vv", 1))
        sqw = b.emit("vector", lambda v: v.tensor_mul(
            Yv[:, 0:NIC, :], sb[:, :, t0:t0 + w], sb[:, :, t0:t0 + w]),
            waits=[cpw], inc=("vv", 1))
        st_done = None
        for stat_i, (srcb, ps) in enumerate(((sb[:, :, t0:t0 + w], psA), (Yv[:, 0:NIC, :], psB))):
            for ic in range(NIC):
                st_done = b.emit(
                    "tensor",
                    lambda t, srcb=srcb, ps=ps, ic=ic:
                    t.matmul(ps[0:1, 1536:1536 + w], onescol[:], srcb[:, ic, :],
                             start=(ic == 0), stop=(ic == NIC - 1)),
                    waits=[sqw, ones_done] + gA.all() if ic == 0 and stat_i == 0 else [],
                    inc=("tm", 1) if ic == NIC - 1 else None)
        Pv = ARF[0:1, 4096:4096 + w]
        Qv = ARF[0:1, 4608:4608 + w]
        Rv = ARF[0:1, 5120:5120 + w]
        wv_ = b.emit("vector", lambda v: v.tensor_scalar(Pv, psA[0:1, 1536:1536 + w],
                                                         1.0 / 1024, None, ALU.mult),
                     waits=[st_done], inc=("vv", 1))
        wv_ = b.emit("vector", lambda v: v.tensor_scalar(Qv, psB[0:1, 1536:1536 + w],
                                                         1.0 / 1024, None, ALU.mult),
                     waits=[wv_], inc=("vv", 1))
        wv_ = b.emit("vector", lambda v: v.tensor_mul(Rv, Pv, Pv), waits=[wv_], inc=("vv", 1))
        wv_ = b.emit("vector", lambda v: v.tensor_sub(Qv, Qv, Rv), waits=[wv_], inc=("vv", 1))
        wv_ = b.emit("vector", lambda v: v.tensor_scalar(Qv, Qv, 1e-5, None, ALU.add),
                     waits=[wv_], inc=("vv", 1))
        wv_ = b.emit("scalar", lambda s: s.activation(Rv, Qv, AF.Sqrt),
                     waits=[wv_], inc=("aa", 1))
        wv_ = b.emit("vector", lambda v: v.reciprocal(Qv, Rv), waits=[wv_], inc=("vv", 1))
        wv_ = b.emit("vector", lambda v: v.tensor_mul(Pv, Pv, Qv), waits=[wv_], inc=("vv", 1))
        wv_ = b.emit("vector", lambda v: v.tensor_scalar(Pv, Pv, -1.0, None, ALU.mult),
                     waits=[wv_], inc=("vv", 1))
        u1 = b.dma("sd", scr_row[None, 0:w], Qv, waits=[wv_])
        u2 = b.dma("sd", scr_row[None, w:2 * w], Pv, waits=[wv_])
        bcv = ARF[:, 5120:5120 + 2 * w]
        u3 = b.dma("sd", bcv[:, None, :], scr_row[None, 0:2 * w].partition_broadcast(128),
                   waits=[u1, u2])
        last = None
        for ic in range(NIC):
            a1 = b.emit("vector", lambda v, ic=ic: v.tensor_mul(
                ybuf[:, ic, :], ybuf[:, ic, :], bcv[:, 0:w]),
                waits=[u3] if ic == 0 else [], inc=("vv", 1))
            a2 = b.emit("vector", lambda v, ic=ic: v.tensor_add(
                ybuf[:, ic, :], ybuf[:, ic, :], bcv[:, w:2 * w]),
                waits=[a1], inc=("vv", 1))
            last = b.emit("vector", lambda v, ic=ic: v.tensor_copy(
                sb[:, ic, t0:t0 + w], ybuf[:, ic, :]),
                waits=[a2], inc=("vv", 1))
        pk = b.dma("cd", y_park[park_idx].rearrange("c p t -> p c t")[:, :, t0:t0 + w],
                   ybuf[:, :, :], waits=[last])
        for i in range(4):
            gA.set(i, last)
        return last, pk

    if DEBUG:
        dbg_ticks.append(b.dma("sd", dbg["d_attn"][:],
                               attn_park.rearrange("h p t -> h p t"),
                               waits=[vst_guard] + b.qall("pk")))
        dbg_ticks.append(b.dma("sd", dbg["d_y1"].rearrange("c p t -> p c t"),
                               res_f[:, :, :], waits=[y1_done]))
    ln1, y1_pk = layernorm(res_f, y1_done if not DEBUG else dbg_ticks[-1], 0)

    wraw = res_f[:, 4:6, :].rearrange("p c t -> p (c t)")
    mxs = wraw[:, 3072:3168]
    rss = wraw[:, 3168:3264]
    wqnat = AR0[:, 24576:27648]
    wqT_s = AR0[:, 27648:30720]
    qw_state = {"last": None, "tr": None}
    quant_jobs = []
    for wsl_, wi_, noc_, nic_ in ((wgate, 0, 24, NIC), (wup, 1, 24, NIC), (wdown, 2, NIC, 24)):
        for oc_ in range(noc_):
            quant_jobs.append((wsl_, wi_, oc_, nic_))

    def emit_quant(n):
        while n > 0 and quant_jobs:
            wsl, wi, oc, nic = quant_jobs.pop(0)
            qw_last = qw_state["last"]
            tr_prev2 = qw_state["tr"]
            I_ = nic * 128
            nb = I_ // 32
            first = (wi == 0 and oc == 0)
            lw = b.dma("gd", wraw[:, 0:I_], wsl[oc],
                       waits=([qw_last, y1_pk] if first else [qw_last]))
            q1 = b.emit("vector",
                        lambda v, I_=I_, nb=nb: v.tensor_reduce(
                            mxs[:, 0:nb].rearrange("p (b o) -> p b o", o=1),
                            wraw[:, 0:I_].rearrange("p (b j) -> p b j", j=32),
                            AX.X, ALU.max, apply_absolute_value=True),
                        waits=[lw], inc=("vv", 1))
            q2 = b.emit("vector",
                        lambda v, nb=nb: v.tensor_scalar(mxs[:, 0:nb], mxs[:, 0:nb],
                                                         1.0 / 7.0, 1e-12, ALU.mult, ALU.max),
                        waits=[q1], inc=("vv", 1))
            q3 = b.emit("vector",
                        lambda v, nb=nb: v.reciprocal(rss[:, 0:nb], mxs[:, 0:nb]),
                        waits=[q2], inc=("vv", 1))
            q4 = b.emit("vector",
                        lambda v, I_=I_, nb=nb: v.tensor_tensor(
                            wraw[:, 0:I_].rearrange("p (b j) -> p b j", j=32),
                            wraw[:, 0:I_].rearrange("p (b j) -> p b j", j=32),
                            rss[:, 0:nb][:, :, None].broadcast_to([128, nb, 32]),
                            ALU.mult),
                        waits=[q3], inc=("vv", 1))
            q5 = b.emit("vector",
                        lambda v, I_=I_: v.tensor_scalar(wraw[:, 0:I_], wraw[:, 0:I_],
                                                         MAGIC, MAGIC, ALU.add, ALU.subtract),
                        waits=[q4], inc=("vv", 1))
            q6 = b.emit("vector",
                        lambda v, I_=I_: v.tensor_scalar(wraw[:, 0:I_], wraw[:, 0:I_],
                                                         7.0, -8.0, ALU.min, ALU.max),
                        waits=[q5], inc=("vv", 1))
            q7 = b.emit("vector",
                        lambda v, I_=I_, nb=nb: v.tensor_tensor(
                            wqnat[:, 0:I_].rearrange("p (b j) -> p b j", j=32),
                            wraw[:, 0:I_].rearrange("p (b j) -> p b j", j=32),
                            mxs[:, 0:nb][:, :, None].broadcast_to([128, nb, 32]),
                            ALU.mult),
                        waits=[q6] + ([tr_prev2] if tr_prev2 else []), inc=("vv", 1))
            pk_n = b.dma("gd", wqn_park[wi, oc, :, 0:I_], wqnat[:, 0:I_],
                         waits=[q7])
            qw_state["tr"] = pk_n
            qw_state["last"] = pk_n
            n -= 1

    TT_F = 512
    NTT = TOK // TT_F
    Xv = AR0[:].rearrange("p (c t) -> p c t", t=TT_F)
    Yv = AR2[:].rearrange("p (c t) -> p c t", t=TT_F)
    wbuf = AR3[:].rearrange("p (j c o) -> p j c o", j=2, o=128)
    y2t = ARF[:, 0:4096].rearrange("p (c t) -> p c t", t=TT_F)

    def ffn_stage(tt, ws, n_oc, n_ic, rhs_fn, evac_fn, wait_extra, dma_q="gd"):
        nonlocal_ticks = []
        for oc in range(n_oc):
            if oc % 4 == 0:
                emit_quant(1)
            jb = oc % 2
            wt = b.dma("sd" if oc % 2 == 0 else "cd", wbuf[:, jb, 0:n_ic, :],
                       ws[oc].rearrange("c p o -> p c o"),
                       waits=[wgrd.wait(jb)] + ((wait_extra + pc_ticks) if oc == 0 else []))
            qi = oc % 4
            tick = None
            for ic in range(n_ic):
                tick = b.emit(
                    "tensor",
                    lambda t, oc=oc, ic=ic, qi=qi, jb=jb, rhs=rhs_fn:
                    t.matmul(psA[:, 512 * qi:512 * qi + 512],
                             wbuf[:, jb, ic, :], rhs(ic),
                             start=(ic == 0), stop=(ic == n_ic - 1)),
                    waits=([wt, gA.wait(qi)] + (wait_extra if oc == 0 else []))
                    if ic == 0 else [],
                    inc=("tm", 1) if ic == n_ic - 1 else None)
            wgrd.set(jb, tick)
            ev = evac_fn(oc, qi, tick)
            gA.set(qi, ev)
            nonlocal_ticks.append(ev)
        return nonlocal_ticks

    wgrd = Guard(2)
    out_ticks = []
    ln2_prev = ln1
    pk_prev = None
    for tt in range(NTT):
        t0 = TT_F * tt
        y1t_ld = b.dma("cd", y2t[:, :, :],
                       y_park[0].rearrange("c p t -> p c t")[:, :, t0:t0 + TT_F],
                       waits=[y1_pk, ln2_prev, pk_prev])
        s1_ticks = []
        for oc in range(32):
            jb = oc % 2
            wt1 = b.dma("sd", wbuf[:, jb, 0:NIC, :], hc_w1[oc].rearrange("c p o -> p c o"),
                        waits=[wgrd.wait(jb)] +
                              (([ln1] + pc_ticks) if oc == 0 and tt == 0 else []))
            qi = (2 * oc) % 4
            tick = None
            for ic in range(NIC):
                tick = b.emit(
                    "tensor",
                    lambda t, oc=oc, ic=ic, qi=qi, jb=jb, t0=t0:
                    t.matmul(psA[:, 512 * qi:512 * qi + 512],
                             wbuf[:, jb, ic, :], stream_b[:, ic, t0:t0 + TT_F],
                             start=(ic == 0), stop=(ic == NIC - 1)),
                    waits=[wt1, gA.wait(qi)] if ic == 0 else [],
                    inc=("tm", 1) if ic == NIC - 1 else None)
            wgrd.set(jb, tick)
            e1 = b.emit("scalar",
                        lambda s, oc=oc, qi=qi:
                        s.activation(Xv[:, oc, :], psA[:, 512 * qi:512 * qi + 512],
                                     AF.Silu, bias=0.0,
                                     scale=smalls[:, c_s1 + oc:c_s1 + oc + 1]),
                        waits=[tick, wait_tiny], inc=("aa", 1))
            gA.set(qi, e1)
            jb2 = (oc + 1) % 2
            wt2 = b.dma("cd", wbuf[:, jb2, 0:NIC, :], hc_w2[oc].rearrange("c p o -> p c o"),
                        waits=[wgrd.wait(jb2)])
            qi2 = (2 * oc + 1) % 4
            tick = None
            for ic in range(NIC):
                tick = b.emit(
                    "tensor",
                    lambda t, oc=oc, ic=ic, qi2=qi2, jb2=jb2, t0=t0:
                    t.matmul(psA[:, 512 * qi2:512 * qi2 + 512],
                             wbuf[:, jb2, ic, :], stream_b[:, ic, t0:t0 + TT_F],
                             start=(ic == 0), stop=(ic == NIC - 1)),
                    waits=[wt2, gA.wait(qi2)] if ic == 0 else [],
                    inc=("tm", 1) if ic == NIC - 1 else None)
            wgrd.set(jb2, tick)
            e2 = b.emit("scalar",
                        lambda s, oc=oc, qi2=qi2:
                        s.activation(Yv[:, oc, :], psA[:, 512 * qi2:512 * qi2 + 512],
                                     AF.Identity, bias=0.0,
                                     scale=smalls[:, c_s2 + oc:c_s2 + oc + 1]),
                        waits=[tick, wait_tiny], inc=("aa", 1))
            gA.set(qi2, e2)
            mul = b.emit("vector",
                         lambda v, oc=oc: v.tensor_mul(Xv[:, oc, :], Xv[:, oc, :],
                                                       Yv[:, oc, :]),
                         waits=[e1, e2], inc=("vv", 1))
            s1_ticks.append(mul)
        def rhs_X(ic):
            return Xv[:, ic, :]

        def rhs_Y(ic):
            return Yv[:, ic, :]

        def ev_gelu(dstv, scol):
            def f(oc, qi, tick):
                return b.emit("scalar",
                              lambda s, oc=oc, qi=qi:
                              s.activation(dstv[:, oc, :],
                                           psA[:, 512 * qi:512 * qi + 512],
                                           AF.Gelu, bias=0.0,
                                           scale=smalls[:, scol + oc:scol + oc + 1]),
                              waits=[tick, wait_tiny], inc=("aa", 1))
            return f
        s2_ticks = ffn_stage(tt, hc_h1, 32, 32, rhs_X, ev_gelu(Yv, c_h1s), s1_ticks)
        s3_ticks = ffn_stage(tt, hc_h2, 32, 32, rhs_Y, ev_gelu(Xv, c_h2s), s2_ticks)
        def ev_out(oc, qi, tick):
            return b.emit("vector",
                          lambda v, oc=oc, qi=qi:
                          v.scalar_tensor_tensor(y2t[:, oc, :],
                                                 psA[:, 512 * qi:512 * qi + 512],
                                                 smalls[:, c_s3 + oc:c_s3 + oc + 1],
                                                 y2t[:, oc, :],
                                                 ALU.mult, ALU.add),
                          waits=[tick, y1t_ld, wait_tiny], inc=("vv", 1))
        s4_ticks = ffn_stage(tt, hc_w3, NIC, 32, rhs_X, ev_out, s3_ticks)
        lnt, pk2 = ln_tile(y2t, stream_b, t0, TT_F, s4_ticks, 1, tt)
        outv = out.rearrange("(c p) t -> p c t", p=128)
        tap = b.dma("sd", outv[:, :, t0:t0 + TT_F], y2t[:, :, :], waits=[lnt])
        ln2_prev = tap
        pk_prev = pk2
        out_ticks.append(tap)
        out_ticks.append(pk2)

    emit_quant(999)
    qw_seed = qw_state["last"]
    tsw_prev = None
    for wsl_, wi_, noc_, nic_ in ((wgate, 0, 24, NIC), (wup, 1, 24, NIC), (wdown, 2, NIC, 24)):
        for oc_ in range(noc_):
            trs = []
            for ic_ in range(nic_):
                trs.append(b.dma("vt", wqT_s[:, 128 * ic_:128 * ic_ + 128],
                                 wqn_park[wi_, oc_, :, 128 * ic_:128 * ic_ + 128],
                                 waits=([qw_seed] if oc_ == 0 and wi_ == 0 else []) +
                                       ([tsw_prev] if tsw_prev else []),
                                 transpose=True))
            tsw_prev = b.dma("gd", wq_park[wi_, oc_, 0:nic_].rearrange("c p o -> p c o"),
                             wqT_s[:, 0:(128 * nic_)].rearrange("p (c o) -> p c o", o=128),
                             waits=trs)
    qw_state["last"] = tsw_prev

    l1w = AR3[:, 0:1536].rearrange("p (w c o) -> p w c o", w=3, o=64)
    l1wo = AR3[:, 1536:2560]
    kftm = AR3[:, 2560:3584].rearrange("p (k c) -> p k c", k=16)
    vtm2 = AR3[:, 3584:4864].rearrange("p (k c) -> p k c", k=16)
    srunb = AR3[:, 4864:4944]
    qf = AR0[:, 0:2048]
    kf = AR0[:, 2048:4096]
    vf = AR0[:, 4096:6144]
    at2 = AR0[:, 6144:6272]
    outn = AR0[:, 8192:10240]
    numbuf = ARF[:, 0:2048]
    stf = ARF[:, 2048:2128]
    prefx = ARF[:, 2128:2208]
    scr1 = ARF[:, 2304:2816]
    scr2 = ARF[:, 2816:3328]

    wl1_l = [b.dma("gd", l1w[:, 0, :, :], l1wqT.rearrange("c p o -> p c o"),
                   waits=[ln2_prev])]
    wl1_l.append(b.dma("gd", l1w[:, 1, :, :], l1wkT.rearrange("c p o -> p c o")))
    wl1_l.append(b.dma("gd", l1w[:, 2, :, :], l1wvT.rearrange("c p o -> p c o")))
    l1wov = l1wo[0:64, :].rearrange("p (c o) -> p c o", o=128)
    wlo = b.dma("gd", l1wov, l1woT.rearrange("p (c o) -> p c o", o=128), waits=[ln2_prev])

    proj_out = {}
    prev_ev = None
    for wi, dst, elu in ((0, qf, True), (1, kf, True), (2, vf, False)):
        for tc in range(4):
            qi = tc % 4
            tick = None
            for ic in range(NIC):
                tick = b.emit(
                    "tensor",
                    lambda t, wi=wi, ic=ic, tc=tc, qi=qi:
                    t.matmul(psA[0:64, 512 * qi:512 * qi + 512],
                             l1w[:, wi, ic, :], stream_b[:, ic, 512 * tc:512 * tc + 512],
                             start=(ic == 0), stop=(ic == NIC - 1)),
                    waits=(wl1_l + [gA.wait(qi), pk_prev]) if ic == 0 else [],
                    inc=("tm", 1) if ic == NIC - 1 else None)
            if elu:
                m1 = b.emit("vector",
                            lambda v, qi=qi: v.tensor_scalar(scr1[0:64, :],
                                                             psA[0:64, 512 * qi:512 * qi + 512],
                                                             0.0, None, ALU.min),
                            waits=[tick] + ([prev_ev] if prev_ev else []), inc=("vv", 1))
                m2 = b.emit("vector",
                            lambda v, qi=qi: v.tensor_scalar(scr2[0:64, :],
                                                             psA[0:64, 512 * qi:512 * qi + 512],
                                                             0.0, None, ALU.max),
                            waits=[m1], inc=("vv", 1))
                m3 = b.emit("scalar", lambda s: s.activation(scr1[0:64, :], scr1[0:64, :],
                                                             AF.Exp),
                            waits=[m2], inc=("aa", 1))
                ev = b.emit("vector",
                            lambda v, dst=dst, tc=tc: v.tensor_add(
                                dst[0:64, 512 * tc:512 * tc + 512],
                                scr1[0:64, :], scr2[0:64, :]),
                            waits=[m3], inc=("vv", 1))
            else:
                ev = b.emit("vector",
                            lambda v, dst=dst, tc=tc, qi=qi: v.tensor_copy(
                                dst[0:64, 512 * tc:512 * tc + 512],
                                psA[0:64, 512 * qi:512 * qi + 512]),
                            waits=[tick], inc=("vv", 1))
            gA.set(qi, ev)
            prev_ev = ev
            proj_out[(wi, tc)] = ev
    proj_done = [proj_out[(wi, tc)] for wi in range(3) for tc in range(4)]

    vo = b.emit("vector", lambda v: v.memset(vtm2[:], 0.0), waits=[ln2_prev], inc=("vv", 1))
    vo2 = b.emit("vector", lambda v: v.memset(vtm2[:, :, 64:65], 1.0), waits=[vo], inc=("vv", 1))
    ttk = None
    for ch in range(16):
        ttk = b.dma("vt", kftm[:, ch, :], kf[0:64, 128 * ch:128 * ch + 128],
                    waits=proj_done if ch == 0 else [], transpose=True)
        ttk = b.dma("vt", vtm2[:, ch, 0:64], vf[0:64, 128 * ch:128 * ch + 128],
                    waits=[vo2] if ch == 0 else [], transpose=True)
    tm_done = b.qall("vt")

    sttk = None
    for ch in range(16):
        sttk = b.emit("tensor",
                      lambda t, ch=ch:
                      t.matmul(psB[0:64, 1536:1601], kftm[:, ch, :], vtm2[:, ch, 0:65],
                               start=(ch == 0), stop=(ch == 15), skip_group_check=True),
                      waits=tm_done + gB.all() if ch == 0 else [],
                      inc=("tm", 1) if ch == 15 else None)
    exz = b.emit("vector", lambda v: v.memset(stf[0:64, 0:80], 0.0),
                 waits=[ln2_prev, pk_prev], inc=("vv", 1))
    exm = b.emit("vector", lambda v: v.tensor_scalar(stf[0:64, 0:65], psB[0:64, 1536:1601],
                                                     pmsk[0:64, 0:1], None, ALU.mult),
                 waits=[sttk, pm_ld, exz], inc=("vv", 1))
    exd = b.dma("pk", st_in.rearrange("(p c) -> p c", p=64),
                stf[0:64, 0:80], waits=[exm])
    cstate = b.emit("gpsimd",
                    lambda g: g.collective_compute(
                        "AllReduce", ALU.add, ins=[st_in[:]], outs=[st_out[:]],
                        replica_groups=[[0, 1], [2, 3], [4, 5], [6, 7]]),
                    waits=[exd], inc=("gg", 1))
    exr = b.dma("pk", prefx[0:64, 0:65], st_out.rearrange("(p c) -> p c", p=64)[:, 0:65],
                waits=[cstate])
    pfx = b.emit("vector", lambda v: v.tensor_scalar(prefx[0:64, 0:65], prefx[0:64, 0:65],
                                                     pmsk[0:64, 1:2], None, ALU.mult),
                 waits=[exr], inc=("vv", 1))
    srb0 = b.emit("vector", lambda v: v.tensor_copy(srunb[0:64, 0:65], prefx[0:64, 0:65]),
                  waits=[pfx], inc=("vv", 1))

    srb_tick = srb0
    num_ev = None
    for ch in range(16):
        atk = b.emit("tensor",
                     lambda t, ch=ch:
                     t.matmul(psB[:, 0:128], kf[0:64, 128 * ch:128 * ch + 128],
                              qf[0:64, 128 * ch:128 * ch + 128],
                              start=True, stop=True, skip_group_check=True),
                     waits=[num_ev] if num_ev else [],
                     inc=("tm", 1))
        msk = b.emit("vector",
                     lambda v: v.tensor_mul(at2[:, :], psB[:, 0:128], tri32[:]),
                     waits=[atk, tri_done], inc=("vv", 1))
        n1 = b.emit("tensor",
                    lambda t, ch=ch:
                    t.matmul(psB[0:65, 512:640], srunb[0:64, 0:65],
                             qf[0:64, 128 * ch:128 * ch + 128],
                             start=True, stop=False, skip_group_check=True),
                    waits=[srb_tick] + ([num_ev] if num_ev else []))
        n2 = b.emit("tensor",
                    lambda t, ch=ch:
                    t.matmul(psB[0:65, 512:640], vtm2[:, ch, 0:65], at2[:, :],
                             start=False, stop=True, skip_group_check=True),
                    waits=[msk], inc=("tm", 1))
        num_ev = b.emit("vector",
                        lambda v, ch=ch: v.tensor_copy(numbuf[0:65, 128 * ch:128 * ch + 128],
                                                       psB[0:65, 512:640]),
                        waits=[n2], inc=("vv", 1))
        if ch < 15:
            sup = b.emit("tensor",
                         lambda t, ch=ch:
                         t.matmul(psB[0:64, 1536:1601], kftm[:, ch, :], vtm2[:, ch, 0:65],
                                  start=(ch == 0), stop=True, skip_group_check=True),
                         waits=[exm] if ch == 0 else [], inc=("tm", 1))
            sev = b.emit("vector",
                         lambda v: v.scalar_tensor_tensor(stf[0:64, 0:65],
                                                          psB[0:64, 1536:1601], 1.0,
                                                          prefx[0:64, 0:65],
                                                          ALU.mult, ALU.add),
                         waits=[sup, pfx], inc=("vv", 1))
            srb_tick = b.emit("vector",
                              lambda v: v.tensor_copy(srunb[0:64, 0:65], stf[0:64, 0:65]),
                              waits=[sev], inc=("vv", 1))

    if DEBUG:
        dbg_ticks.append(b.dma("sd", dbg["d_qf"][:], AR0[:, 0:2048], waits=proj_done))
        dbg_ticks.append(b.dma("sd", dbg["d_kf"][:], AR0[:, 2048:4096], waits=proj_done))
        dbg_ticks.append(b.dma("sd", dbg["d_vf"][:], AR0[:, 4096:6144], waits=proj_done))
        dbg_ticks.append(b.dma("sd", dbg["d_num"][:], numbuf[:, :], waits=[num_ev]))
        dbg_ticks.append(b.dma("sd", dbg["d_st"][:], ARF[:, 2128:2208], waits=[pfx]))
        dbg_ticks.append(b.dma("sd", dbg["d_y2"].rearrange("c p t -> p c t"),
                               y_park[1].rearrange("c p t -> p c t"), waits=[pk_prev]))
    dn1 = b.emit("vector", lambda v: v.tensor_scalar(numbuf[64:65, :], numbuf[64:65, :],
                                                     1e-6, None, ALU.add),
                 waits=[num_ev], inc=("vv", 1))
    dn2 = b.emit("vector", lambda v: v.reciprocal(numbuf[64:65, :], numbuf[64:65, :]),
                 waits=[dn1], inc=("vv", 1))
    dn3 = b.dma("sd", scr_l[None, :], numbuf[64:65, :], waits=[dn2])
    dn4 = b.dma("sd", lrow[0:64, None, :], scr_l[None, :].partition_broadcast(64),
                waits=[dn3])
    dnm = b.emit("vector", lambda v: v.tensor_mul(outn[0:64, :], numbuf[0:64, :],
                                                  lrow[0:64, :]),
                 waits=[dn4], inc=("vv", 1))

    y3t = ARF[:, 0:4096].rearrange("p (c t) -> p c t", t=TT_F)
    ln3_prev = None
    pk3_prev = None
    for tt in range(NTT):
        t0 = TT_F * tt
        y2ld = b.dma("cd", y3t[:, :, :],
                     y_park[1].rearrange("c p t -> p c t")[:, :, t0:t0 + TT_F],
                     waits=[pk_prev, ln2_prev, dnm] +
                           ([ln3_prev, pk3_prev] if ln3_prev else []) + dbg_ticks)
        s4t = []
        for oc in range(NIC):
            qi = oc % 4
            tick = None
            tick = b.emit("tensor",
                          lambda t, oc=oc, qi=qi, t0=t0:
                          t.matmul(psA[:, 512 * qi:512 * qi + 512],
                                   l1wov[:, oc, :],
                                   outn[0:64, t0:t0 + TT_F],
                                   start=True, stop=True),
                          waits=[wlo, dnm, gA.wait(qi)],
                          inc=("tm", 1))
            ev = b.emit("vector",
                        lambda v, oc=oc, qi=qi:
                        v.scalar_tensor_tensor(y3t[:, oc, :],
                                               psA[:, 512 * qi:512 * qi + 512], 1.0,
                                               y3t[:, oc, :], ALU.mult, ALU.add),
                        waits=[tick, y2ld], inc=("vv", 1))
            gA.set(qi, ev)
            s4t.append(ev)
        if DEBUG and tt == 0:
            s4t = s4t + [b.dma("sd", dbg["d_y3p"].rearrange("c p t -> p c t"),
                               y3t[:, :, :], waits=s4t)]
            dbg_ticks.append(s4t[-1])
        ln3_prev, pk3_prev = ln_tile(y3t, stream_b, t0, TT_F, s4t, 2, 10 + tt)

    if DEBUG:
        dbg_ticks.append(b.dma("sd", dbg["d_outn"][:], AR0[:, 8192:10240], waits=[dnm]))
        dbg_ticks.append(b.dma("sd", dbg["d_y3"].rearrange("c p t -> p c t"),
                               y_park[2].rearrange("c p t -> p c t"), waits=[pk3_prev]))
    gm_prev = [None]
    def global_max(local_col, idx, n, wait_list):
        g1 = b.dma("sd", scr_pm[None, :], local_col[:, 0:1],
                   waits=wait_list + ([gm_prev[0]] if gm_prev[0] else []))
        g2 = b.dma("sd", ARF[0:1, 6016:6144], scr_pm[None, :], waits=[g1])
        g3 = b.emit("vector", lambda v: v.tensor_reduce(ARF[0:1, 6012:6013],
                                                        ARF[0:1, 6016:6144],
                                                        AX.X, ALU.max),
                    waits=[g2], inc=("vv", 1))
        g3b = b.emit("vector", lambda v: v.tensor_copy(
            ARF[0:1, 6008:6012], ARF[0:1, 6012:6013].broadcast_to([1, 4])),
            waits=[g3], inc=("vv", 1))
        g4 = b.dma("pk", cc_in[idx][None, 0:4], ARF[0:1, 6008:6012], waits=[g3b])
        g5 = b.emit("gpsimd",
                    lambda g: g.collective_compute(
                        "AllReduce", ALU.max, ins=[cc_in[idx][:]], outs=[cc_out[idx][:]],
                        replica_groups=[list(range(8))]),
                    waits=[g4], inc=("gg", 1))
        g6 = b.dma("pk", ARF[0:1, 6013:6014], cc_out[idx][None, 0:1], waits=[g5])
        g7 = b.emit("vector", lambda v: v.tensor_scalar(ARF[0:1, 6014:6015],
                                                        ARF[0:1, 6013:6014],
                                                        1.0 / 32767.0, 1e-12,
                                                        ALU.mult, ALU.max),
                    waits=[g6], inc=("vv", 1))
        g8 = b.emit("vector", lambda v: v.reciprocal(ARF[0:1, 6015:6016],
                                                     ARF[0:1, 6014:6015]),
                    waits=[g7], inc=("vv", 1))
        g9 = b.dma("sd", scr_pm[None, 0:2], ARF[0:1, 6014:6016], waits=[g8])
        col = 208 + 2 * idx
        g10 = b.dma("sd", smalls[:, col:col + 2][:, None, :],
                    scr_pm[None, 0:2].partition_broadcast(128), waits=[g9])
        gm_prev[0] = g10
        return g10, smalls[:, col + 1:col + 2], smalls[:, col:col + 1]

    def qdq3(dst_ap, src_ap, inv_ap, sc_ap, wait_list, scr=None, out_guard=None):
        sq_ = scr if scr is not None else scr_q
        a1 = b.emit("vector", lambda v: v.tensor_scalar(sq_[:], src_ap, inv_ap, MAGIC,
                                                        ALU.mult, ALU.add),
                    waits=wait_list, inc=("vv", 1))
        a2 = b.emit("vector", lambda v: v.tensor_scalar(sq_[:], sq_[:], MAGIC, 32767.0,
                                                        ALU.subtract, ALU.min),
                    waits=[a1], inc=("vv", 1))
        a3 = b.emit("vector", lambda v: v.tensor_scalar(dst_ap, sq_[:], -32768.0, sc_ap,
                                                        ALU.max, ALU.mult),
                    waits=[a2] + ([out_guard] if out_guard else []), inc=("vv", 1))
        return a3

    emit_quant(999)
    qw_last = qw_state["last"]
    lm = None
    xmax_ticks = []
    for tt in range(NTT):
        t0 = TT_F * tt
        yld = b.dma("cd", y3t[:, :, :],
                    y_park[2].rearrange("c p t -> p c t")[:, :, t0:t0 + TT_F],
                    waits=[pk3_prev, qw_last] if tt == 0 else [lm])
        r1 = b.emit("vector", lambda v: v.tensor_reduce(
            ARF[:, 4096 + 0:4096 + 1].rearrange("p a -> p a 1")[:, :, 0] if False
            else ARF[:, 4100:4101],
            y3t[:, :, :].rearrange("p c t -> p (c t)"), AX.X, ALU.max,
            apply_absolute_value=True),
            waits=[yld], inc=("vv", 1))
        lm = b.emit("vector", lambda v, tt=tt: v.tensor_tensor(
            ARF[:, 4101:4102], ARF[:, 4101:4102], ARF[:, 4100:4101], ALU.max)
            if tt > 0 else v.tensor_copy(ARF[:, 4101:4102], ARF[:, 4100:4101]),
            waits=[r1], inc=("vv", 1))
        xmax_ticks.append(lm)
    gx, inv_x, sc_x = global_max(ARF[:, 4101:4102], 0, 1, [lm])
    scr_q = ARF[:, 4608:5120]
    xq_ticks = []
    xq_prev = None
    for tt in range(NTT):
        t0 = TT_F * tt
        yld = b.dma("cd", y3t[:, :, :],
                    y_park[2].rearrange("c p t -> p c t")[:, :, t0:t0 + TT_F],
                    waits=[gx] if tt == 0 else [xq_prev])
        for oc in range(NIC):
            xq_prev = qdq3(stream_b[:, oc, t0:t0 + TT_F], y3t[:, oc, :], inv_x, sc_x,
                           [yld, gx] + ([xq_prev] if xq_prev else []))
        xq_ticks.append(xq_prev)

    wbuf2 = AR3[:].rearrange("p (j c o) -> p j c o", j=2, o=128)
    gstages = [AR0[:, 512 * j:512 * j + 512] for j in range(4)]
    gmx = [ARF[:, 4096 + j:4097 + j] for j in range(4)]
    gumax = ARF[:, 4102:4104]
    gmz = b.emit("vector", lambda v: v.memset(ARF[:, 4096:4104], 0.0),
                 waits=[dnm], inc=("vv", 1))
    gu_ticks = []
    wgrd2 = Guard(2)
    sgrds = Guard(4)
    it = 0
    for wi, wpk in ((0, 0), (1, 1)):
        for tt in range(NTT):
            t0 = TT_F * tt
            for oc in range(24):
                jb = it % 2
                sj = it % 4
                wt = b.dma("sd" if oc % 2 == 0 else "cd", wbuf2[:, jb, 0:NIC, :],
                           wq_park[wi, oc, 0:NIC].rearrange("c p o -> p c o"),
                           waits=[wgrd2.wait(jb)] + ([qw_last] + xq_ticks if oc == 0 and tt == 0 else []))
                qi = it % 4
                tick = None
                for ic in range(NIC):
                    tick = b.emit(
                        "tensor",
                        lambda t, oc=oc, ic=ic, qi=qi, jb=jb, t0=t0:
                        t.matmul(psA[:, 512 * qi:512 * qi + 512],
                                 wbuf2[:, jb, ic, :], stream_b[:, ic, t0:t0 + TT_F],
                                 start=(ic == 0), stop=(ic == NIC - 1)),
                        waits=([wt, gA.wait(qi)] + (xq_ticks if oc == 0 and tt == 0 else []))
                        if ic == 0 else [],
                        inc=("tm", 1) if ic == NIC - 1 else None)
                wgrd2.set(jb, tick)
                ev = b.emit("scalar",
                            lambda s, qi=qi, sj=sj:
                            s.activation(gstages[sj][:], psA[:, 512 * qi:512 * qi + 512],
                                         AF.Copy),
                            waits=[tick] + ([sgrds.wait(sj)] if sgrds.wait(sj) else []),
                            inc=("aa", 1))
                gA.set(qi, ev)
                mx1 = b.emit("vector",
                             lambda v, sj=sj: v.tensor_reduce(gmx[sj], gstages[sj][:],
                                                              AX.X, ALU.max,
                                                              apply_absolute_value=True),
                             waits=[ev, gmz], inc=("vv", 1))
                mx2 = b.emit("vector",
                             lambda v, wi=wi, sj=sj: v.tensor_tensor(gumax[:, wi:wi + 1],
                                                                     gumax[:, wi:wi + 1],
                                                                     gmx[sj], ALU.max),
                             waits=[mx1], inc=("vv", 1))
                pkq = "pk" if it % 2 == 0 else "gd"
                sg = b.dma(pkq, gu_park[wpk, oc, :, t0:t0 + TT_F], gstages[sj][:],
                           waits=[ev, mx1])
                sgrds.set(sj, sg)
                gu_ticks.append(sg)
                it += 1
    gg1, inv_up, sc_up = global_max(gumax[:, 1:2], 1, 1, [mx2] + gu_ticks[-1:])
    gg2, inv_gt, sc_gt = global_max(gumax[:, 0:1], 2, 1, [mx2])

    g2max = ARF[:, 4104:4105]
    gz2 = b.emit("vector", lambda v: v.memset(ARF[:, 4104:4112], 0.0), inc=("vv", 1))
    SETS = []
    SETS.append(dict(tq1=AR0[:, 1024:1536], tq2=ARF[:, 5120:5632],
                     tq3=ARF[:, 5632:6144], scr=ARF[:, 4608:5120],
                     mx=ARF[:, 4105:4106]))
    SETS.append(dict(tq1=AR0[:, 1536:2048], tq2=res_f[:, 4, 0:512],
                     tq3=res_f[:, 4, 512:1024], scr=res_f[:, 4, 1024:1536],
                     mx=ARF[:, 4106:4107]))
    g2_ticks = []
    sguard = Guard(2)
    it2 = 0
    mx2b = None
    for tt in range(NTT):
        t0 = TT_F * tt
        for oc in range(24):
            S_ = SETS[it2 % 2]
            tq1, tq2, tq3 = S_["tq1"], S_["tq2"], S_["tq3"]
            rld = b.dma("sd" if it2 % 2 == 0 else "cd", tq1[:],
                        gu_park[0, oc, :, t0:t0 + TT_F],
                        waits=([gg2] + gu_ticks if it2 < 2 else []) +
                              ([sguard.wait(it2)] if sguard.wait(it2) else []))
            gq = qdq3(tq2[:], tq1[:], inv_gt, sc_gt, [rld, gg2, gz2], scr=S_["scr"])
            sg = b.emit("scalar", lambda s, tq2=tq2, tq3=tq3:
                        s.activation(tq3[:], tq2[:], AF.Sigmoid),
                        waits=[gq], inc=("aa", 1))
            f1 = b.emit("vector", lambda v, tq3=tq3: v.tensor_scalar(tq3[:], tq3[:], 65536.0, MAGIC,
                                                                     ALU.mult, ALU.add),
                        waits=[sg], inc=("vv", 1))
            f2 = b.emit("vector", lambda v, tq3=tq3: v.tensor_scalar(tq3[:], tq3[:], MAGIC, 65535.0,
                                                                     ALU.subtract, ALU.min),
                        waits=[f1], inc=("vv", 1))
            f3 = b.emit("vector", lambda v, tq3=tq3: v.tensor_scalar(tq3[:], tq3[:], 0.0,
                                                                     1.0 / 65536.0,
                                                                     ALU.max, ALU.mult),
                        waits=[f2], inc=("vv", 1))
            g2r = b.emit("vector", lambda v, tq2=tq2, tq3=tq3: v.tensor_mul(tq2[:], tq2[:], tq3[:]),
                         waits=[f3], inc=("vv", 1))
            mx1 = b.emit("vector",
                         lambda v, tq2=tq2, S_=S_: v.tensor_reduce(S_["mx"], tq2[:],
                                                                   AX.X, ALU.max,
                                                                   apply_absolute_value=True),
                         waits=[g2r], inc=("vv", 1))
            mx2b = b.emit("vector",
                          lambda v, S_=S_: v.tensor_tensor(g2max[:], g2max[:],
                                                           S_["mx"], ALU.max),
                          waits=[mx1], inc=("vv", 1))
            cstg = b.emit("vector", lambda v, tq1=tq1, tq2=tq2: v.tensor_copy(tq1[:], tq2[:]),
                          waits=[g2r], inc=("vv", 1))
            pk_ = b.dma("pk" if it2 % 2 == 0 else "gd",
                        gu_park[0, oc, :, t0:t0 + TT_F], tq1[:], waits=[cstg])
            sguard.set(it2, pk_)
            g2_ticks.append(pk_)
            it2 += 1
    gg3, inv_g2, sc_g2 = global_max(g2max[:], 3, 1, [mx2b] + g2_ticks[-1:])

    omax = ARF[:, 4107:4108]
    oz = b.emit("vector", lambda v: v.memset(omax[:], 0.0), inc=("vv", 1))
    up_b = [AR0[:, 2048:2560], AR0[:, 2560:3072]]
    o_ticks = []
    oguard = Guard(2)
    it3 = 0
    mo2 = None
    for tt in range(NTT):
        t0 = TT_F * tt
        for oc in range(24):
            S_ = SETS[it3 % 2]
            tq1, tq2, tq3 = S_["tq1"], S_["tq2"], S_["tq3"]
            ub = up_b[it3 % 2]
            rl1 = b.dma("sd" if it3 % 2 == 0 else "cd", tq1[:],
                        gu_park[0, oc, :, t0:t0 + TT_F],
                        waits=([gg3] + g2_ticks if it3 < 2 else []) +
                              ([oguard.wait(it3)] if oguard.wait(it3) else []))
            rl2 = b.dma("sd" if it3 % 2 == 0 else "cd", ub[:],
                        gu_park[1, oc, :, t0:t0 + TT_F],
                        waits=([gg3] if it3 < 2 else []) +
                              ([oguard.wait(it3)] if oguard.wait(it3) else []))
            g2q = qdq3(tq2[:], tq1[:], inv_g2, sc_g2, [rl1, gg3, oz], scr=S_["scr"])
            upq = qdq3(tq3[:], ub[:], inv_up, sc_up, [rl2, gg1, g2q], scr=S_["scr"])
            orw = b.emit("vector", lambda v, tq2=tq2, tq3=tq3: v.tensor_mul(tq2[:], tq2[:], tq3[:]),
                         waits=[g2q, upq], inc=("vv", 1))
            mo1 = b.emit("vector",
                         lambda v, tq2=tq2, S_=S_: v.tensor_reduce(S_["mx"], tq2[:],
                                                                   AX.X, ALU.max,
                                                                   apply_absolute_value=True),
                         waits=[orw], inc=("vv", 1))
            mo2 = b.emit("vector",
                         lambda v, S_=S_: v.tensor_tensor(omax[:], omax[:],
                                                          S_["mx"], ALU.max),
                         waits=[mo1], inc=("vv", 1))
            cst2 = b.emit("vector", lambda v, tq1=tq1, tq2=tq2: v.tensor_copy(tq1[:], tq2[:]),
                          waits=[orw], inc=("vv", 1))
            pk_ = b.dma("pk" if it3 % 2 == 0 else "gd",
                        gu_park[0, oc, :, t0:t0 + TT_F], tq1[:], waits=[cst2])
            oguard.set(it3, pk_)
            o_ticks.append(pk_)
            it3 += 1
    gg4, inv_o, sc_o = global_max(omax[:], 0, 1, [mo2] + o_ticks[-1:])

    if DEBUG:
        dbg_ticks.append(b.dma("sd", dbg["d_sc"][:], smalls[:, 208:224], waits=[gg4]))
        dbg_ticks.append(b.dma("sd", dbg["d_xq"].rearrange("c p t -> p c t"),
                               stream_b[:, :, :], waits=xq_ticks))
        dbg_ticks.append(b.dma("sd", dbg["d_gu"][0], gu_park[0, 0, :, :], waits=[rp]))
        dbg_ticks.append(b.dma("sd", dbg["d_gu"][1], gu_park[1, 0, :, :], waits=[rp]))
        dbg_ticks.append(b.dma("sd", dbg["d_wq"].rearrange("c p o -> p c o"),
                               wq_park[2, 6, 2:2 + NIC].rearrange("c p o -> p c o"),
                               waits=[qw_last]))
    oq = AR0[:, 20480:32768].rearrange("p (c t) -> p c t", t=TT_F)
    y4t = ARF[:, 0:4096].rearrange("p (c t) -> p c t", t=TT_F)
    outv = out.rearrange("(c p) t -> p c t", p=128)
    fin_ticks = []
    ln4_prev = None
    pk4_prev = None
    rp2 = None
    for tt in range(NTT):
        t0 = TT_F * tt
        oq_ticks = []
        for oc in range(24):
            S_ = SETS[oc % 2]
            tq1 = S_["tq1"]
            rl = b.dma("sd" if oc % 2 == 0 else "cd", tq1[:],
                       gu_park[0, oc, :, t0:t0 + TT_F],
                       waits=([gg4] + o_ticks if oc < 2 and tt == 0 else [])
                       + ([oq_ticks[-2]] if len(oq_ticks) >= 2 else
                          ([rp2] if rp2 else [])))
            qt = qdq3(oq[:, oc, :], tq1[:], inv_o, sc_o,
                      [rl, gg4] + ([ln4_prev] if ln4_prev else []), scr=S_["scr"])
            oq_ticks.append(qt)
        rp2 = oq_ticks[-1]
        y3ld = b.dma("cd", y4t[:, :, :],
                     y_park[2].rearrange("c p t -> p c t")[:, :, t0:t0 + TT_F],
                     waits=([ln4_prev, pk4_prev] if ln4_prev else g2_ticks[-2:]))
        s4t = []
        for oc in range(NIC):
            jb = (oc + tt) % 2
            wt = b.dma("sd", wbuf2[:, jb, 0:24, :],
                       wq_park[2, oc, 0:24].rearrange("c p o -> p c o"),
                       waits=[wgrd2.wait(jb)])
            qi = oc % 4
            tick = None
            for ic in range(24):
                tick = b.emit(
                    "tensor",
                    lambda t, oc=oc, ic=ic, qi=qi, jb=jb:
                    t.matmul(psA[:, 512 * qi:512 * qi + 512],
                             wbuf2[:, jb, ic, :], oq[:, ic, :],
                             start=(ic == 0), stop=(ic == 23)),
                    waits=([wt, gA.wait(qi)] + oq_ticks) if ic == 0 else [],
                    inc=("tm", 1) if ic == 23 else None)
            wgrd2.set(jb, tick)
            ev = b.emit("vector",
                        lambda v, oc=oc, qi=qi:
                        v.scalar_tensor_tensor(y4t[:, oc, :],
                                               psA[:, 512 * qi:512 * qi + 512], 1.0,
                                               y4t[:, oc, :], ALU.mult, ALU.add),
                        waits=[tick, y3ld], inc=("vv", 1))
            gA.set(qi, ev)
            s4t.append(ev)
        if DEBUG and tt == 0:
            dbg_ticks.append(b.dma("sd", dbg["d_oq"].rearrange("c p t -> p c t"),
                                   oq[:, :, :], waits=oq_ticks))
            s4t = s4t + [b.dma("sd", dbg["d_y4p"].rearrange("c p t -> p c t"),
                               y4t[:, :, :], waits=s4t)]
            dbg_ticks.append(s4t[-1])
        ln4_prev, pk4_prev = ln_tile(y4t, stream_b, t0, TT_F, s4t, 0, 20 + tt)
        fin = b.dma("sd", outv[:, :, t0:t0 + TT_F], y4t[:, :, :], waits=[ln4_prev])
        fin_ticks.append(fin)
        ln4_prev = fin

    b.emit("sync", None, waits=fin_ticks + dbg_ticks + b.qall("cd") + b.qall("pk"))

    b.finalize()
    return nc


def _layout_inputs(inputs):
    f = lambda a: np.ascontiguousarray(np.asarray(a), np.float32)
    x = np.asarray(inputs["L0_attn_x"])
    TT = lambda w: np.asarray(w).T
    wq, wk, wv, wo = (np.asarray(inputs["L0_attn_W" + k]) for k in "qkvo")
    shared = {
        "wqT": f(wq.T.reshape(NIC, 128, D)),
        "wkT": f(wk.T.reshape(NIC, 128, D)),
        "wvT": f(wv.T.reshape(NIC, 128, D)),
        "woT": f(wo.T.reshape(NIC, 128, D)),
        "bq": f(np.asarray(inputs["L0_attn_bq"]).reshape(NIC, 128).T),
        "bk": f(np.asarray(inputs["L0_attn_bk"]).reshape(NIC, 128).T),
        "bv": f(np.asarray(inputs["L0_attn_bv"]).reshape(NIC, 128).T),
        "bo": f(np.asarray(inputs["L0_attn_bo"]).reshape(NIC, 128).T),
        "wqsT": f(TT(inputs["L0_attn_Wqs"])),
        "wksT": f(TT(inputs["L0_attn_Wks"])),
        "wvsT": f(TT(inputs["L0_attn_Wvs"])),
        "bqs": f(inputs["L0_attn_bqs"]),
        "bks": f(inputs["L0_attn_bks"]),
        "bvs": f(inputs["L0_attn_bvs"]),
        "temp": f(np.asarray(inputs["L0_attn_temp"]).reshape(1)),
        "w1T": f(TT(inputs["L0_ffn_w1_w"]).reshape(NIC, 128, 32, 128).transpose(2, 0, 1, 3)),
        "w2T": f(TT(inputs["L0_ffn_w2_w"]).reshape(NIC, 128, 32, 128).transpose(2, 0, 1, 3)),
        "h1T": f(TT(inputs["L0_ffn_h1_w"]).reshape(32, 128, 32, 128).transpose(2, 0, 1, 3)),
        "h2T": f(TT(inputs["L0_ffn_h2_w"]).reshape(32, 128, 32, 128).transpose(2, 0, 1, 3)),
        "w3T": f(TT(inputs["L0_ffn_w3_w"]).reshape(32, 128, NIC, 128).transpose(2, 0, 1, 3)),
        "s1": f(np.asarray(inputs["L0_ffn_w1_s"]).reshape(32, 128).T),
        "s2": f(np.asarray(inputs["L0_ffn_w2_s"]).reshape(32, 128).T),
        "h1s": f(np.asarray(inputs["L0_ffn_h1_s"]).reshape(32, 128).T),
        "h2s": f(np.asarray(inputs["L0_ffn_h2_s"]).reshape(32, 128).T),
        "s3": f(np.asarray(inputs["L0_ffn_w3_s"]).reshape(NIC, 128).T),
        "l1wqT": f(TT(inputs["L1_attn_Wq"]).reshape(NIC, 128, 64)),
        "l1wkT": f(TT(inputs["L1_attn_Wk"]).reshape(NIC, 128, 64)),
        "l1wvT": f(TT(inputs["L1_attn_Wv"]).reshape(NIC, 128, 64)),
        "l1woT": f(TT(inputs["L1_attn_Wo"])),
        "wgate": f(np.asarray(inputs["L1_ffn_w_gate"]).reshape(24, 128, D)),
        "wup": f(np.asarray(inputs["L1_ffn_w_up"]).reshape(24, 128, D)),
        "wdown": f(np.asarray(inputs["L1_ffn_w_down"]).reshape(NIC, 128, 3072)),
    }
    per_core = []
    for c in range(NCORES):
        bi, hf = c // 2, c % 2
        own = x[bi, hf * TOK:(hf + 1) * TOK]
        oth = x[bi, (1 - hf) * TOK:(2 - hf) * TOK]
        xc = np.concatenate([own, oth], 0)
        d = dict(shared)
        d["xkv"] = f(xc.T.reshape(NIC, 128, S))
        d["pmask"] = f(np.tile(np.array([[1.0 - hf, float(hf)]], np.float32), (128, 1)))
        per_core.append(d)
    return per_core


_NC_CACHE = {}


def kernel(**inputs):
    per_core = _layout_inputs(inputs)
    if "nc" not in _NC_CACHE:
        _NC_CACHE["nc"] = build_program()
    nc = _NC_CACHE["nc"]
    res = run_bass_kernel_spmd(nc, per_core, core_ids=list(range(NCORES)))
    out = np.empty((B, S, D), np.float32)
    for c in range(NCORES):
        bi, hf = c // 2, c % 2
        o = res.results[c]["out"]
        out[bi, hf * TOK:(hf + 1) * TOK] = o.T
    return out


# version 09
# speedup vs baseline: 1.0128x; 1.0128x over previous
import numpy as np
import concourse.bass as bass
import concourse.mybir as mybir
from concourse.bass_utils import run_bass_kernel_spmd

F32 = mybir.dt.float32
BF16 = mybir.dt.bfloat16
AF = mybir.ActivationFunctionType
ALU = mybir.AluOpType
AX = mybir.AxisListType

DEBUG = False
B, S, D = 4, 4096, 1024
H, HD = 16, 64
NCORES = 8
TOK = 2048
NIC = 8
MAGIC = 12582912.0

ENGINES = ("tensor", "vector", "scalar", "gpsimd", "sync")

QUEUES = {
    "gd": ("gpsimd", 10),
    "pk": ("gpsimd", 1),
    "sd": ("sync", 10),
    "vt": ("sync", 4),
    "sp": ("sync", 1),
    "cd": ("scalar", 6),
}


class Builder:
    def __init__(self, nc):
        self.nc = nc
        self.ops = {e: [] for e in ENGINES}
        self.cnt = {}
        self.sems = {}
        for name in ("tm", "vv", "aa", "gg"):
            self.sems[name] = nc.alloc_semaphore(name=name)
            self.cnt[name] = 0
        self.qpool = {}
        self.qidx = {}
        for q, (eng, n) in QUEUES.items():
            self.qpool[q] = []
            self.qidx[q] = 0
            for i in range(n):
                nm = f"{q}{i}"
                self.sems[nm] = nc.alloc_semaphore(name=nm)
                self.cnt[nm] = 0
                self.qpool[q].append(nm)

    def emit(self, eng, fn=None, waits=(), inc=None):
        sems = self.sems
        merged = {}
        for w in waits:
            if w is None:
                continue
            s, v = w
            if v is None:
                continue
            merged[s] = max(merged.get(s, 0), v)
        wl = sorted(merged.items())
        if inc is not None:
            self.cnt[inc[0]] += inc[1]
        post = self.cnt[inc[0]] if inc is not None else None

        def run(e):
            for s, v in wl:
                e.wait_ge(sems[s], v)
            if fn is not None:
                instr = fn(e)
                if inc is not None:
                    instr.then_inc(sems[inc[0]], inc[1])
        self.ops[eng].append(run)
        return (inc[0], post) if inc is not None else None

    def dma(self, queue, out, in_, waits=(), transpose=False):
        eng = QUEUES[queue][0]
        pool = self.qpool[queue]
        nm = pool[self.qidx[queue] % len(pool)]
        self.qidx[queue] += 1
        w = list(waits)
        if self.cnt[nm] > 0:
            w.append((nm, self.cnt[nm]))
        return self.emit(eng, lambda e: e.dma_start(out=out, in_=in_, transpose=transpose),
                         waits=w, inc=(nm, 16))

    def qall(self, queue):
        return [(nm, self.cnt[nm]) for nm in self.qpool[queue] if self.cnt[nm] > 0]

    def finalize(self):
        nc = self.nc
        block_cm = nc.Block()
        block = block_cm.__enter__()
        for eng in ENGINES:
            opslist = self.ops[eng]
            getattr(block, eng)(lambda e, _l=opslist: [f(e) for f in _l])
        block_cm.__exit__(None, None, None)


class Guard:
    def __init__(self, n):
        self.t = [None] * n

    def wait(self, i):
        return self.t[i % len(self.t)]

    def all(self):
        return [t for t in self.t if t is not None]

    def set(self, i, tick):
        self.t[i % len(self.t)] = tick


def build_program():
    nc = bass.Bass("TRN2", target_bir_lowering=False, num_devices=NCORES)

    def inp(name, shape, dtype=F32):
        return nc.dram_tensor(name, shape, dtype, kind="ExternalInput").ap()

    xkv = inp("xkv", [NIC, 128, S])
    wqT = inp("wqT", [NIC, 128, D])
    wkT = inp("wkT", [NIC, 128, D])
    wvT = inp("wvT", [NIC, 128, D])
    woT = inp("woT", [NIC, 128, D])
    bq = inp("bq", [128, NIC])
    bk = inp("bk", [128, NIC])
    bv = inp("bv", [128, NIC])
    bo = inp("bo", [128, NIC])
    wqsT = inp("wqsT", [64, 3])
    wksT = inp("wksT", [64, 3])
    wvsT = inp("wvsT", [64, 3])
    bqs = inp("bqs", [3])
    bks = inp("bks", [3])
    bvs = inp("bvs", [3])
    temp = inp("temp", [1])
    w1T = inp("w1T", [32, NIC, 128, 128])
    w2T = inp("w2T", [32, NIC, 128, 128])
    h1T = inp("h1T", [32, 32, 128, 128])
    h2T = inp("h2T", [32, 32, 128, 128])
    w3T = inp("w3T", [NIC, 32, 128, 128])
    s1 = inp("s1", [128, 32])
    s2 = inp("s2", [128, 32])
    h1s = inp("h1s", [128, 32])
    h2s = inp("h2s", [128, 32])
    s3 = inp("s3", [128, NIC])
    l1wqT = inp("l1wqT", [NIC, 128, 64])
    l1wkT = inp("l1wkT", [NIC, 128, 64])
    l1wvT = inp("l1wvT", [NIC, 128, 64])
    l1woT = inp("l1woT", [64, D])
    wgate = inp("wgate", [24, 128, D])
    wup = inp("wup", [24, 128, D])
    wdown = inp("wdown", [NIC, 128, 3072])
    pmask = inp("pmask", [128, 2])

    out = nc.dram_tensor("out", [D, TOK], F32, kind="ExternalOutput").ap()
    dbg = {}
    dbg_ticks = []
    if DEBUG:
        for nm, shape, dt in (("d_kpair", [128, S], BF16), ("d_vpair", [128, S], BF16),
                              ("d_qpair", [128, TOK], BF16), ("d_stpK", [128, S], BF16),
                              ("d_stpQ", [128, TOK], BF16), ("d_attn", [H, 64, TOK], BF16),
                              ("d_y1", [NIC, 128, TOK], F32), ("d_y2", [NIC, 128, TOK], F32),
                              ("d_qf", [128, TOK], BF16), ("d_kf", [128, TOK], BF16),
                              ("d_vf", [128, TOK], BF16), ("d_num", [128, TOK], F32),
                              ("d_st", [128, 80], F32), ("d_outn", [128, TOK], BF16),
                              ("d_y3", [NIC, 128, TOK], F32),
                              ("d_y3p", [NIC, 128, 512], F32), ("d_sc", [128, 16], F32),
                              ("d_xq", [NIC, 128, TOK], BF16), ("d_gu", [2, 128, TOK], BF16),
                              ("d_wq", [NIC, 128, 128], BF16),
                              ("d_oq", [24, 128, 512], BF16), ("d_y4p", [NIC, 128, 512], F32),
                              ("d_ps", [128, 512], F32), ("d_y3ld", [NIC, 128, 512], F32)):
            dbg[nm] = nc.dram_tensor(nm, shape, dt, kind="ExternalOutput").ap()

    scr_row = nc.dram_tensor("scr_row", [8192], F32, kind="Internal").ap()
    scr_l = nc.dram_tensor("scr_l", [TOK], F32, kind="Internal").ap()
    y_park = nc.dram_tensor("y_park", [3, NIC, 128, TOK], F32, kind="Internal").ap()
    attn_park = nc.dram_tensor("attn_park", [H, 64, TOK], BF16, kind="Internal").ap()
    gu_park = nc.dram_tensor("gu_park", [2, 24, 128, TOK], BF16, kind="Internal").ap()
    hc_w1 = nc.dram_tensor("hc_w1", [32, NIC, 128, 128], BF16, kind="Internal").ap()
    hc_w2 = nc.dram_tensor("hc_w2", [32, NIC, 128, 128], BF16, kind="Internal").ap()
    hc_h1 = nc.dram_tensor("hc_h1", [32, 32, 128, 128], BF16, kind="Internal").ap()
    hc_h2 = nc.dram_tensor("hc_h2", [32, 32, 128, 128], BF16, kind="Internal").ap()
    hc_w3 = nc.dram_tensor("hc_w3", [NIC, 32, 128, 128], BF16, kind="Internal").ap()
    wq_park = nc.dram_tensor("wq_park", [3, 24, NIC * 3, 128, 128], BF16, kind="Internal").ap()
    wqn_park = nc.dram_tensor("wqn_park", [3, 24, 128, 3072], BF16, kind="Internal").ap()
    cc_in = [nc.dram_tensor(f"cc_in{i}", [4], F32, kind="Internal").ap() for i in range(4)]
    cc_out = [nc.dram_tensor(f"cc_out{i}", [4], F32, kind="Internal").ap() for i in range(4)]
    st_in = nc.dram_tensor("st_in", [64 * 80], F32, kind="Internal").ap()
    st_out = nc.dram_tensor("st_out", [64 * 80], F32, kind="Internal").ap()
    scr_pm = nc.dram_tensor("scr_pm", [128], F32, kind="Internal").ap()

    b = Builder(nc)
    A = nc.alloc_sbuf_tensor
    P = nc.alloc_psum_tensor

    AR0 = A("AR0", [128, 32768], BF16)
    STR = A("STR", [128, 16384], BF16)
    AR2 = A("AR2", [128, 16384], BF16)
    AR3 = A("AR3", [128, 8192], BF16)
    AR4 = A("AR4", [128, 16384], BF16)
    ARF = A("ARF", [128, 6144], F32)
    mcols = A("mcols", [128, 16], F32)
    mcb = A("mcb", [128, 128], BF16)
    mtr = A("mtr", [128, 128], BF16)
    smalls = A("smalls", [128, 256], F32)
    smallw = A("smallw", [128, 64], BF16)
    ident = A("ident", [128, 128], BF16)
    tri32 = A("tri32", [128, 128], F32)
    pmsk = A("pmsk", [128, 2], F32)
    onescol = A("onescol", [128, 1], BF16)

    res_h = nc.alloc_sbuf_tensor_at("res_f", [128, NIC, TOK], F32,
                                    offset=nc.lookup_mloc(AR0).addr)
    res_f = res_h.ap()

    psA = P("psA", [128, 2048], F32)
    psB = P("psB", [128, 2048], F32)

    xkv_b = AR0[:].rearrange("p (c t) -> p c t", c=NIC)
    stream_b = STR[:].rearrange("p (c t) -> p c t", c=NIC)
    wq_s = AR2[:, 0:8192].rearrange("p (c o) -> p c o", c=NIC)
    wk_s = AR2[:, 8192:16384].rearrange("p (c o) -> p c o", c=NIC)
    wv_s = AR3[:, 0:8192].rearrange("p (c o) -> p c o", c=NIC)
    kpair = AR4[:, 0:4096]
    vpair = AR4[:, 4096:8192]
    qpair = AR4[:, 8192:10240]
    stpK = AR4[:, 10240:14336]
    stpQ = AR4[:, 14336:16384]
    at_buf = STR[:, 0:2048].rearrange("p (h j t) -> p h j t", h=2, j=2)
    vtm = STR[:, 2048:7168].rearrange("p (k c) -> p k c", k=32)
    VB = 80
    kk6 = ARF[:, 0:2048]
    lrow = ARF[:, 2048:4096]
    avstage = ARF[:, 0:2048]
    statw = ARF[:]

    c_bq, c_bk, c_bv, c_bo = 0, 8, 16, 24
    c_tmp, c_bks6, c_bvs6, c_bqs6 = 32, 33, 34, 35
    c_wks, c_wvs, c_wqs = 40, 43, 46
    c_s1, c_s2, c_h1s, c_h2s, c_s3 = 64, 96, 128, 160, 192

    g_io = b.emit("gpsimd", lambda g: g.iota(ARF[:, 0:128], [[1, 128]], base=0,
                                             channel_multiplier=-1,
                                             allow_small_or_imprecise_dtypes=True),
                  inc=("gg", 1))
    id_done = b.emit("vector", lambda v: v.tensor_scalar(ident[:], ARF[:, 0:128],
                                                         0.0, None, ALU.is_equal),
                     waits=[g_io], inc=("vv", 1))
    tri_done = b.emit("vector", lambda v: v.tensor_scalar(tri32[:], ARF[:, 0:128],
                                                          0.0, None, ALU.is_ge),
                      waits=[g_io], inc=("vv", 1))
    pm_ld = b.dma("sp", pmsk[:], pmask[:])
    ones_done = b.emit("vector", lambda v: v.memset(onescol[:], 1.0), inc=("vv", 1))
    mcb_init = b.emit("vector", lambda v: v.memset(mcb[:], 0.0), inc=("vv", 1))

    ld_xkv = b.dma("gd", xkv_b[:, :, :], xkv.rearrange("c p t -> p c t"))
    ld_wq = b.dma("gd", wq_s, wqT.rearrange("c p o -> p c o"))
    ld_wk = b.dma("gd", wk_s, wkT.rearrange("c p o -> p c o"))
    ld_wv = b.dma("gd", wv_s, wvT.rearrange("c p o -> p c o"))

    ld_b = None
    for col, bsrc in ((c_bq, bq), (c_bk, bk), (c_bv, bv), (c_bo, bo)):
        ld_b = b.dma("sp", smalls[:, col:col + NIC], bsrc[:])
    ld_tmp = b.dma("sp", smalls[:, c_tmp:c_tmp + 1][:, None, :],
                   temp[None, :].partition_broadcast(128))
    ld_t = None
    for col, bsrc in ((c_wks, wksT), (c_wvs, wvsT), (c_wqs, wqsT)):
        ld_t = b.dma("sp", smalls[0:64, col:col + 3], bsrc[:])
        ld_t = b.dma("sp", smalls[64:128, col:col + 3], bsrc[:])
    for base in (0, 64):
        for cc, bsrc, p1 in ((c_bks6, bks, True), (c_bvs6, bvs, False)):
            pa, pb_ = ((1, 2, 0), (2, 0, 1)) if p1 else ((2, 0, 1), (1, 2, 0))
            for j in range(3):
                ld_t = b.dma("sp", smalls[base + j:base + j + 1, cc:cc + 1],
                             bsrc[pa[j]:pa[j] + 1][:, None])
                ld_t = b.dma("sp", smalls[base + 3 + j:base + 4 + j, cc:cc + 1],
                             bsrc[pb_[j]:pb_[j] + 1][:, None])
        ld_t = b.dma("sp", smalls[base + 0:base + 3, c_bqs6:c_bqs6 + 1], bqs[:, None])
        ld_t = b.dma("sp", smalls[base + 3:base + 6, c_bqs6:c_bqs6 + 1], bqs[:, None])
    c_stg = 60
    ld_t = b.dma("sp", smalls[32:35, c_stg:c_stg + 1], bqs[:, None])
    for col, ssrc in ((c_s1, s1), (c_s2, s2), (c_h1s, h1s), (c_h2s, h2s), (c_s3, s3)):
        ld_t = b.dma("sp", smalls[:, col:col + (NIC if col == c_s3 else 32)], ssrc[:])
    wait_tiny = ld_t
    negs = b.emit("vector", lambda v: v.tensor_scalar(smalls[32:35, c_stg:c_stg + 1],
                                                      smalls[32:35, c_stg:c_stg + 1],
                                                      -1.0, None, ALU.mult),
                  waits=[wait_tiny], inc=("vv", 1))
    neg1 = b.dma("sp", smalls[3:6, c_bqs6:c_bqs6 + 1], smalls[32:35, c_stg:c_stg + 1],
                 waits=[negs])
    neg2 = b.dma("sp", smalls[67:70, c_bqs6:c_bqs6 + 1], smalls[32:35, c_stg:c_stg + 1],
                 waits=[negs])

    extk = smallw[:, 0:12]
    extv = smallw[:, 12:24]
    extq = smallw[:, 24:36]
    zext = b.emit("vector", lambda v: v.memset(smallw[:, 0:40], 0.0), inc=("vv", 1))

    def build_ext(v):
        kp1 = (1, 2, 0)
        kp2 = (2, 0, 1)
        last = None
        for half, base in ((0, 0), (1, 64)):
            r = slice(base, base + 64)
            for j in range(3):
                co = half * 6
                last = v.tensor_copy(extk[r, co + j:co + j + 1],
                                     smalls[r, c_wks + kp1[j]:c_wks + kp1[j] + 1])
                last = v.tensor_copy(extk[r, co + 3 + j:co + 4 + j],
                                     smalls[r, c_wks + kp2[j]:c_wks + kp2[j] + 1])
                last = v.tensor_copy(extv[r, co + j:co + j + 1],
                                     smalls[r, c_wvs + kp2[j]:c_wvs + kp2[j] + 1])
                last = v.tensor_copy(extv[r, co + 3 + j:co + 4 + j],
                                     smalls[r, c_wvs + kp1[j]:c_wvs + kp1[j] + 1])
                last = v.tensor_copy(extq[r, co + j:co + j + 1],
                                     smalls[r, c_wqs + j:c_wqs + j + 1])
                last = v.tensor_scalar(extq[r, co + 3 + j:co + 4 + j],
                                       smalls[r, c_wqs + j:c_wqs + j + 1],
                                       -1.0, None, ALU.mult)
        return last
    ext_done = b.emit("vector", build_ext, waits=[wait_tiny, zext], inc=("vv", 1))

    stka = b.emit("vector", lambda v: v.memset(stpK[:], 0.0), inc=("vv", 1))
    stk0 = b.emit("vector", lambda v: v.memset(stpK[32:33, :], -1.0),
                  waits=[stka], inc=("vv", 1))
    b.dma("sp", stpK[6:7, :], stpK[32:33, :], waits=[stk0])
    stk_init = b.dma("sp", stpK[70:71, :], stpK[32:33, :], waits=[stk0])
    stq_init = b.emit("vector", lambda v: v.memset(stpQ[:], 0.0), inc=("vv", 1))
    vt0 = b.emit("vector", lambda v: v.memset(vtm[:], 0.0), inc=("vv", 1))
    vt1 = b.emit("vector", lambda v: v.memset(vtm[:, :, 64:65], 1.0),
                 waits=[vt0], inc=("vv", 1))
    vtm_init = b.emit("vector", lambda v: v.memset(vtm[:, :, VB + 64:VB + 65], 1.0),
                      waits=[vt1], inc=("vv", 1))

    pc_ticks = []
    pc_jobs = []
    for dst_t, src_t, n_oc in ((hc_w1, w1T, 32), (hc_w2, w2T, 32), (hc_h1, h1T, 32),
                               (hc_h2, h2T, 32), (hc_w3, w3T, NIC)):
        for o0 in range(0, n_oc, 8):
            pc_jobs.append((dst_t, src_t, o0))

    def emit_precast(n):
        while n > 0 and pc_jobs:
            dst_t, src_t, o0 = pc_jobs.pop(0)
            pc_ticks.append(b.dma("gd", dst_t[o0:o0 + 8], src_t[o0:o0 + 8]))
            n -= 1

    NT_S = S // 512
    NT_Q = TOK // 512
    gA = Guard(4)
    gB = Guard(4)
    av_guard = None
    vst_guard = None
    vtm_guard = None
    pair_sb_guard = None
    mc_guard = None
    c_done = None
    last_proj_tick = None

    for hp in range(8):
        emit_precast(3)
        hA, hB = 2 * hp, 2 * hp + 1
        kv_evac = q_evac = None
        for which, wsl, dst, bcol in ((0, wk_s, kpair, c_bk), (1, wv_s, vpair, c_bv),
                                      (2, wq_s, qpair, c_bq)):
            ntc = NT_S if which < 2 else NT_Q
            for tc in range(ntc):
                qi = tc % 4
                first_w = []
                if hp == 0 and which == 0 and tc == 0:
                    first_w = [ld_xkv, ld_wq, ld_wk, ld_wv]
                if hp > 0 and which == 0 and tc == 0:
                    first_w = [pair_sb_guard] + (dbg_ticks if hp == 1 else [])
                tick = None
                for ic in range(NIC):
                    tick = b.emit(
                        "tensor",
                        lambda t, wsl=wsl, ic=ic, hp=hp, tc=tc, qi=qi:
                        t.matmul(psA[:, 512 * qi:512 * qi + 512],
                                 wsl[:, ic, 128 * hp:128 * hp + 128],
                                 xkv_b[:, ic, 512 * tc:512 * tc + 512],
                                 start=(ic == 0), stop=(ic == NIC - 1)),
                        waits=(first_w + [gA.wait(qi)]) if ic == 0 else [],
                        inc=("tm", 1) if ic == NIC - 1 else None)
                last_proj_tick = tick
                if which < 2:
                    ev = b.emit("scalar",
                                lambda s, dst=dst, qi=qi, hp=hp, tc=tc, bcol=bcol:
                                s.activation(dst[:, 512 * tc:512 * tc + 512],
                                             psA[:, 512 * qi:512 * qi + 512],
                                             AF.Identity, bias=smalls[:, bcol + hp:bcol + hp + 1],
                                             scale=1.0),
                                waits=[tick, ld_b], inc=("aa", 1))
                    kv_evac = ev
                else:
                    ev = b.emit("vector",
                                lambda v, qi=qi, hp=hp, tc=tc, bcol=bcol:
                                v.tensor_scalar(qpair[:, 512 * tc:512 * tc + 512],
                                                psA[:, 512 * qi:512 * qi + 512],
                                                smalls[:, bcol + hp:bcol + hp + 1],
                                                smalls[:, c_tmp:c_tmp + 1],
                                                ALU.add, ALU.mult),
                                waits=[tick, ld_b, ld_tmp], inc=("vv", 1))
                    q_evac = ev
                gA.set(qi, ev)

        vt_done = None
        for kt in range(32):
            w0 = [kv_evac, vtm_init] if kt == 0 else []
            if kt == 0 and vtm_guard is not None:
                w0.append(vtm_guard)
            b.dma("vt", vtm[:, kt, 0:64], vpair[0:64, 128 * kt:128 * kt + 128],
                  waits=w0, transpose=True)
            vt_done = b.dma("vt", vtm[:, kt, VB:VB + 64], vpair[64:128, 128 * kt:128 * kt + 128],
                            transpose=True)

        stp_mul = None
        for sh in range(2):
            k_ev = {}
            vmul = {}
            for half, pbase in ((0, 0), (1, 64)):
                tp = (0, 64) if half == 1 else None
                for t4 in range(4):
                    tc = 4 * sh + t4
                    qi = t4
                    tick = b.emit(
                        "tensor",
                        lambda t, half=half, tc=tc, pbase=pbase, tp=tp, qi=qi:
                        t.matmul(psB[pbase:pbase + 6, 512 * qi:512 * qi + 512],
                                 extk[:, half * 6:half * 6 + 6],
                                 kpair[:, 512 * tc:512 * tc + 512],
                                 start=True, stop=True, tile_position=tp),
                        waits=([ext_done, kv_evac] if tc == 0 and sh == 0 else []) + [gB.wait(qi)]
                              + ([k_ev[(0, t4)]] if half == 1 else []),
                        inc=("tm", 1))
                    k_ev[(half, t4)] = b.emit(
                        "scalar",
                        lambda s, pbase=pbase, t4=t4, qi=qi:
                        s.activation(kk6[pbase:pbase + 6, 512 * t4:512 * t4 + 512],
                                     psB[pbase:pbase + 6, 512 * qi:512 * qi + 512],
                                     AF.Identity, bias=smalls[pbase:pbase + 6, c_bks6:c_bks6 + 1],
                                     scale=1.0),
                        waits=[tick, wait_tiny] +
                              ([stp_mul, vst_guard] if t4 == 0 and half == 0 else []),
                        inc=("aa", 1))
                    gB.set(qi, k_ev[(half, t4)])
            for half, pbase in ((0, 0), (1, 64)):
                tp = (0, 64) if half == 1 else None
                for t4 in range(4):
                    tc = 4 * sh + t4
                    qi = t4
                    tick = b.emit(
                        "tensor",
                        lambda t, half=half, tc=tc, pbase=pbase, tp=tp, qi=qi:
                        t.matmul(psB[pbase:pbase + 6, 512 * qi:512 * qi + 512],
                                 extv[:, half * 6:half * 6 + 6],
                                 vpair[:, 512 * tc:512 * tc + 512],
                                 start=True, stop=True, tile_position=tp),
                        waits=[gB.wait(qi)] + ([vmul[(0, t4)]] if half == 1 and (0, t4) in vmul else []),
                        inc=("tm", 1))
                    stp_mul = b.emit(
                        "vector",
                        lambda v, pbase=pbase, tc=tc, t4=t4, qi=qi:
                        v.scalar_tensor_tensor(stpK[pbase:pbase + 6, 512 * tc:512 * tc + 512],
                                               psB[pbase:pbase + 6, 512 * qi:512 * qi + 512],
                                               smalls[pbase:pbase + 6, c_bvs6:c_bvs6 + 1],
                                               kk6[pbase:pbase + 6, 512 * t4:512 * t4 + 512],
                                               ALU.add, ALU.mult),
                        waits=[tick, k_ev[(half, t4)], stk_init], inc=("vv", 1))
                    vmul[(half, t4)] = stp_mul
                    gB.set(qi, stp_mul)
        stq_ev = None
        qev = {}
        for half, pbase in ((0, 0), (1, 64)):
            tp = (0, 64) if half == 1 else None
            for tc in range(NT_Q):
                qi = tc % 4
                tick = b.emit(
                    "tensor",
                    lambda t, half=half, tc=tc, pbase=pbase, tp=tp, qi=qi:
                    t.matmul(psB[pbase:pbase + 6, 512 * qi:512 * qi + 512],
                             extq[:, half * 6:half * 6 + 6],
                             qpair[:, 512 * tc:512 * tc + 512],
                             start=True, stop=True, tile_position=tp),
                    waits=([q_evac, ext_done, stq_init] if tc == 0 and half == 0 else []) +
                          [gB.wait(qi)] + ([qev[(0, tc)]] if half == 1 else []),
                    inc=("tm", 1))
                stq_ev = b.emit(
                    "vector",
                    lambda v, pbase=pbase, tc=tc, qi=qi:
                    v.tensor_scalar(stpQ[pbase:pbase + 6, 512 * tc:512 * tc + 512],
                                    psB[pbase:pbase + 6, 512 * qi:512 * qi + 512],
                                    smalls[pbase:pbase + 6, c_bqs6:c_bqs6 + 1],
                                    smalls[pbase:pbase + 6, c_tmp:c_tmp + 1],
                                    ALU.add, ALU.mult),
                    waits=[tick, neg1, neg2, ld_tmp], inc=("vv", 1))
                qev[(half, tc)] = stq_ev
                gB.set(qi, stq_ev)

        ksub = STR[:, 7168:7424]
        ssub = STR[:, 7424:7680]
        kc1 = b.emit("vector", lambda v: v.tensor_copy(ksub[:, :], kpair[:, ::16]),
                     waits=[kv_evac] + ([c_done] if c_done else []), inc=("vv", 1))
        kc2 = b.emit("vector", lambda v: v.tensor_copy(ssub[:, :], stpK[:, ::16]),
                     waits=[stp_mul, kc1], inc=("vv", 1))
        for half, pbase in ((0, 0), (1, 64)):
            h = hA if half == 0 else hB
            red = None
            for qt in range(16):
                qi = qt % 4
                b.emit("tensor",
                       lambda t, pbase=pbase, qt=qt, qi=qi:
                       t.matmul(psB[:, 512 * qi:512 * qi + 256],
                                qpair[pbase:pbase + 64, 128 * qt:128 * qt + 128],
                                ksub[pbase:pbase + 64, :],
                                start=True, stop=False),
                       waits=([kc2, stq_ev] if qt == 0 and half == 0 else []) +
                             [gB.wait(qi)])
                tick = b.emit("tensor",
                              lambda t, pbase=pbase, qt=qt, qi=qi:
                              t.matmul(psB[:, 512 * qi:512 * qi + 256],
                                       stpQ[pbase:pbase + 6, 128 * qt:128 * qt + 128],
                                       ssub[pbase:pbase + 6, :],
                                       start=False, stop=True),
                              inc=("tm", 1))
                red = b.emit("vector",
                             lambda v, qt=qt, qi=qi:
                             v.tensor_reduce(mcols[:, qt:qt + 1],
                                             psB[:, 512 * qi:512 * qi + 256],
                                             AX.X, ALU.max),
                             waits=[tick] + ([mc_guard] if qt == 0 else []),
                             inc=("vv", 1))
                gB.set(qi, red)
            cst = b.emit("vector", lambda v: v.tensor_copy(mcb[:, 0:16], mcols[:, :]),
                         waits=[red, mcb_init] + ([mc_guard] if mc_guard else []),
                         inc=("vv", 1))
            tr = b.dma("vt", mtr[:, :], mcb[:, :],
                       waits=[cst] + ([c_done] if c_done else []), transpose=True)
            mc_guard = tr
            c_done = b.dma("sd", stpQ[pbase + 6:pbase + 7, :], mtr[0:16, :],
                           waits=[tr])

        if DEBUG and hp == 0:
            for nm, tens in (("d_kpair", kpair), ("d_vpair", vpair), ("d_qpair", qpair),
                             ("d_stpK", stpK), ("d_stpQ", stpQ)):
                dbg_ticks.append(b.dma("sd", dbg[nm][:], tens[:],
                                       waits=[c_done, stp_mul, stq_ev]))
        av_mm_last = {0: None, 1: None}
        exp_ticks = {0: None, 1: None}
        exp_hist = {}
        av_hist = {}
        first_qq = True
        for qq in range(4):
            q0 = 512 * qq
            for kt in range(33):
                if kt < 32:
                    jj = kt % 2
                    for half, pbase, sc0 in ((0, 0, 0), (1, 64, 1024)):
                        scol = sc0 + 512 * jj
                        b.emit("tensor",
                               lambda t, pbase=pbase, scol=scol, kt=kt, q0=q0:
                               t.matmul(psA[:, scol:scol + 512],
                                        kpair[pbase:pbase + 64, 128 * kt:128 * kt + 128],
                                        qpair[pbase:pbase + 64, q0:q0 + 512],
                                        start=True, stop=False),
                               waits=(([c_done] + b.qall("vt") + gA.all())
                                      if first_qq and kt == 0 and half == 0 else []) +
                                     ([exp_hist.get((half, kt - 2))] if kt >= 2 else
                                      ([exp_hist.get((half, 30 + kt)), exp_hist.get((half, 31 + kt))]
                                       if not first_qq else [])))
                    tick_h = {}
                    for half, pbase, sc0 in ((0, 0, 0), (1, 64, 1024)):
                        scol = sc0 + 512 * jj
                        tick_h[half] = b.emit(
                            "tensor",
                            lambda t, pbase=pbase, scol=scol, kt=kt, q0=q0:
                            t.matmul(psA[:, scol:scol + 512],
                                     stpK[pbase:pbase + 7, 128 * kt:128 * kt + 128],
                                     stpQ[pbase:pbase + 7, q0:q0 + 512],
                                     start=False, stop=True),
                            inc=("tm", 1))
                if kt >= 1:
                    for half in (0, 1):
                        av_mm_last[half] = b.emit(
                            "tensor",
                            lambda t, half=half, kt=kt:
                            t.matmul(psB[0:65, 512 * half:512 * half + 512],
                                     vtm[:, kt - 1, VB * half:VB * half + 65],
                                     at_buf[:, half, (kt - 1) % 2, :],
                                     start=(kt - 1 == 0), stop=(kt - 1 == 31)),
                            waits=[exp_hist[(half, kt - 1)]] +
                                  ([av_guard] if kt == 1 and av_guard else []),
                            inc=("tm", 1))
                        av_hist[(half, kt - 1)] = av_mm_last[half]
                if kt < 32:
                    for half, pbase, sc0 in ((0, 0, 0), (1, 64, 1024)):
                        scol = sc0 + 512 * jj
                        exp_hist[(half, kt)] = exp_ticks[half] = b.emit(
                            "scalar",
                            lambda s, half=half, scol=scol, kt=kt:
                            s.activation(at_buf[:, half, kt % 2, :],
                                         psA[:, scol:scol + 512], AF.Exp),
                            waits=[tick_h[half]] +
                                  ([av_hist.get((half, kt - 2))] if kt >= 2 else []),
                            inc=("aa", 1))
            first_qq = False
            av_done = av_mm_last[1]
            rc = b.emit("vector",
                        lambda v: v.reciprocal(lrow[64:65, 0:1024], psB[64:65, 0:1024]),
                        waits=[av_done, av_mm_last[0]], inc=("vv", 1))
            s1_ = b.dma("sd", scr_l[None, 0:1024], lrow[64:65, 0:1024], waits=[rc])
            s2_ = b.dma("sd", lrow[0:64, None, 0:1024],
                        scr_l[None, 0:1024].partition_broadcast(64),
                        waits=[s1_])
            e1 = b.emit("vector",
                        lambda v:
                        v.tensor_mul(avstage[0:64, 0:512],
                                     psB[0:64, 0:512], lrow[0:64, 0:512]),
                        waits=[s2_] + ([vst_guard] if vst_guard else []), inc=("vv", 1))
            e2 = b.emit("vector",
                        lambda v:
                        v.tensor_mul(avstage[0:64, 512:1024],
                                     psB[0:64, 512:1024], lrow[0:64, 512:1024]),
                        inc=("vv", 1))
            av_guard = e2
            b.dma("pk", attn_park[hA, :, q0:q0 + 512], avstage[0:64, 0:512],
                  waits=[e1])
            vst_guard = b.dma("pk", attn_park[hB, :, q0:q0 + 512],
                              avstage[0:64, 512:1024], waits=[e2])
        vtm_guard = av_done
        pair_sb_guard = av_done
        for i in range(4):
            gA.set(i, exp_ticks[1])
            gB.set(i, e2)

    emit_precast(99)
    ld_wo = b.dma("gd", wq_s, woT.rearrange("c p o -> p c o"), waits=[pair_sb_guard])
    ld_attn = b.dma("sd", stream_b[:, :, :],
                    attn_park.rearrange("(c w) p t -> (w p) c t", w=2),
                    waits=[vst_guard])
    ld_xf = b.dma("sd", res_f[:, :, :],
                  xkv.rearrange("c p t -> p c t")[:, :, 0:TOK],
                  waits=[last_proj_tick])
    y1_done = None
    for oc in range(NIC):
        for tc in range(NT_Q):
            qi = (oc * NT_Q + tc) % 4
            tick = None
            for ic in range(NIC):
                tick = b.emit(
                    "tensor",
                    lambda t, oc=oc, tc=tc, ic=ic, qi=qi:
                    t.matmul(psA[:, 512 * qi:512 * qi + 512],
                             wq_s[:, ic, 128 * oc:128 * oc + 128],
                             stream_b[:, ic, 512 * tc:512 * tc + 512],
                             start=(ic == 0), stop=(ic == NIC - 1)),
                    waits=(([ld_wo, ld_attn] if oc == 0 and tc == 0 else []) +
                           [gA.wait(qi)]) if ic == 0 else [],
                    inc=("tm", 1) if ic == NIC - 1 else None)
            y1_done = b.emit(
                "vector",
                lambda v, oc=oc, tc=tc, qi=qi:
                v.scalar_tensor_tensor(res_f[:, oc, 512 * tc:512 * tc + 512],
                                       psA[:, 512 * qi:512 * qi + 512],
                                       smalls[:, c_bo + oc:c_bo + oc + 1],
                                       res_f[:, oc, 512 * tc:512 * tc + 512],
                                       ALU.add, ALU.add),
                waits=[tick, ld_xf, ld_b], inc=("vv", 1))
            gA.set(qi, y1_done)

    def layernorm(src_f32, wait_prev, park_idx):
        sb = stream_b
        sqv = AR2[:].rearrange("p (c t) -> p c t", c=NIC)
        cp = b.emit("vector", lambda v: v.tensor_copy(sb[:, :, :], src_f32[:, :, :]),
                    waits=[wait_prev], inc=("vv", 1))
        sq = b.emit("vector", lambda v: v.tensor_mul(sqv[:, :, :], sb[:, :, :], sb[:, :, :]),
                    waits=[cp], inc=("vv", 1))
        st_done = None
        for stat_i, (srcb, ps) in enumerate(((sb, psA), (sqv, psB))):
            for nt in range(4):
                for ic in range(NIC):
                    st_done = b.emit(
                        "tensor",
                        lambda t, srcb=srcb, ps=ps, ic=ic, nt=nt:
                        t.matmul(ps[0:1, 512 * nt:512 * nt + 512], onescol[:],
                                 srcb[:, ic, 512 * nt:512 * nt + 512],
                                 start=(ic == 0), stop=(ic == NIC - 1)),
                        waits=[sq, ones_done] if ic == 0 and nt == 0 and stat_i == 0 else [],
                        inc=("tm", 1) if ic == NIC - 1 else None)
        Pv, Qv, Rv = statw[0:1, 0:2048], statw[0:1, 2048:4096], statw[0:1, 4096:6144]
        w = b.emit("vector", lambda v: v.tensor_scalar(Pv, psA[0:1, 0:2048],
                                                       1.0 / 1024, None, ALU.mult),
                   waits=[st_done], inc=("vv", 1))
        w = b.emit("vector", lambda v: v.tensor_scalar(Qv, psB[0:1, 0:2048],
                                                       1.0 / 1024, None, ALU.mult),
                   waits=[w], inc=("vv", 1))
        w = b.emit("vector", lambda v: v.tensor_mul(Rv, Pv, Pv), waits=[w], inc=("vv", 1))
        w = b.emit("vector", lambda v: v.tensor_sub(Qv, Qv, Rv), waits=[w], inc=("vv", 1))
        c4 = b.emit("vector", lambda v: v.tensor_scalar(Qv, Qv, 1e-5, None, ALU.add),
                    waits=[w], inc=("vv", 1))
        c5 = b.emit("scalar", lambda s: s.activation(Rv, Qv, AF.Sqrt),
                    waits=[c4], inc=("aa", 1))
        w = b.emit("vector", lambda v: v.reciprocal(Qv, Rv), waits=[c5], inc=("vv", 1))
        w = b.emit("vector", lambda v: v.tensor_mul(Pv, Pv, Qv), waits=[w], inc=("vv", 1))
        c8 = b.emit("vector", lambda v: v.tensor_scalar(Pv, Pv, -1.0, None, ALU.mult),
                    waits=[w], inc=("vv", 1))
        s1_ = b.dma("sd", scr_row[None, 0:2048], Qv, waits=[c8])
        s2_ = b.dma("sd", scr_row[None, 2048:4096], Pv)
        s3_ = b.dma("sd", statw[:, None, 0:4096],
                    scr_row[None, 0:4096].partition_broadcast(128), waits=[s1_, s2_])
        last = None
        for ic in range(NIC):
            t1 = b.emit("vector", lambda v, ic=ic: v.tensor_mul(
                src_f32[:, ic, :], src_f32[:, ic, :], statw[:, 0:2048]),
                waits=[s3_] if ic == 0 else [], inc=("vv", 1))
            t2 = b.emit("vector", lambda v, ic=ic: v.tensor_add(
                src_f32[:, ic, :], src_f32[:, ic, :], statw[:, 2048:4096]),
                waits=[t1], inc=("vv", 1))
            last = b.emit("vector", lambda v, ic=ic: v.tensor_copy(
                sb[:, ic, :], src_f32[:, ic, :]),
                waits=[t2], inc=("vv", 1))
        pk = b.dma("cd", y_park[park_idx].rearrange("c p t -> p c t"), src_f32[:, :, :],
                   waits=[last])
        return last, pk

    def ln_tile(ybuf, dst_b16, t0, w, wait_list, park_idx, tag):
        sb = dst_b16
        cpw = b.emit("vector", lambda v: v.tensor_copy(sb[:, :, t0:t0 + w], ybuf[:, :, :]),
                     waits=wait_list, inc=("vv", 1))
        sqw = b.emit("vector", lambda v: v.tensor_mul(
            Yv[:, 0:NIC, :], sb[:, :, t0:t0 + w], sb[:, :, t0:t0 + w]),
            waits=[cpw], inc=("vv", 1))
        st_done = None
        for stat_i, (srcb, ps) in enumerate(((sb[:, :, t0:t0 + w], psA), (Yv[:, 0:NIC, :], psB))):
            for ic in range(NIC):
                st_done = b.emit(
                    "tensor",
                    lambda t, srcb=srcb, ps=ps, ic=ic:
                    t.matmul(ps[0:1, 1536:1536 + w], onescol[:], srcb[:, ic, :],
                             start=(ic == 0), stop=(ic == NIC - 1)),
                    waits=[sqw, ones_done] + gA.all() if ic == 0 and stat_i == 0 else [],
                    inc=("tm", 1) if ic == NIC - 1 else None)
        Pv = ARF[0:1, 4096:4096 + w]
        Qv = ARF[0:1, 4608:4608 + w]
        Rv = ARF[0:1, 5120:5120 + w]
        wv_ = b.emit("vector", lambda v: v.tensor_scalar(Pv, psA[0:1, 1536:1536 + w],
                                                         1.0 / 1024, None, ALU.mult),
                     waits=[st_done], inc=("vv", 1))
        wv_ = b.emit("vector", lambda v: v.tensor_scalar(Qv, psB[0:1, 1536:1536 + w],
                                                         1.0 / 1024, None, ALU.mult),
                     waits=[wv_], inc=("vv", 1))
        wv_ = b.emit("vector", lambda v: v.tensor_mul(Rv, Pv, Pv), waits=[wv_], inc=("vv", 1))
        wv_ = b.emit("vector", lambda v: v.tensor_sub(Qv, Qv, Rv), waits=[wv_], inc=("vv", 1))
        wv_ = b.emit("vector", lambda v: v.tensor_scalar(Qv, Qv, 1e-5, None, ALU.add),
                     waits=[wv_], inc=("vv", 1))
        wv_ = b.emit("scalar", lambda s: s.activation(Rv, Qv, AF.Sqrt),
                     waits=[wv_], inc=("aa", 1))
        wv_ = b.emit("vector", lambda v: v.reciprocal(Qv, Rv), waits=[wv_], inc=("vv", 1))
        wv_ = b.emit("vector", lambda v: v.tensor_mul(Pv, Pv, Qv), waits=[wv_], inc=("vv", 1))
        wv_ = b.emit("vector", lambda v: v.tensor_scalar(Pv, Pv, -1.0, None, ALU.mult),
                     waits=[wv_], inc=("vv", 1))
        u1 = b.dma("sd", scr_row[None, 0:w], Qv, waits=[wv_])
        u2 = b.dma("sd", scr_row[None, w:2 * w], Pv, waits=[wv_])
        bcv = ARF[:, 5120:5120 + 2 * w]
        u3 = b.dma("sd", bcv[:, None, :], scr_row[None, 0:2 * w].partition_broadcast(128),
                   waits=[u1, u2])
        last = None
        for ic in range(NIC):
            a1 = b.emit("vector", lambda v, ic=ic: v.tensor_mul(
                ybuf[:, ic, :], ybuf[:, ic, :], bcv[:, 0:w]),
                waits=[u3] if ic == 0 else [], inc=("vv", 1))
            a2 = b.emit("vector", lambda v, ic=ic: v.tensor_add(
                ybuf[:, ic, :], ybuf[:, ic, :], bcv[:, w:2 * w]),
                waits=[a1], inc=("vv", 1))
            last = b.emit("vector", lambda v, ic=ic: v.tensor_copy(
                sb[:, ic, t0:t0 + w], ybuf[:, ic, :]),
                waits=[a2], inc=("vv", 1))
        pk = b.dma("cd", y_park[park_idx].rearrange("c p t -> p c t")[:, :, t0:t0 + w],
                   ybuf[:, :, :], waits=[last])
        for i in range(4):
            gA.set(i, last)
        return last, pk

    if DEBUG:
        dbg_ticks.append(b.dma("sd", dbg["d_attn"][:],
                               attn_park.rearrange("h p t -> h p t"),
                               waits=[vst_guard] + b.qall("pk")))
        dbg_ticks.append(b.dma("sd", dbg["d_y1"].rearrange("c p t -> p c t"),
                               res_f[:, :, :], waits=[y1_done]))
    ln1, y1_pk = layernorm(res_f, y1_done if not DEBUG else dbg_ticks[-1], 0)

    wraw = res_f[:, 4:6, :].rearrange("p c t -> p (c t)")
    mxs = wraw[:, 3072:3168]
    rss = wraw[:, 3168:3264]
    wqnat = AR0[:, 24576:27648]
    wqT_s = AR0[:, 27648:30720]
    qw_state = {"last": None, "tr": None}
    quant_jobs = []
    for wsl_, wi_, noc_, nic_ in ((wgate, 0, 24, NIC), (wup, 1, 24, NIC), (wdown, 2, NIC, 24)):
        for oc_ in range(noc_):
            quant_jobs.append((wsl_, wi_, oc_, nic_))

    def emit_quant(n):
        while n > 0 and quant_jobs:
            wsl, wi, oc, nic = quant_jobs.pop(0)
            qw_last = qw_state["last"]
            tr_prev2 = qw_state["tr"]
            I_ = nic * 128
            nb = I_ // 32
            first = (wi == 0 and oc == 0)
            lw = b.dma("gd", wraw[:, 0:I_], wsl[oc],
                       waits=([qw_last, y1_pk] if first else [qw_last]))
            q1 = b.emit("vector",
                        lambda v, I_=I_, nb=nb: v.tensor_reduce(
                            mxs[:, 0:nb].rearrange("p (b o) -> p b o", o=1),
                            wraw[:, 0:I_].rearrange("p (b j) -> p b j", j=32),
                            AX.X, ALU.max, apply_absolute_value=True),
                        waits=[lw], inc=("vv", 1))
            q2 = b.emit("vector",
                        lambda v, nb=nb: v.tensor_scalar(mxs[:, 0:nb], mxs[:, 0:nb],
                                                         1.0 / 7.0, 1e-12, ALU.mult, ALU.max),
                        waits=[q1], inc=("vv", 1))
            q3 = b.emit("vector",
                        lambda v, nb=nb: v.reciprocal(rss[:, 0:nb], mxs[:, 0:nb]),
                        waits=[q2], inc=("vv", 1))
            q4 = b.emit("vector",
                        lambda v, I_=I_, nb=nb: v.tensor_tensor(
                            wraw[:, 0:I_].rearrange("p (b j) -> p b j", j=32),
                            wraw[:, 0:I_].rearrange("p (b j) -> p b j", j=32),
                            rss[:, 0:nb][:, :, None].broadcast_to([128, nb, 32]),
                            ALU.mult),
                        waits=[q3], inc=("vv", 1))
            q5 = b.emit("vector",
                        lambda v, I_=I_: v.tensor_scalar(wraw[:, 0:I_], wraw[:, 0:I_],
                                                         MAGIC, MAGIC, ALU.add, ALU.subtract),
                        waits=[q4], inc=("vv", 1))
            q6 = b.emit("vector",
                        lambda v, I_=I_: v.tensor_scalar(wraw[:, 0:I_], wraw[:, 0:I_],
                                                         7.0, -8.0, ALU.min, ALU.max),
                        waits=[q5], inc=("vv", 1))
            q7 = b.emit("vector",
                        lambda v, I_=I_, nb=nb: v.tensor_tensor(
                            wqnat[:, 0:I_].rearrange("p (b j) -> p b j", j=32),
                            wraw[:, 0:I_].rearrange("p (b j) -> p b j", j=32),
                            mxs[:, 0:nb][:, :, None].broadcast_to([128, nb, 32]),
                            ALU.mult),
                        waits=[q6] + ([tr_prev2] if tr_prev2 else []), inc=("vv", 1))
            pk_n = b.dma("gd", wqn_park[wi, oc, :, 0:I_], wqnat[:, 0:I_],
                         waits=[q7])
            qw_state["tr"] = pk_n
            qw_state["last"] = pk_n
            n -= 1

    TT_F = 512
    NTT = TOK // TT_F
    Xv = AR0[:].rearrange("p (c t) -> p c t", t=TT_F)
    Yv = AR2[:].rearrange("p (c t) -> p c t", t=TT_F)
    wbuf = AR3[:].rearrange("p (j c o) -> p j c o", j=2, o=128)
    y2t = ARF[:, 0:4096].rearrange("p (c t) -> p c t", t=TT_F)

    def ffn_stage(tt, ws, n_oc, n_ic, rhs_fn, evac_fn, wait_extra, dma_q="gd"):
        nonlocal_ticks = []
        h = n_ic // 2
        for oc in range(n_oc):
            emit_quant(1)
            jb = oc % 2
            src = ws[oc].rearrange("c p o -> p c o")
            w_ex = (wait_extra + pc_ticks) if oc == 0 else []
            wta = b.dma("sd", wbuf[:, jb, 0:h, :], src[:, 0:h, :],
                        waits=[wgrd.wait(jb)] + w_ex)
            wtb = b.dma("cd", wbuf[:, jb, h:n_ic, :], src[:, h:n_ic, :],
                        waits=[wgrd.wait(jb)] + w_ex)
            qi = oc % 4
            tick = None
            for ic in range(n_ic):
                tick = b.emit(
                    "tensor",
                    lambda t, oc=oc, ic=ic, qi=qi, jb=jb, rhs=rhs_fn:
                    t.matmul(psA[:, 512 * qi:512 * qi + 512],
                             wbuf[:, jb, ic, :], rhs(ic),
                             start=(ic == 0), stop=(ic == n_ic - 1)),
                    waits=([wta, gA.wait(qi)] + (wait_extra if oc == 0 else []))
                    if ic == 0 else ([wtb] if ic == h else []),
                    inc=("tm", 1) if ic == n_ic - 1 else None)
            wgrd.set(jb, tick)
            ev = evac_fn(oc, qi, tick)
            gA.set(qi, ev)
            nonlocal_ticks.append(ev)
        return nonlocal_ticks

    wgrd = Guard(2)
    out_ticks = []
    ln2_prev = ln1
    pk_prev = None
    for tt in range(NTT):
        t0 = TT_F * tt
        y1t_ld = b.dma("cd", y2t[:, :, :],
                       y_park[0].rearrange("c p t -> p c t")[:, :, t0:t0 + TT_F],
                       waits=[y1_pk, ln2_prev, pk_prev])
        s1_ticks = []
        for oc in range(32):
            jb = oc % 2
            wt1 = b.dma("sd", wbuf[:, jb, 0:NIC, :], hc_w1[oc].rearrange("c p o -> p c o"),
                        waits=[wgrd.wait(jb)] +
                              (([ln1] + pc_ticks) if oc == 0 and tt == 0 else []))
            qi = (2 * oc) % 4
            tick = None
            for ic in range(NIC):
                tick = b.emit(
                    "tensor",
                    lambda t, oc=oc, ic=ic, qi=qi, jb=jb, t0=t0:
                    t.matmul(psA[:, 512 * qi:512 * qi + 512],
                             wbuf[:, jb, ic, :], stream_b[:, ic, t0:t0 + TT_F],
                             start=(ic == 0), stop=(ic == NIC - 1)),
                    waits=[wt1, gA.wait(qi)] if ic == 0 else [],
                    inc=("tm", 1) if ic == NIC - 1 else None)
            wgrd.set(jb, tick)
            e1 = b.emit("scalar",
                        lambda s, oc=oc, qi=qi:
                        s.activation(Xv[:, oc, :], psA[:, 512 * qi:512 * qi + 512],
                                     AF.Silu, bias=0.0,
                                     scale=smalls[:, c_s1 + oc:c_s1 + oc + 1]),
                        waits=[tick, wait_tiny], inc=("aa", 1))
            gA.set(qi, e1)
            jb2 = (oc + 1) % 2
            wt2 = b.dma("cd", wbuf[:, jb2, 0:NIC, :], hc_w2[oc].rearrange("c p o -> p c o"),
                        waits=[wgrd.wait(jb2)])
            qi2 = (2 * oc + 1) % 4
            tick = None
            for ic in range(NIC):
                tick = b.emit(
                    "tensor",
                    lambda t, oc=oc, ic=ic, qi2=qi2, jb2=jb2, t0=t0:
                    t.matmul(psA[:, 512 * qi2:512 * qi2 + 512],
                             wbuf[:, jb2, ic, :], stream_b[:, ic, t0:t0 + TT_F],
                             start=(ic == 0), stop=(ic == NIC - 1)),
                    waits=[wt2, gA.wait(qi2)] if ic == 0 else [],
                    inc=("tm", 1) if ic == NIC - 1 else None)
            wgrd.set(jb2, tick)
            e2 = b.emit("scalar",
                        lambda s, oc=oc, qi2=qi2:
                        s.activation(Yv[:, oc, :], psA[:, 512 * qi2:512 * qi2 + 512],
                                     AF.Identity, bias=0.0,
                                     scale=smalls[:, c_s2 + oc:c_s2 + oc + 1]),
                        waits=[tick, wait_tiny], inc=("aa", 1))
            gA.set(qi2, e2)
            mul = b.emit("vector",
                         lambda v, oc=oc: v.tensor_mul(Xv[:, oc, :], Xv[:, oc, :],
                                                       Yv[:, oc, :]),
                         waits=[e1, e2], inc=("vv", 1))
            s1_ticks.append(mul)
        def rhs_X(ic):
            return Xv[:, ic, :]

        def rhs_Y(ic):
            return Yv[:, ic, :]

        def ev_gelu(dstv, scol):
            def f(oc, qi, tick):
                return b.emit("scalar",
                              lambda s, oc=oc, qi=qi:
                              s.activation(dstv[:, oc, :],
                                           psA[:, 512 * qi:512 * qi + 512],
                                           AF.Gelu, bias=0.0,
                                           scale=smalls[:, scol + oc:scol + oc + 1]),
                              waits=[tick, wait_tiny], inc=("aa", 1))
            return f
        s2_ticks = ffn_stage(tt, hc_h1, 32, 32, rhs_X, ev_gelu(Yv, c_h1s), s1_ticks)
        s3_ticks = ffn_stage(tt, hc_h2, 32, 32, rhs_Y, ev_gelu(Xv, c_h2s), s2_ticks)
        def ev_out(oc, qi, tick):
            return b.emit("vector",
                          lambda v, oc=oc, qi=qi:
                          v.scalar_tensor_tensor(y2t[:, oc, :],
                                                 psA[:, 512 * qi:512 * qi + 512],
                                                 smalls[:, c_s3 + oc:c_s3 + oc + 1],
                                                 y2t[:, oc, :],
                                                 ALU.mult, ALU.add),
                          waits=[tick, y1t_ld, wait_tiny], inc=("vv", 1))
        s4_ticks = ffn_stage(tt, hc_w3, NIC, 32, rhs_X, ev_out, s3_ticks)
        lnt, pk2 = ln_tile(y2t, stream_b, t0, TT_F, s4_ticks, 1, tt)
        outv = out.rearrange("(c p) t -> p c t", p=128)
        tap = b.dma("sd", outv[:, :, t0:t0 + TT_F], y2t[:, :, :], waits=[lnt])
        ln2_prev = tap
        pk_prev = pk2
        out_ticks.append(tap)
        out_ticks.append(pk2)

    emit_quant(999)
    qw_seed = qw_state["last"]
    tsw_prev = None
    for wsl_, wi_, noc_, nic_ in ((wgate, 0, 24, NIC), (wup, 1, 24, NIC), (wdown, 2, NIC, 24)):
        for oc_ in range(noc_):
            trs = []
            for ic_ in range(nic_):
                trs.append(b.dma("vt", wqT_s[:, 128 * ic_:128 * ic_ + 128],
                                 wqn_park[wi_, oc_, :, 128 * ic_:128 * ic_ + 128],
                                 waits=([qw_seed] if oc_ == 0 and wi_ == 0 else []) +
                                       ([tsw_prev] if tsw_prev else []),
                                 transpose=True))
            tsw_prev = b.dma("gd", wq_park[wi_, oc_, 0:nic_].rearrange("c p o -> p c o"),
                             wqT_s[:, 0:(128 * nic_)].rearrange("p (c o) -> p c o", o=128),
                             waits=trs)
    qw_state["last"] = tsw_prev

    l1w = AR3[:, 0:1536].rearrange("p (w c o) -> p w c o", w=3, o=64)
    l1wo = AR3[:, 1536:2560]
    kftm = AR3[:, 2560:3584].rearrange("p (k c) -> p k c", k=16)
    vtm2 = AR3[:, 3584:4864].rearrange("p (k c) -> p k c", k=16)
    srunb = AR3[:, 4864:4944]
    qf = AR0[:, 0:2048]
    kf = AR0[:, 2048:4096]
    vf = AR0[:, 4096:6144]
    at2 = AR0[:, 6144:6272]
    outn = AR0[:, 8192:10240]
    numbuf = ARF[:, 0:2048]
    stf = ARF[:, 2048:2128]
    prefx = ARF[:, 2128:2208]
    scr1 = ARF[:, 2304:2816]
    scr2 = ARF[:, 2816:3328]

    wl1_l = [b.dma("gd", l1w[:, 0, :, :], l1wqT.rearrange("c p o -> p c o"),
                   waits=[ln2_prev])]
    wl1_l.append(b.dma("gd", l1w[:, 1, :, :], l1wkT.rearrange("c p o -> p c o")))
    wl1_l.append(b.dma("gd", l1w[:, 2, :, :], l1wvT.rearrange("c p o -> p c o")))
    l1wov = l1wo[0:64, :].rearrange("p (c o) -> p c o", o=128)
    wlo = b.dma("gd", l1wov, l1woT.rearrange("p (c o) -> p c o", o=128), waits=[ln2_prev])

    proj_out = {}
    prev_ev = None
    for wi, dst, elu in ((0, qf, True), (1, kf, True), (2, vf, False)):
        for tc in range(4):
            qi = tc % 4
            tick = None
            for ic in range(NIC):
                tick = b.emit(
                    "tensor",
                    lambda t, wi=wi, ic=ic, tc=tc, qi=qi:
                    t.matmul(psA[0:64, 512 * qi:512 * qi + 512],
                             l1w[:, wi, ic, :], stream_b[:, ic, 512 * tc:512 * tc + 512],
                             start=(ic == 0), stop=(ic == NIC - 1)),
                    waits=(wl1_l + [gA.wait(qi), pk_prev]) if ic == 0 else [],
                    inc=("tm", 1) if ic == NIC - 1 else None)
            if elu:
                m1 = b.emit("vector",
                            lambda v, qi=qi: v.tensor_scalar(scr1[0:64, :],
                                                             psA[0:64, 512 * qi:512 * qi + 512],
                                                             0.0, None, ALU.min),
                            waits=[tick] + ([prev_ev] if prev_ev else []), inc=("vv", 1))
                m2 = b.emit("vector",
                            lambda v, qi=qi: v.tensor_scalar(scr2[0:64, :],
                                                             psA[0:64, 512 * qi:512 * qi + 512],
                                                             0.0, None, ALU.max),
                            waits=[m1], inc=("vv", 1))
                m3 = b.emit("scalar", lambda s: s.activation(scr1[0:64, :], scr1[0:64, :],
                                                             AF.Exp),
                            waits=[m2], inc=("aa", 1))
                ev = b.emit("vector",
                            lambda v, dst=dst, tc=tc: v.tensor_add(
                                dst[0:64, 512 * tc:512 * tc + 512],
                                scr1[0:64, :], scr2[0:64, :]),
                            waits=[m3], inc=("vv", 1))
            else:
                ev = b.emit("vector",
                            lambda v, dst=dst, tc=tc, qi=qi: v.tensor_copy(
                                dst[0:64, 512 * tc:512 * tc + 512],
                                psA[0:64, 512 * qi:512 * qi + 512]),
                            waits=[tick], inc=("vv", 1))
            gA.set(qi, ev)
            prev_ev = ev
            proj_out[(wi, tc)] = ev
    proj_done = [proj_out[(wi, tc)] for wi in range(3) for tc in range(4)]

    vo = b.emit("vector", lambda v: v.memset(vtm2[:], 0.0), waits=[ln2_prev], inc=("vv", 1))
    vo2 = b.emit("vector", lambda v: v.memset(vtm2[:, :, 64:65], 1.0), waits=[vo], inc=("vv", 1))
    ttk = None
    for ch in range(16):
        ttk = b.dma("vt", kftm[:, ch, :], kf[0:64, 128 * ch:128 * ch + 128],
                    waits=proj_done if ch == 0 else [], transpose=True)
        ttk = b.dma("vt", vtm2[:, ch, 0:64], vf[0:64, 128 * ch:128 * ch + 128],
                    waits=[vo2] if ch == 0 else [], transpose=True)
    tm_done = b.qall("vt")

    sttk = None
    for ch in range(16):
        sttk = b.emit("tensor",
                      lambda t, ch=ch:
                      t.matmul(psB[0:64, 1536:1601], kftm[:, ch, :], vtm2[:, ch, 0:65],
                               start=(ch == 0), stop=(ch == 15), skip_group_check=True),
                      waits=tm_done + gB.all() if ch == 0 else [],
                      inc=("tm", 1) if ch == 15 else None)
    exz = b.emit("vector", lambda v: v.memset(stf[0:64, 0:80], 0.0),
                 waits=[ln2_prev, pk_prev], inc=("vv", 1))
    exm = b.emit("vector", lambda v: v.tensor_scalar(stf[0:64, 0:65], psB[0:64, 1536:1601],
                                                     pmsk[0:64, 0:1], None, ALU.mult),
                 waits=[sttk, pm_ld, exz], inc=("vv", 1))
    exd = b.dma("pk", st_in.rearrange("(p c) -> p c", p=64),
                stf[0:64, 0:80], waits=[exm])
    cstate = b.emit("gpsimd",
                    lambda g: g.collective_compute(
                        "AllReduce", ALU.add, ins=[st_in[:]], outs=[st_out[:]],
                        replica_groups=[[0, 1], [2, 3], [4, 5], [6, 7]]),
                    waits=[exd], inc=("gg", 1))
    exr = b.dma("pk", prefx[0:64, 0:65], st_out.rearrange("(p c) -> p c", p=64)[:, 0:65],
                waits=[cstate])
    pfx = b.emit("vector", lambda v: v.tensor_scalar(prefx[0:64, 0:65], prefx[0:64, 0:65],
                                                     pmsk[0:64, 1:2], None, ALU.mult),
                 waits=[exr], inc=("vv", 1))
    srb0 = b.emit("vector", lambda v: v.tensor_copy(srunb[0:64, 0:65], prefx[0:64, 0:65]),
                  waits=[pfx], inc=("vv", 1))

    srb_tick = srb0
    num_ev = None
    for ch in range(16):
        atk = b.emit("tensor",
                     lambda t, ch=ch:
                     t.matmul(psB[:, 0:128], kf[0:64, 128 * ch:128 * ch + 128],
                              qf[0:64, 128 * ch:128 * ch + 128],
                              start=True, stop=True, skip_group_check=True),
                     waits=[num_ev] if num_ev else [],
                     inc=("tm", 1))
        msk = b.emit("vector",
                     lambda v: v.tensor_mul(at2[:, :], psB[:, 0:128], tri32[:]),
                     waits=[atk, tri_done], inc=("vv", 1))
        n1 = b.emit("tensor",
                    lambda t, ch=ch:
                    t.matmul(psB[0:65, 512:640], srunb[0:64, 0:65],
                             qf[0:64, 128 * ch:128 * ch + 128],
                             start=True, stop=False, skip_group_check=True),
                    waits=[srb_tick] + ([num_ev] if num_ev else []))
        n2 = b.emit("tensor",
                    lambda t, ch=ch:
                    t.matmul(psB[0:65, 512:640], vtm2[:, ch, 0:65], at2[:, :],
                             start=False, stop=True, skip_group_check=True),
                    waits=[msk], inc=("tm", 1))
        num_ev = b.emit("vector",
                        lambda v, ch=ch: v.tensor_copy(numbuf[0:65, 128 * ch:128 * ch + 128],
                                                       psB[0:65, 512:640]),
                        waits=[n2], inc=("vv", 1))
        if ch < 15:
            sup = b.emit("tensor",
                         lambda t, ch=ch:
                         t.matmul(psB[0:64, 1536:1601], kftm[:, ch, :], vtm2[:, ch, 0:65],
                                  start=(ch == 0), stop=True, skip_group_check=True),
                         waits=[exm] if ch == 0 else [], inc=("tm", 1))
            sev = b.emit("vector",
                         lambda v: v.scalar_tensor_tensor(stf[0:64, 0:65],
                                                          psB[0:64, 1536:1601], 1.0,
                                                          prefx[0:64, 0:65],
                                                          ALU.mult, ALU.add),
                         waits=[sup, pfx], inc=("vv", 1))
            srb_tick = b.emit("vector",
                              lambda v: v.tensor_copy(srunb[0:64, 0:65], stf[0:64, 0:65]),
                              waits=[sev], inc=("vv", 1))

    if DEBUG:
        dbg_ticks.append(b.dma("sd", dbg["d_qf"][:], AR0[:, 0:2048], waits=proj_done))
        dbg_ticks.append(b.dma("sd", dbg["d_kf"][:], AR0[:, 2048:4096], waits=proj_done))
        dbg_ticks.append(b.dma("sd", dbg["d_vf"][:], AR0[:, 4096:6144], waits=proj_done))
        dbg_ticks.append(b.dma("sd", dbg["d_num"][:], numbuf[:, :], waits=[num_ev]))
        dbg_ticks.append(b.dma("sd", dbg["d_st"][:], ARF[:, 2128:2208], waits=[pfx]))
        dbg_ticks.append(b.dma("sd", dbg["d_y2"].rearrange("c p t -> p c t"),
                               y_park[1].rearrange("c p t -> p c t"), waits=[pk_prev]))
    dn1 = b.emit("vector", lambda v: v.tensor_scalar(numbuf[64:65, :], numbuf[64:65, :],
                                                     1e-6, None, ALU.add),
                 waits=[num_ev], inc=("vv", 1))
    dn2 = b.emit("vector", lambda v: v.reciprocal(numbuf[64:65, :], numbuf[64:65, :]),
                 waits=[dn1], inc=("vv", 1))
    dn3 = b.dma("sd", scr_l[None, :], numbuf[64:65, :], waits=[dn2])
    dn4 = b.dma("sd", lrow[0:64, None, :], scr_l[None, :].partition_broadcast(64),
                waits=[dn3])
    dnm = b.emit("vector", lambda v: v.tensor_mul(outn[0:64, :], numbuf[0:64, :],
                                                  lrow[0:64, :]),
                 waits=[dn4], inc=("vv", 1))

    y3t = ARF[:, 0:4096].rearrange("p (c t) -> p c t", t=TT_F)
    ln3_prev = None
    pk3_prev = None
    for tt in range(NTT):
        t0 = TT_F * tt
        y2ld = b.dma("cd", y3t[:, :, :],
                     y_park[1].rearrange("c p t -> p c t")[:, :, t0:t0 + TT_F],
                     waits=[pk_prev, ln2_prev, dnm] +
                           ([ln3_prev, pk3_prev] if ln3_prev else []) + dbg_ticks)
        s4t = []
        for oc in range(NIC):
            qi = oc % 4
            tick = None
            tick = b.emit("tensor",
                          lambda t, oc=oc, qi=qi, t0=t0:
                          t.matmul(psA[:, 512 * qi:512 * qi + 512],
                                   l1wov[:, oc, :],
                                   outn[0:64, t0:t0 + TT_F],
                                   start=True, stop=True),
                          waits=[wlo, dnm, gA.wait(qi)],
                          inc=("tm", 1))
            ev = b.emit("vector",
                        lambda v, oc=oc, qi=qi:
                        v.scalar_tensor_tensor(y3t[:, oc, :],
                                               psA[:, 512 * qi:512 * qi + 512], 1.0,
                                               y3t[:, oc, :], ALU.mult, ALU.add),
                        waits=[tick, y2ld], inc=("vv", 1))
            gA.set(qi, ev)
            s4t.append(ev)
        if DEBUG and tt == 0:
            s4t = s4t + [b.dma("sd", dbg["d_y3p"].rearrange("c p t -> p c t"),
                               y3t[:, :, :], waits=s4t)]
            dbg_ticks.append(s4t[-1])
        ln3_prev, pk3_prev = ln_tile(y3t, stream_b, t0, TT_F, s4t, 2, 10 + tt)

    if DEBUG:
        dbg_ticks.append(b.dma("sd", dbg["d_outn"][:], AR0[:, 8192:10240], waits=[dnm]))
        dbg_ticks.append(b.dma("sd", dbg["d_y3"].rearrange("c p t -> p c t"),
                               y_park[2].rearrange("c p t -> p c t"), waits=[pk3_prev]))
    gm_prev = [None]
    def global_max(local_col, idx, n, wait_list):
        g1 = b.dma("sd", scr_pm[None, :], local_col[:, 0:1],
                   waits=wait_list + ([gm_prev[0]] if gm_prev[0] else []))
        g2 = b.dma("sd", ARF[0:1, 6016:6144], scr_pm[None, :], waits=[g1])
        g3 = b.emit("vector", lambda v: v.tensor_reduce(ARF[0:1, 6012:6013],
                                                        ARF[0:1, 6016:6144],
                                                        AX.X, ALU.max),
                    waits=[g2], inc=("vv", 1))
        g3b = b.emit("vector", lambda v: v.tensor_copy(
            ARF[0:1, 6008:6012], ARF[0:1, 6012:6013].broadcast_to([1, 4])),
            waits=[g3], inc=("vv", 1))
        g4 = b.dma("pk", cc_in[idx][None, 0:4], ARF[0:1, 6008:6012], waits=[g3b])
        g5 = b.emit("gpsimd",
                    lambda g: g.collective_compute(
                        "AllReduce", ALU.max, ins=[cc_in[idx][:]], outs=[cc_out[idx][:]],
                        replica_groups=[list(range(8))]),
                    waits=[g4], inc=("gg", 1))
        g6 = b.dma("pk", ARF[0:1, 6013:6014], cc_out[idx][None, 0:1], waits=[g5])
        g7 = b.emit("vector", lambda v: v.tensor_scalar(ARF[0:1, 6014:6015],
                                                        ARF[0:1, 6013:6014],
                                                        1.0 / 32767.0, 1e-12,
                                                        ALU.mult, ALU.max),
                    waits=[g6], inc=("vv", 1))
        g8 = b.emit("vector", lambda v: v.reciprocal(ARF[0:1, 6015:6016],
                                                     ARF[0:1, 6014:6015]),
                    waits=[g7], inc=("vv", 1))
        g9 = b.dma("sd", scr_pm[None, 0:2], ARF[0:1, 6014:6016], waits=[g8])
        col = 208 + 2 * idx
        g10 = b.dma("sd", smalls[:, col:col + 2][:, None, :],
                    scr_pm[None, 0:2].partition_broadcast(128), waits=[g9])
        gm_prev[0] = g10
        return g10, smalls[:, col + 1:col + 2], smalls[:, col:col + 1]

    def qdq3(dst_ap, src_ap, inv_ap, sc_ap, wait_list, scr=None, out_guard=None):
        sq_ = scr if scr is not None else scr_q
        a1 = b.emit("vector", lambda v: v.tensor_scalar(sq_[:], src_ap, inv_ap, MAGIC,
                                                        ALU.mult, ALU.add),
                    waits=wait_list, inc=("vv", 1))
        a2 = b.emit("vector", lambda v: v.tensor_scalar(sq_[:], sq_[:], MAGIC, 32767.0,
                                                        ALU.subtract, ALU.min),
                    waits=[a1], inc=("vv", 1))
        a3 = b.emit("vector", lambda v: v.tensor_scalar(dst_ap, sq_[:], -32768.0, sc_ap,
                                                        ALU.max, ALU.mult),
                    waits=[a2] + ([out_guard] if out_guard else []), inc=("vv", 1))
        return a3

    emit_quant(999)
    qw_last = qw_state["last"]
    lm = None
    xmax_ticks = []
    for tt in range(NTT):
        t0 = TT_F * tt
        yld = b.dma("cd", y3t[:, :, :],
                    y_park[2].rearrange("c p t -> p c t")[:, :, t0:t0 + TT_F],
                    waits=[pk3_prev, qw_last] if tt == 0 else [lm])
        r1 = b.emit("vector", lambda v: v.tensor_reduce(
            ARF[:, 4096 + 0:4096 + 1].rearrange("p a -> p a 1")[:, :, 0] if False
            else ARF[:, 4100:4101],
            y3t[:, :, :].rearrange("p c t -> p (c t)"), AX.X, ALU.max,
            apply_absolute_value=True),
            waits=[yld], inc=("vv", 1))
        lm = b.emit("vector", lambda v, tt=tt: v.tensor_tensor(
            ARF[:, 4101:4102], ARF[:, 4101:4102], ARF[:, 4100:4101], ALU.max)
            if tt > 0 else v.tensor_copy(ARF[:, 4101:4102], ARF[:, 4100:4101]),
            waits=[r1], inc=("vv", 1))
        xmax_ticks.append(lm)
    gx, inv_x, sc_x = global_max(ARF[:, 4101:4102], 0, 1, [lm])
    scr_q = ARF[:, 4608:5120]
    xq_ticks = []
    xq_prev = None
    for tt in range(NTT):
        t0 = TT_F * tt
        yld = b.dma("cd", y3t[:, :, :],
                    y_park[2].rearrange("c p t -> p c t")[:, :, t0:t0 + TT_F],
                    waits=[gx] if tt == 0 else [xq_prev])
        for oc in range(NIC):
            xq_prev = qdq3(stream_b[:, oc, t0:t0 + TT_F], y3t[:, oc, :], inv_x, sc_x,
                           [yld, gx] + ([xq_prev] if xq_prev else []))
        xq_ticks.append(xq_prev)

    wbuf2 = AR3[:].rearrange("p (j c o) -> p j c o", j=2, o=128)
    gstages = [AR0[:, 512 * j:512 * j + 512] for j in range(4)]
    gmx = [ARF[:, 4096 + j:4097 + j] for j in range(4)]
    gumax = ARF[:, 4102:4104]
    gmz = b.emit("vector", lambda v: v.memset(ARF[:, 4096:4104], 0.0),
                 waits=[dnm], inc=("vv", 1))
    gu_ticks = []
    wgrd2 = Guard(2)
    sgrds = Guard(4)
    it = 0
    for wi, wpk in ((0, 0), (1, 1)):
        for tt in range(NTT):
            t0 = TT_F * tt
            for oc in range(24):
                jb = it % 2
                sj = it % 4
                wt = b.dma("sd" if oc % 2 == 0 else "cd", wbuf2[:, jb, 0:NIC, :],
                           wq_park[wi, oc, 0:NIC].rearrange("c p o -> p c o"),
                           waits=[wgrd2.wait(jb)] + ([qw_last] + xq_ticks if oc == 0 and tt == 0 else []))
                qi = it % 4
                tick = None
                for ic in range(NIC):
                    tick = b.emit(
                        "tensor",
                        lambda t, oc=oc, ic=ic, qi=qi, jb=jb, t0=t0:
                        t.matmul(psA[:, 512 * qi:512 * qi + 512],
                                 wbuf2[:, jb, ic, :], stream_b[:, ic, t0:t0 + TT_F],
                                 start=(ic == 0), stop=(ic == NIC - 1)),
                        waits=([wt, gA.wait(qi)] + (xq_ticks if oc == 0 and tt == 0 else []))
                        if ic == 0 else [],
                        inc=("tm", 1) if ic == NIC - 1 else None)
                wgrd2.set(jb, tick)
                ev = b.emit("scalar",
                            lambda s, qi=qi, sj=sj:
                            s.activation(gstages[sj][:], psA[:, 512 * qi:512 * qi + 512],
                                         AF.Copy),
                            waits=[tick] + ([sgrds.wait(sj)] if sgrds.wait(sj) else []),
                            inc=("aa", 1))
                gA.set(qi, ev)
                mx1 = b.emit("vector",
                             lambda v, sj=sj: v.tensor_reduce(gmx[sj], gstages[sj][:],
                                                              AX.X, ALU.max,
                                                              apply_absolute_value=True),
                             waits=[ev, gmz], inc=("vv", 1))
                mx2 = b.emit("vector",
                             lambda v, wi=wi, sj=sj: v.tensor_tensor(gumax[:, wi:wi + 1],
                                                                     gumax[:, wi:wi + 1],
                                                                     gmx[sj], ALU.max),
                             waits=[mx1], inc=("vv", 1))
                pkq = "pk" if it % 2 == 0 else "gd"
                sg = b.dma(pkq, gu_park[wpk, oc, :, t0:t0 + TT_F], gstages[sj][:],
                           waits=[ev, mx1])
                sgrds.set(sj, sg)
                gu_ticks.append(sg)
                it += 1
    gg1, inv_up, sc_up = global_max(gumax[:, 1:2], 1, 1, [mx2] + gu_ticks[-1:])
    gg2, inv_gt, sc_gt = global_max(gumax[:, 0:1], 2, 1, [mx2])

    g2max = ARF[:, 4104:4105]
    gz2 = b.emit("vector", lambda v: v.memset(ARF[:, 4104:4112], 0.0), inc=("vv", 1))
    SETS = []
    SETS.append(dict(tq1=AR0[:, 1024:1536], tq2=ARF[:, 5120:5632],
                     tq3=ARF[:, 5632:6144], scr=ARF[:, 4608:5120],
                     mx=ARF[:, 4105:4106]))
    SETS.append(dict(tq1=AR0[:, 1536:2048], tq2=res_f[:, 4, 0:512],
                     tq3=res_f[:, 4, 512:1024], scr=res_f[:, 4, 1024:1536],
                     mx=ARF[:, 4106:4107]))
    g2_ticks = []
    sguard = Guard(2)
    it2 = 0
    mx2b = None
    for tt in range(NTT):
        t0 = TT_F * tt
        for oc in range(24):
            S_ = SETS[it2 % 2]
            tq1, tq2, tq3 = S_["tq1"], S_["tq2"], S_["tq3"]
            rld = b.dma("sd" if it2 % 2 == 0 else "cd", tq1[:],
                        gu_park[0, oc, :, t0:t0 + TT_F],
                        waits=([gg2] + gu_ticks if it2 < 2 else []) +
                              ([sguard.wait(it2)] if sguard.wait(it2) else []))
            gq = qdq3(tq2[:], tq1[:], inv_gt, sc_gt, [rld, gg2, gz2], scr=S_["scr"])
            sg = b.emit("scalar", lambda s, tq2=tq2, tq3=tq3:
                        s.activation(tq3[:], tq2[:], AF.Sigmoid),
                        waits=[gq], inc=("aa", 1))
            f1 = b.emit("vector", lambda v, tq3=tq3: v.tensor_scalar(tq3[:], tq3[:], 65536.0, MAGIC,
                                                                     ALU.mult, ALU.add),
                        waits=[sg], inc=("vv", 1))
            f2 = b.emit("vector", lambda v, tq3=tq3: v.tensor_scalar(tq3[:], tq3[:], MAGIC, 65535.0,
                                                                     ALU.subtract, ALU.min),
                        waits=[f1], inc=("vv", 1))
            f3 = b.emit("vector", lambda v, tq3=tq3: v.tensor_scalar(tq3[:], tq3[:], 0.0,
                                                                     1.0 / 65536.0,
                                                                     ALU.max, ALU.mult),
                        waits=[f2], inc=("vv", 1))
            g2r = b.emit("vector", lambda v, tq2=tq2, tq3=tq3: v.tensor_mul(tq2[:], tq2[:], tq3[:]),
                         waits=[f3], inc=("vv", 1))
            mx1 = b.emit("vector",
                         lambda v, tq2=tq2, S_=S_: v.tensor_reduce(S_["mx"], tq2[:],
                                                                   AX.X, ALU.max,
                                                                   apply_absolute_value=True),
                         waits=[g2r], inc=("vv", 1))
            mx2b = b.emit("vector",
                          lambda v, S_=S_: v.tensor_tensor(g2max[:], g2max[:],
                                                           S_["mx"], ALU.max),
                          waits=[mx1], inc=("vv", 1))
            cstg = b.emit("vector", lambda v, tq1=tq1, tq2=tq2: v.tensor_copy(tq1[:], tq2[:]),
                          waits=[g2r], inc=("vv", 1))
            pk_ = b.dma("pk" if it2 % 2 == 0 else "gd",
                        gu_park[0, oc, :, t0:t0 + TT_F], tq1[:], waits=[cstg])
            sguard.set(it2, pk_)
            g2_ticks.append(pk_)
            it2 += 1
    gg3, inv_g2, sc_g2 = global_max(g2max[:], 3, 1, [mx2b] + g2_ticks[-1:])

    omax = ARF[:, 4107:4108]
    oz = b.emit("vector", lambda v: v.memset(omax[:], 0.0), inc=("vv", 1))
    up_b = [AR0[:, 2048:2560], AR0[:, 2560:3072]]
    o_ticks = []
    oguard = Guard(2)
    it3 = 0
    mo2 = None
    for tt in range(NTT):
        t0 = TT_F * tt
        for oc in range(24):
            S_ = SETS[it3 % 2]
            tq1, tq2, tq3 = S_["tq1"], S_["tq2"], S_["tq3"]
            ub = up_b[it3 % 2]
            rl1 = b.dma("sd" if it3 % 2 == 0 else "cd", tq1[:],
                        gu_park[0, oc, :, t0:t0 + TT_F],
                        waits=([gg3] + g2_ticks if it3 < 2 else []) +
                              ([oguard.wait(it3)] if oguard.wait(it3) else []))
            rl2 = b.dma("sd" if it3 % 2 == 0 else "cd", ub[:],
                        gu_park[1, oc, :, t0:t0 + TT_F],
                        waits=([gg3] if it3 < 2 else []) +
                              ([oguard.wait(it3)] if oguard.wait(it3) else []))
            g2q = qdq3(tq2[:], tq1[:], inv_g2, sc_g2, [rl1, gg3, oz], scr=S_["scr"])
            upq = qdq3(tq3[:], ub[:], inv_up, sc_up, [rl2, gg1, g2q], scr=S_["scr"])
            orw = b.emit("vector", lambda v, tq2=tq2, tq3=tq3: v.tensor_mul(tq2[:], tq2[:], tq3[:]),
                         waits=[g2q, upq], inc=("vv", 1))
            mo1 = b.emit("vector",
                         lambda v, tq2=tq2, S_=S_: v.tensor_reduce(S_["mx"], tq2[:],
                                                                   AX.X, ALU.max,
                                                                   apply_absolute_value=True),
                         waits=[orw], inc=("vv", 1))
            mo2 = b.emit("vector",
                         lambda v, S_=S_: v.tensor_tensor(omax[:], omax[:],
                                                          S_["mx"], ALU.max),
                         waits=[mo1], inc=("vv", 1))
            cst2 = b.emit("vector", lambda v, tq1=tq1, tq2=tq2: v.tensor_copy(tq1[:], tq2[:]),
                          waits=[orw], inc=("vv", 1))
            pk_ = b.dma("pk" if it3 % 2 == 0 else "gd",
                        gu_park[0, oc, :, t0:t0 + TT_F], tq1[:], waits=[cst2])
            oguard.set(it3, pk_)
            o_ticks.append(pk_)
            it3 += 1
    gg4, inv_o, sc_o = global_max(omax[:], 0, 1, [mo2] + o_ticks[-1:])

    if DEBUG:
        dbg_ticks.append(b.dma("sd", dbg["d_sc"][:], smalls[:, 208:224], waits=[gg4]))
        dbg_ticks.append(b.dma("sd", dbg["d_xq"].rearrange("c p t -> p c t"),
                               stream_b[:, :, :], waits=xq_ticks))
        dbg_ticks.append(b.dma("sd", dbg["d_gu"][0], gu_park[0, 0, :, :], waits=[rp]))
        dbg_ticks.append(b.dma("sd", dbg["d_gu"][1], gu_park[1, 0, :, :], waits=[rp]))
        dbg_ticks.append(b.dma("sd", dbg["d_wq"].rearrange("c p o -> p c o"),
                               wq_park[2, 6, 2:2 + NIC].rearrange("c p o -> p c o"),
                               waits=[qw_last]))
    oq = AR0[:, 20480:32768].rearrange("p (c t) -> p c t", t=TT_F)
    y4t = ARF[:, 0:4096].rearrange("p (c t) -> p c t", t=TT_F)
    outv = out.rearrange("(c p) t -> p c t", p=128)
    fin_ticks = []
    ln4_prev = None
    pk4_prev = None
    rp2 = None
    for tt in range(NTT):
        t0 = TT_F * tt
        oq_ticks = []
        for oc in range(24):
            S_ = SETS[oc % 2]
            tq1 = S_["tq1"]
            rl = b.dma("sd" if oc % 2 == 0 else "cd", tq1[:],
                       gu_park[0, oc, :, t0:t0 + TT_F],
                       waits=([gg4] + o_ticks if oc < 2 and tt == 0 else [])
                       + ([oq_ticks[-2]] if len(oq_ticks) >= 2 else
                          ([rp2] if rp2 else [])))
            qt = qdq3(oq[:, oc, :], tq1[:], inv_o, sc_o,
                      [rl, gg4] + ([ln4_prev] if ln4_prev else []), scr=S_["scr"])
            oq_ticks.append(qt)
        rp2 = oq_ticks[-1]
        y3ld = b.dma("cd", y4t[:, :, :],
                     y_park[2].rearrange("c p t -> p c t")[:, :, t0:t0 + TT_F],
                     waits=([ln4_prev, pk4_prev] if ln4_prev else g2_ticks[-2:]))
        s4t = []
        for oc in range(NIC):
            jb = (oc + tt) % 2
            wt = b.dma("sd", wbuf2[:, jb, 0:24, :],
                       wq_park[2, oc, 0:24].rearrange("c p o -> p c o"),
                       waits=[wgrd2.wait(jb)])
            qi = oc % 4
            tick = None
            for ic in range(24):
                tick = b.emit(
                    "tensor",
                    lambda t, oc=oc, ic=ic, qi=qi, jb=jb:
                    t.matmul(psA[:, 512 * qi:512 * qi + 512],
                             wbuf2[:, jb, ic, :], oq[:, ic, :],
                             start=(ic == 0), stop=(ic == 23)),
                    waits=([wt, gA.wait(qi)] + oq_ticks) if ic == 0 else [],
                    inc=("tm", 1) if ic == 23 else None)
            wgrd2.set(jb, tick)
            ev = b.emit("vector",
                        lambda v, oc=oc, qi=qi:
                        v.scalar_tensor_tensor(y4t[:, oc, :],
                                               psA[:, 512 * qi:512 * qi + 512], 1.0,
                                               y4t[:, oc, :], ALU.mult, ALU.add),
                        waits=[tick, y3ld], inc=("vv", 1))
            gA.set(qi, ev)
            s4t.append(ev)
        if DEBUG and tt == 0:
            dbg_ticks.append(b.dma("sd", dbg["d_oq"].rearrange("c p t -> p c t"),
                                   oq[:, :, :], waits=oq_ticks))
            s4t = s4t + [b.dma("sd", dbg["d_y4p"].rearrange("c p t -> p c t"),
                               y4t[:, :, :], waits=s4t)]
            dbg_ticks.append(s4t[-1])
        ln4_prev, pk4_prev = ln_tile(y4t, stream_b, t0, TT_F, s4t, 0, 20 + tt)
        fin = b.dma("sd", outv[:, :, t0:t0 + TT_F], y4t[:, :, :], waits=[ln4_prev])
        fin_ticks.append(fin)
        ln4_prev = fin

    b.emit("sync", None, waits=fin_ticks + dbg_ticks + b.qall("cd") + b.qall("pk"))

    b.finalize()
    return nc


def _layout_inputs(inputs):
    f = lambda a: np.ascontiguousarray(np.asarray(a), np.float32)
    x = np.asarray(inputs["L0_attn_x"])
    TT = lambda w: np.asarray(w).T
    wq, wk, wv, wo = (np.asarray(inputs["L0_attn_W" + k]) for k in "qkvo")
    shared = {
        "wqT": f(wq.T.reshape(NIC, 128, D)),
        "wkT": f(wk.T.reshape(NIC, 128, D)),
        "wvT": f(wv.T.reshape(NIC, 128, D)),
        "woT": f(wo.T.reshape(NIC, 128, D)),
        "bq": f(np.asarray(inputs["L0_attn_bq"]).reshape(NIC, 128).T),
        "bk": f(np.asarray(inputs["L0_attn_bk"]).reshape(NIC, 128).T),
        "bv": f(np.asarray(inputs["L0_attn_bv"]).reshape(NIC, 128).T),
        "bo": f(np.asarray(inputs["L0_attn_bo"]).reshape(NIC, 128).T),
        "wqsT": f(TT(inputs["L0_attn_Wqs"])),
        "wksT": f(TT(inputs["L0_attn_Wks"])),
        "wvsT": f(TT(inputs["L0_attn_Wvs"])),
        "bqs": f(inputs["L0_attn_bqs"]),
        "bks": f(inputs["L0_attn_bks"]),
        "bvs": f(inputs["L0_attn_bvs"]),
        "temp": f(np.asarray(inputs["L0_attn_temp"]).reshape(1)),
        "w1T": f(TT(inputs["L0_ffn_w1_w"]).reshape(NIC, 128, 32, 128).transpose(2, 0, 1, 3)),
        "w2T": f(TT(inputs["L0_ffn_w2_w"]).reshape(NIC, 128, 32, 128).transpose(2, 0, 1, 3)),
        "h1T": f(TT(inputs["L0_ffn_h1_w"]).reshape(32, 128, 32, 128).transpose(2, 0, 1, 3)),
        "h2T": f(TT(inputs["L0_ffn_h2_w"]).reshape(32, 128, 32, 128).transpose(2, 0, 1, 3)),
        "w3T": f(TT(inputs["L0_ffn_w3_w"]).reshape(32, 128, NIC, 128).transpose(2, 0, 1, 3)),
        "s1": f(np.asarray(inputs["L0_ffn_w1_s"]).reshape(32, 128).T),
        "s2": f(np.asarray(inputs["L0_ffn_w2_s"]).reshape(32, 128).T),
        "h1s": f(np.asarray(inputs["L0_ffn_h1_s"]).reshape(32, 128).T),
        "h2s": f(np.asarray(inputs["L0_ffn_h2_s"]).reshape(32, 128).T),
        "s3": f(np.asarray(inputs["L0_ffn_w3_s"]).reshape(NIC, 128).T),
        "l1wqT": f(TT(inputs["L1_attn_Wq"]).reshape(NIC, 128, 64)),
        "l1wkT": f(TT(inputs["L1_attn_Wk"]).reshape(NIC, 128, 64)),
        "l1wvT": f(TT(inputs["L1_attn_Wv"]).reshape(NIC, 128, 64)),
        "l1woT": f(TT(inputs["L1_attn_Wo"])),
        "wgate": f(np.asarray(inputs["L1_ffn_w_gate"]).reshape(24, 128, D)),
        "wup": f(np.asarray(inputs["L1_ffn_w_up"]).reshape(24, 128, D)),
        "wdown": f(np.asarray(inputs["L1_ffn_w_down"]).reshape(NIC, 128, 3072)),
    }
    per_core = []
    for c in range(NCORES):
        bi, hf = c // 2, c % 2
        own = x[bi, hf * TOK:(hf + 1) * TOK]
        oth = x[bi, (1 - hf) * TOK:(2 - hf) * TOK]
        xc = np.concatenate([own, oth], 0)
        d = dict(shared)
        d["xkv"] = f(xc.T.reshape(NIC, 128, S))
        d["pmask"] = f(np.tile(np.array([[1.0 - hf, float(hf)]], np.float32), (128, 1)))
        per_core.append(d)
    return per_core


_NC_CACHE = {}


def kernel(**inputs):
    per_core = _layout_inputs(inputs)
    if "nc" not in _NC_CACHE:
        _NC_CACHE["nc"] = build_program()
    nc = _NC_CACHE["nc"]
    res = run_bass_kernel_spmd(nc, per_core, core_ids=list(range(NCORES)))
    out = np.empty((B, S, D), np.float32)
    for c in range(NCORES):
        bi, hf = c // 2, c % 2
        o = res.results[c]["out"]
        out[bi, hf * TOK:(hf + 1) * TOK] = o.T
    return out


# version 10 (final)
# speedup vs baseline: 1.0157x; 1.0029x over previous
import numpy as np
import concourse.bass as bass
import concourse.mybir as mybir
from concourse.bass_utils import run_bass_kernel_spmd

F32 = mybir.dt.float32
BF16 = mybir.dt.bfloat16
AF = mybir.ActivationFunctionType
ALU = mybir.AluOpType
AX = mybir.AxisListType

DEBUG = False
B, S, D = 4, 4096, 1024
H, HD = 16, 64
NCORES = 8
TOK = 2048
NIC = 8
MAGIC = 12582912.0

ENGINES = ("tensor", "vector", "scalar", "gpsimd", "sync")

QUEUES = {
    "gd": ("gpsimd", 10),
    "pk": ("gpsimd", 1),
    "sd": ("sync", 10),
    "vt": ("sync", 4),
    "sp": ("sync", 1),
    "cd": ("scalar", 6),
}


class Builder:
    def __init__(self, nc):
        self.nc = nc
        self.ops = {e: [] for e in ENGINES}
        self.cnt = {}
        self.sems = {}
        for name in ("tm", "vv", "aa", "gg"):
            self.sems[name] = nc.alloc_semaphore(name=name)
            self.cnt[name] = 0
        self.qpool = {}
        self.qidx = {}
        for q, (eng, n) in QUEUES.items():
            self.qpool[q] = []
            self.qidx[q] = 0
            for i in range(n):
                nm = f"{q}{i}"
                self.sems[nm] = nc.alloc_semaphore(name=nm)
                self.cnt[nm] = 0
                self.qpool[q].append(nm)

    def emit(self, eng, fn=None, waits=(), inc=None):
        sems = self.sems
        merged = {}
        for w in waits:
            if w is None:
                continue
            s, v = w
            if v is None:
                continue
            merged[s] = max(merged.get(s, 0), v)
        wl = sorted(merged.items())
        if inc is not None:
            self.cnt[inc[0]] += inc[1]
        post = self.cnt[inc[0]] if inc is not None else None

        def run(e):
            for s, v in wl:
                e.wait_ge(sems[s], v)
            if fn is not None:
                instr = fn(e)
                if inc is not None:
                    instr.then_inc(sems[inc[0]], inc[1])
        self.ops[eng].append(run)
        return (inc[0], post) if inc is not None else None

    def dma(self, queue, out, in_, waits=(), transpose=False):
        eng = QUEUES[queue][0]
        pool = self.qpool[queue]
        nm = pool[self.qidx[queue] % len(pool)]
        self.qidx[queue] += 1
        w = list(waits)
        if self.cnt[nm] > 0:
            w.append((nm, self.cnt[nm]))
        return self.emit(eng, lambda e: e.dma_start(out=out, in_=in_, transpose=transpose),
                         waits=w, inc=(nm, 16))

    def qall(self, queue):
        return [(nm, self.cnt[nm]) for nm in self.qpool[queue] if self.cnt[nm] > 0]

    def finalize(self):
        nc = self.nc
        block_cm = nc.Block()
        block = block_cm.__enter__()
        for eng in ENGINES:
            opslist = self.ops[eng]
            getattr(block, eng)(lambda e, _l=opslist: [f(e) for f in _l])
        block_cm.__exit__(None, None, None)


class Guard:
    def __init__(self, n):
        self.t = [None] * n

    def wait(self, i):
        return self.t[i % len(self.t)]

    def all(self):
        return [t for t in self.t if t is not None]

    def set(self, i, tick):
        self.t[i % len(self.t)] = tick


def build_program():
    nc = bass.Bass("TRN2", target_bir_lowering=False, num_devices=NCORES)

    def inp(name, shape, dtype=F32):
        return nc.dram_tensor(name, shape, dtype, kind="ExternalInput").ap()

    xkv = inp("xkv", [NIC, 128, S])
    wqT = inp("wqT", [NIC, 128, D])
    wkT = inp("wkT", [NIC, 128, D])
    wvT = inp("wvT", [NIC, 128, D])
    woT = inp("woT", [NIC, 128, D])
    bq = inp("bq", [128, NIC])
    bk = inp("bk", [128, NIC])
    bv = inp("bv", [128, NIC])
    bo = inp("bo", [128, NIC])
    wqsT = inp("wqsT", [64, 3])
    wksT = inp("wksT", [64, 3])
    wvsT = inp("wvsT", [64, 3])
    bqs = inp("bqs", [3])
    bks = inp("bks", [3])
    bvs = inp("bvs", [3])
    temp = inp("temp", [1])
    w1T = inp("w1T", [32, NIC, 128, 128])
    w2T = inp("w2T", [32, NIC, 128, 128])
    h1T = inp("h1T", [32, 32, 128, 128])
    h2T = inp("h2T", [32, 32, 128, 128])
    w3T = inp("w3T", [NIC, 32, 128, 128])
    s1 = inp("s1", [128, 32])
    s2 = inp("s2", [128, 32])
    h1s = inp("h1s", [128, 32])
    h2s = inp("h2s", [128, 32])
    s3 = inp("s3", [128, NIC])
    l1wqT = inp("l1wqT", [NIC, 128, 64])
    l1wkT = inp("l1wkT", [NIC, 128, 64])
    l1wvT = inp("l1wvT", [NIC, 128, 64])
    l1woT = inp("l1woT", [64, D])
    wgate = inp("wgate", [24, 128, D])
    wup = inp("wup", [24, 128, D])
    wdown = inp("wdown", [NIC, 128, 3072])
    pmask = inp("pmask", [128, 2])

    out = nc.dram_tensor("out", [D, TOK], F32, kind="ExternalOutput").ap()
    dbg = {}
    dbg_ticks = []
    if DEBUG:
        for nm, shape, dt in (("d_kpair", [128, S], BF16), ("d_vpair", [128, S], BF16),
                              ("d_qpair", [128, TOK], BF16), ("d_stpK", [128, S], BF16),
                              ("d_stpQ", [128, TOK], BF16), ("d_attn", [H, 64, TOK], BF16),
                              ("d_y1", [NIC, 128, TOK], F32), ("d_y2", [NIC, 128, TOK], F32),
                              ("d_qf", [128, TOK], BF16), ("d_kf", [128, TOK], BF16),
                              ("d_vf", [128, TOK], BF16), ("d_num", [128, TOK], F32),
                              ("d_st", [128, 80], F32), ("d_outn", [128, TOK], BF16),
                              ("d_y3", [NIC, 128, TOK], F32),
                              ("d_y3p", [NIC, 128, 512], F32), ("d_sc", [128, 16], F32),
                              ("d_xq", [NIC, 128, TOK], BF16), ("d_gu", [2, 128, TOK], BF16),
                              ("d_wq", [NIC, 128, 128], BF16),
                              ("d_oq", [24, 128, 512], BF16), ("d_y4p", [NIC, 128, 512], F32),
                              ("d_ps", [128, 512], F32), ("d_y3ld", [NIC, 128, 512], F32)):
            dbg[nm] = nc.dram_tensor(nm, shape, dt, kind="ExternalOutput").ap()

    scr_row = nc.dram_tensor("scr_row", [8192], F32, kind="Internal").ap()
    scr_l = nc.dram_tensor("scr_l", [TOK], F32, kind="Internal").ap()
    y_park = nc.dram_tensor("y_park", [3, NIC, 128, TOK], F32, kind="Internal").ap()
    attn_park = nc.dram_tensor("attn_park", [H, 64, TOK], BF16, kind="Internal").ap()
    gu_park = nc.dram_tensor("gu_park", [2, 24, 128, TOK], BF16, kind="Internal").ap()
    hc_w1 = nc.dram_tensor("hc_w1", [32, NIC, 128, 128], BF16, kind="Internal").ap()
    hc_w2 = nc.dram_tensor("hc_w2", [32, NIC, 128, 128], BF16, kind="Internal").ap()
    hc_h1 = nc.dram_tensor("hc_h1", [32, 32, 128, 128], BF16, kind="Internal").ap()
    hc_h2 = nc.dram_tensor("hc_h2", [32, 32, 128, 128], BF16, kind="Internal").ap()
    hc_w3 = nc.dram_tensor("hc_w3", [NIC, 32, 128, 128], BF16, kind="Internal").ap()
    wq_park = nc.dram_tensor("wq_park", [3, 24, NIC * 3, 128, 128], BF16, kind="Internal").ap()
    wqn_park = nc.dram_tensor("wqn_park", [3, 24, 128, 3072], BF16, kind="Internal").ap()
    cc_in = [nc.dram_tensor(f"cc_in{i}", [4], F32, kind="Internal").ap() for i in range(4)]
    cc_out = [nc.dram_tensor(f"cc_out{i}", [4], F32, kind="Internal").ap() for i in range(4)]
    st_in = nc.dram_tensor("st_in", [64 * 80], F32, kind="Internal").ap()
    st_out = nc.dram_tensor("st_out", [64 * 80], F32, kind="Internal").ap()
    scr_pm = nc.dram_tensor("scr_pm", [128], F32, kind="Internal").ap()

    b = Builder(nc)
    A = nc.alloc_sbuf_tensor
    P = nc.alloc_psum_tensor

    AR0 = A("AR0", [128, 32768], BF16)
    STR = A("STR", [128, 16384], BF16)
    AR2 = A("AR2", [128, 16384], BF16)
    AR3 = A("AR3", [128, 8192], BF16)
    AR4 = A("AR4", [128, 16384], BF16)
    ARF = A("ARF", [128, 6144], F32)
    mcols = A("mcols", [128, 16], F32)
    mcb = A("mcb", [128, 128], BF16)
    mtr = A("mtr", [128, 128], BF16)
    smalls = A("smalls", [128, 256], F32)
    smallw = A("smallw", [128, 64], BF16)
    ident = A("ident", [128, 128], BF16)
    tri32 = A("tri32", [128, 128], F32)
    pmsk = A("pmsk", [128, 2], F32)
    onescol = A("onescol", [128, 1], BF16)

    res_h = nc.alloc_sbuf_tensor_at("res_f", [128, NIC, TOK], F32,
                                    offset=nc.lookup_mloc(AR0).addr)
    res_f = res_h.ap()

    psA = P("psA", [128, 2048], F32)
    psB = P("psB", [128, 2048], F32)

    xkv_b = AR0[:].rearrange("p (c t) -> p c t", c=NIC)
    stream_b = STR[:].rearrange("p (c t) -> p c t", c=NIC)
    wq_s = AR2[:, 0:8192].rearrange("p (c o) -> p c o", c=NIC)
    wk_s = AR2[:, 8192:16384].rearrange("p (c o) -> p c o", c=NIC)
    wv_s = AR3[:, 0:8192].rearrange("p (c o) -> p c o", c=NIC)
    kpair = AR4[:, 0:4096]
    vpair = AR4[:, 4096:8192]
    qpair = AR4[:, 8192:10240]
    stpK = AR4[:, 10240:14336]
    stpQ = AR4[:, 14336:16384]
    at_buf = STR[:, 0:2048].rearrange("p (h j t) -> p h j t", h=2, j=2)
    vtm = STR[:, 2048:7168].rearrange("p (k c) -> p k c", k=32)
    VB = 80
    kk6 = ARF[:, 0:2048]
    lrow = ARF[:, 2048:4096]
    avstage = ARF[:, 0:2048]
    statw = ARF[:]

    c_bq, c_bk, c_bv, c_bo = 0, 8, 16, 24
    c_tmp, c_bks6, c_bvs6, c_bqs6 = 32, 33, 34, 35
    c_wks, c_wvs, c_wqs = 40, 43, 46
    c_s1, c_s2, c_h1s, c_h2s, c_s3 = 64, 96, 128, 160, 192

    g_io = b.emit("gpsimd", lambda g: g.iota(ARF[:, 0:128], [[1, 128]], base=0,
                                             channel_multiplier=-1,
                                             allow_small_or_imprecise_dtypes=True),
                  inc=("gg", 1))
    id_done = b.emit("vector", lambda v: v.tensor_scalar(ident[:], ARF[:, 0:128],
                                                         0.0, None, ALU.is_equal),
                     waits=[g_io], inc=("vv", 1))
    tri_done = b.emit("vector", lambda v: v.tensor_scalar(tri32[:], ARF[:, 0:128],
                                                          0.0, None, ALU.is_ge),
                      waits=[g_io], inc=("vv", 1))
    pm_ld = b.dma("sp", pmsk[:], pmask[:])
    ones_done = b.emit("vector", lambda v: v.memset(onescol[:], 1.0), inc=("vv", 1))
    mcb_init = b.emit("vector", lambda v: v.memset(mcb[:], 0.0), inc=("vv", 1))

    ld_xkv = b.dma("gd", xkv_b[:, :, :], xkv.rearrange("c p t -> p c t"))
    ld_wq = b.dma("gd", wq_s, wqT.rearrange("c p o -> p c o"))
    ld_wk = b.dma("gd", wk_s, wkT.rearrange("c p o -> p c o"))
    ld_wv = b.dma("gd", wv_s, wvT.rearrange("c p o -> p c o"))

    ld_b = None
    for col, bsrc in ((c_bq, bq), (c_bk, bk), (c_bv, bv), (c_bo, bo)):
        ld_b = b.dma("sp", smalls[:, col:col + NIC], bsrc[:])
    ld_tmp = b.dma("sp", smalls[:, c_tmp:c_tmp + 1][:, None, :],
                   temp[None, :].partition_broadcast(128))
    ld_t = None
    for col, bsrc in ((c_wks, wksT), (c_wvs, wvsT), (c_wqs, wqsT)):
        ld_t = b.dma("sp", smalls[0:64, col:col + 3], bsrc[:])
        ld_t = b.dma("sp", smalls[64:128, col:col + 3], bsrc[:])
    for base in (0, 64):
        for cc, bsrc, p1 in ((c_bks6, bks, True), (c_bvs6, bvs, False)):
            pa, pb_ = ((1, 2, 0), (2, 0, 1)) if p1 else ((2, 0, 1), (1, 2, 0))
            for j in range(3):
                ld_t = b.dma("sp", smalls[base + j:base + j + 1, cc:cc + 1],
                             bsrc[pa[j]:pa[j] + 1][:, None])
                ld_t = b.dma("sp", smalls[base + 3 + j:base + 4 + j, cc:cc + 1],
                             bsrc[pb_[j]:pb_[j] + 1][:, None])
        ld_t = b.dma("sp", smalls[base + 0:base + 3, c_bqs6:c_bqs6 + 1], bqs[:, None])
        ld_t = b.dma("sp", smalls[base + 3:base + 6, c_bqs6:c_bqs6 + 1], bqs[:, None])
    c_stg = 60
    ld_t = b.dma("sp", smalls[32:35, c_stg:c_stg + 1], bqs[:, None])
    for col, ssrc in ((c_s1, s1), (c_s2, s2), (c_h1s, h1s), (c_h2s, h2s), (c_s3, s3)):
        ld_t = b.dma("sp", smalls[:, col:col + (NIC if col == c_s3 else 32)], ssrc[:])
    wait_tiny = ld_t
    negs = b.emit("vector", lambda v: v.tensor_scalar(smalls[32:35, c_stg:c_stg + 1],
                                                      smalls[32:35, c_stg:c_stg + 1],
                                                      -1.0, None, ALU.mult),
                  waits=[wait_tiny], inc=("vv", 1))
    neg1 = b.dma("sp", smalls[3:6, c_bqs6:c_bqs6 + 1], smalls[32:35, c_stg:c_stg + 1],
                 waits=[negs])
    neg2 = b.dma("sp", smalls[67:70, c_bqs6:c_bqs6 + 1], smalls[32:35, c_stg:c_stg + 1],
                 waits=[negs])

    extk = smallw[:, 0:12]
    extv = smallw[:, 12:24]
    extq = smallw[:, 24:36]
    zext = b.emit("vector", lambda v: v.memset(smallw[:, 0:40], 0.0), inc=("vv", 1))

    def build_ext(v):
        kp1 = (1, 2, 0)
        kp2 = (2, 0, 1)
        last = None
        for half, base in ((0, 0), (1, 64)):
            r = slice(base, base + 64)
            for j in range(3):
                co = half * 6
                last = v.tensor_copy(extk[r, co + j:co + j + 1],
                                     smalls[r, c_wks + kp1[j]:c_wks + kp1[j] + 1])
                last = v.tensor_copy(extk[r, co + 3 + j:co + 4 + j],
                                     smalls[r, c_wks + kp2[j]:c_wks + kp2[j] + 1])
                last = v.tensor_copy(extv[r, co + j:co + j + 1],
                                     smalls[r, c_wvs + kp2[j]:c_wvs + kp2[j] + 1])
                last = v.tensor_copy(extv[r, co + 3 + j:co + 4 + j],
                                     smalls[r, c_wvs + kp1[j]:c_wvs + kp1[j] + 1])
                last = v.tensor_copy(extq[r, co + j:co + j + 1],
                                     smalls[r, c_wqs + j:c_wqs + j + 1])
                last = v.tensor_scalar(extq[r, co + 3 + j:co + 4 + j],
                                       smalls[r, c_wqs + j:c_wqs + j + 1],
                                       -1.0, None, ALU.mult)
        return last
    ext_done = b.emit("vector", build_ext, waits=[wait_tiny, zext], inc=("vv", 1))

    stka = b.emit("vector", lambda v: v.memset(stpK[:], 0.0), inc=("vv", 1))
    stk0 = b.emit("vector", lambda v: v.memset(stpK[32:33, :], -1.0),
                  waits=[stka], inc=("vv", 1))
    b.dma("sp", stpK[6:7, :], stpK[32:33, :], waits=[stk0])
    stk_init = b.dma("sp", stpK[70:71, :], stpK[32:33, :], waits=[stk0])
    stq_init = b.emit("vector", lambda v: v.memset(stpQ[:], 0.0), inc=("vv", 1))
    vt0 = b.emit("vector", lambda v: v.memset(vtm[:], 0.0), inc=("vv", 1))
    vt1 = b.emit("vector", lambda v: v.memset(vtm[:, :, 64:65], 1.0),
                 waits=[vt0], inc=("vv", 1))
    vtm_init = b.emit("vector", lambda v: v.memset(vtm[:, :, VB + 64:VB + 65], 1.0),
                      waits=[vt1], inc=("vv", 1))

    pc_ticks = []
    pc_jobs = []
    for dst_t, src_t, n_oc in ((hc_w1, w1T, 32), (hc_w2, w2T, 32), (hc_h1, h1T, 32),
                               (hc_h2, h2T, 32), (hc_w3, w3T, NIC)):
        for o0 in range(0, n_oc, 8):
            pc_jobs.append((dst_t, src_t, o0))

    def emit_precast(n):
        while n > 0 and pc_jobs:
            dst_t, src_t, o0 = pc_jobs.pop(0)
            pc_ticks.append(b.dma("gd", dst_t[o0:o0 + 8], src_t[o0:o0 + 8]))
            n -= 1

    NT_S = S // 512
    NT_Q = TOK // 512
    gA = Guard(4)
    gB = Guard(4)
    av_guard = None
    vst_guard = None
    vtm_guard = None
    pair_sb_guard = None
    mc_guard = None
    c_done = None
    last_proj_tick = None

    for hp in range(8):
        emit_precast(3)
        hA, hB = 2 * hp, 2 * hp + 1
        kv_evac = q_evac = None
        for which, wsl, dst, bcol in ((0, wk_s, kpair, c_bk), (1, wv_s, vpair, c_bv),
                                      (2, wq_s, qpair, c_bq)):
            ntc = NT_S if which < 2 else NT_Q
            for tc in range(ntc):
                qi = tc % 4
                first_w = []
                if hp == 0 and which == 0 and tc == 0:
                    first_w = [ld_xkv, ld_wq, ld_wk, ld_wv]
                if hp > 0 and which == 0 and tc == 0:
                    first_w = [pair_sb_guard] + (dbg_ticks if hp == 1 else [])
                tick = None
                for ic in range(NIC):
                    tick = b.emit(
                        "tensor",
                        lambda t, wsl=wsl, ic=ic, hp=hp, tc=tc, qi=qi:
                        t.matmul(psA[:, 512 * qi:512 * qi + 512],
                                 wsl[:, ic, 128 * hp:128 * hp + 128],
                                 xkv_b[:, ic, 512 * tc:512 * tc + 512],
                                 start=(ic == 0), stop=(ic == NIC - 1)),
                        waits=(first_w + [gA.wait(qi)]) if ic == 0 else [],
                        inc=("tm", 1) if ic == NIC - 1 else None)
                last_proj_tick = tick
                if which < 2:
                    ev = b.emit("scalar",
                                lambda s, dst=dst, qi=qi, hp=hp, tc=tc, bcol=bcol:
                                s.activation(dst[:, 512 * tc:512 * tc + 512],
                                             psA[:, 512 * qi:512 * qi + 512],
                                             AF.Identity, bias=smalls[:, bcol + hp:bcol + hp + 1],
                                             scale=1.0),
                                waits=[tick, ld_b], inc=("aa", 1))
                    kv_evac = ev
                else:
                    ev = b.emit("vector",
                                lambda v, qi=qi, hp=hp, tc=tc, bcol=bcol:
                                v.tensor_scalar(qpair[:, 512 * tc:512 * tc + 512],
                                                psA[:, 512 * qi:512 * qi + 512],
                                                smalls[:, bcol + hp:bcol + hp + 1],
                                                smalls[:, c_tmp:c_tmp + 1],
                                                ALU.add, ALU.mult),
                                waits=[tick, ld_b, ld_tmp], inc=("vv", 1))
                    q_evac = ev
                gA.set(qi, ev)

        vt_done = None
        for kt in range(32):
            w0 = [kv_evac, vtm_init] if kt == 0 else []
            if kt == 0 and vtm_guard is not None:
                w0.append(vtm_guard)
            b.dma("vt", vtm[:, kt, 0:64], vpair[0:64, 128 * kt:128 * kt + 128],
                  waits=w0, transpose=True)
            vt_done = b.dma("vt", vtm[:, kt, VB:VB + 64], vpair[64:128, 128 * kt:128 * kt + 128],
                            transpose=True)

        stp_mul = None
        for sh in range(2):
            k_ev = {}
            vmul = {}
            for half, pbase in ((0, 0), (1, 64)):
                tp = (0, 64) if half == 1 else None
                for t4 in range(4):
                    tc = 4 * sh + t4
                    qi = t4
                    tick = b.emit(
                        "tensor",
                        lambda t, half=half, tc=tc, pbase=pbase, tp=tp, qi=qi:
                        t.matmul(psB[pbase:pbase + 6, 512 * qi:512 * qi + 512],
                                 extk[:, half * 6:half * 6 + 6],
                                 kpair[:, 512 * tc:512 * tc + 512],
                                 start=True, stop=True, tile_position=tp),
                        waits=([ext_done, kv_evac] if tc == 0 and sh == 0 else []) + [gB.wait(qi)]
                              + ([k_ev[(0, t4)]] if half == 1 else []),
                        inc=("tm", 1))
                    k_ev[(half, t4)] = b.emit(
                        "scalar",
                        lambda s, pbase=pbase, t4=t4, qi=qi:
                        s.activation(kk6[pbase:pbase + 6, 512 * t4:512 * t4 + 512],
                                     psB[pbase:pbase + 6, 512 * qi:512 * qi + 512],
                                     AF.Identity, bias=smalls[pbase:pbase + 6, c_bks6:c_bks6 + 1],
                                     scale=1.0),
                        waits=[tick, wait_tiny] +
                              ([stp_mul, vst_guard] if t4 == 0 and half == 0 else []),
                        inc=("aa", 1))
                    gB.set(qi, k_ev[(half, t4)])
            for half, pbase in ((0, 0), (1, 64)):
                tp = (0, 64) if half == 1 else None
                for t4 in range(4):
                    tc = 4 * sh + t4
                    qi = t4
                    tick = b.emit(
                        "tensor",
                        lambda t, half=half, tc=tc, pbase=pbase, tp=tp, qi=qi:
                        t.matmul(psB[pbase:pbase + 6, 512 * qi:512 * qi + 512],
                                 extv[:, half * 6:half * 6 + 6],
                                 vpair[:, 512 * tc:512 * tc + 512],
                                 start=True, stop=True, tile_position=tp),
                        waits=[gB.wait(qi)] + ([vmul[(0, t4)]] if half == 1 and (0, t4) in vmul else []),
                        inc=("tm", 1))
                    stp_mul = b.emit(
                        "vector",
                        lambda v, pbase=pbase, tc=tc, t4=t4, qi=qi:
                        v.scalar_tensor_tensor(stpK[pbase:pbase + 6, 512 * tc:512 * tc + 512],
                                               psB[pbase:pbase + 6, 512 * qi:512 * qi + 512],
                                               smalls[pbase:pbase + 6, c_bvs6:c_bvs6 + 1],
                                               kk6[pbase:pbase + 6, 512 * t4:512 * t4 + 512],
                                               ALU.add, ALU.mult),
                        waits=[tick, k_ev[(half, t4)], stk_init], inc=("vv", 1))
                    vmul[(half, t4)] = stp_mul
                    gB.set(qi, stp_mul)
        stq_ev = None
        qev = {}
        for half, pbase in ((0, 0), (1, 64)):
            tp = (0, 64) if half == 1 else None
            for tc in range(NT_Q):
                qi = tc % 4
                tick = b.emit(
                    "tensor",
                    lambda t, half=half, tc=tc, pbase=pbase, tp=tp, qi=qi:
                    t.matmul(psB[pbase:pbase + 6, 512 * qi:512 * qi + 512],
                             extq[:, half * 6:half * 6 + 6],
                             qpair[:, 512 * tc:512 * tc + 512],
                             start=True, stop=True, tile_position=tp),
                    waits=([q_evac, ext_done, stq_init] if tc == 0 and half == 0 else []) +
                          [gB.wait(qi)] + ([qev[(0, tc)]] if half == 1 else []),
                    inc=("tm", 1))
                stq_ev = b.emit(
                    "vector",
                    lambda v, pbase=pbase, tc=tc, qi=qi:
                    v.tensor_scalar(stpQ[pbase:pbase + 6, 512 * tc:512 * tc + 512],
                                    psB[pbase:pbase + 6, 512 * qi:512 * qi + 512],
                                    smalls[pbase:pbase + 6, c_bqs6:c_bqs6 + 1],
                                    smalls[pbase:pbase + 6, c_tmp:c_tmp + 1],
                                    ALU.add, ALU.mult),
                    waits=[tick, neg1, neg2, ld_tmp], inc=("vv", 1))
                qev[(half, tc)] = stq_ev
                gB.set(qi, stq_ev)

        ksub = STR[:, 7168:7424]
        ssub = STR[:, 7424:7680]
        kc1 = b.emit("vector", lambda v: v.tensor_copy(ksub[:, :], kpair[:, ::16]),
                     waits=[kv_evac] + ([c_done] if c_done else []), inc=("vv", 1))
        kc2 = b.emit("vector", lambda v: v.tensor_copy(ssub[:, :], stpK[:, ::16]),
                     waits=[stp_mul, kc1], inc=("vv", 1))
        for half, pbase in ((0, 0), (1, 64)):
            h = hA if half == 0 else hB
            red = None
            for qt in range(16):
                qi = qt % 4
                b.emit("tensor",
                       lambda t, pbase=pbase, qt=qt, qi=qi:
                       t.matmul(psB[:, 512 * qi:512 * qi + 256],
                                qpair[pbase:pbase + 64, 128 * qt:128 * qt + 128],
                                ksub[pbase:pbase + 64, :],
                                start=True, stop=False),
                       waits=([kc2, stq_ev] if qt == 0 and half == 0 else []) +
                             [gB.wait(qi)])
                tick = b.emit("tensor",
                              lambda t, pbase=pbase, qt=qt, qi=qi:
                              t.matmul(psB[:, 512 * qi:512 * qi + 256],
                                       stpQ[pbase:pbase + 6, 128 * qt:128 * qt + 128],
                                       ssub[pbase:pbase + 6, :],
                                       start=False, stop=True),
                              inc=("tm", 1))
                red = b.emit("vector",
                             lambda v, qt=qt, qi=qi:
                             v.tensor_reduce(mcols[:, qt:qt + 1],
                                             psB[:, 512 * qi:512 * qi + 256],
                                             AX.X, ALU.max),
                             waits=[tick] + ([mc_guard] if qt == 0 else []),
                             inc=("vv", 1))
                gB.set(qi, red)
            cst = b.emit("vector", lambda v: v.tensor_copy(mcb[:, 0:16], mcols[:, :]),
                         waits=[red, mcb_init] + ([mc_guard] if mc_guard else []),
                         inc=("vv", 1))
            tr = b.dma("vt", mtr[:, :], mcb[:, :],
                       waits=[cst] + ([c_done] if c_done else []), transpose=True)
            mc_guard = tr
            c_done = b.dma("sd", stpQ[pbase + 6:pbase + 7, :], mtr[0:16, :],
                           waits=[tr])

        if DEBUG and hp == 0:
            for nm, tens in (("d_kpair", kpair), ("d_vpair", vpair), ("d_qpair", qpair),
                             ("d_stpK", stpK), ("d_stpQ", stpQ)):
                dbg_ticks.append(b.dma("sd", dbg[nm][:], tens[:],
                                       waits=[c_done, stp_mul, stq_ev]))
        av_mm_last = {0: None, 1: None}
        exp_ticks = {0: None, 1: None}
        exp_hist = {}
        av_hist = {}
        first_qq = True
        for qq in range(4):
            q0 = 512 * qq
            for kt in range(33):
                if kt < 32:
                    jj = kt % 2
                    for half, pbase, sc0 in ((0, 0, 0), (1, 64, 1024)):
                        scol = sc0 + 512 * jj
                        b.emit("tensor",
                               lambda t, pbase=pbase, scol=scol, kt=kt, q0=q0:
                               t.matmul(psA[:, scol:scol + 512],
                                        kpair[pbase:pbase + 64, 128 * kt:128 * kt + 128],
                                        qpair[pbase:pbase + 64, q0:q0 + 512],
                                        start=True, stop=False),
                               waits=(([c_done] + b.qall("vt") + gA.all())
                                      if first_qq and kt == 0 and half == 0 else []) +
                                     ([exp_hist.get((half, kt - 2))] if kt >= 2 else
                                      ([exp_hist.get((half, 30 + kt)), exp_hist.get((half, 31 + kt))]
                                       if not first_qq else [])))
                    tick_h = {}
                    for half, pbase, sc0 in ((0, 0, 0), (1, 64, 1024)):
                        scol = sc0 + 512 * jj
                        tick_h[half] = b.emit(
                            "tensor",
                            lambda t, pbase=pbase, scol=scol, kt=kt, q0=q0:
                            t.matmul(psA[:, scol:scol + 512],
                                     stpK[pbase:pbase + 7, 128 * kt:128 * kt + 128],
                                     stpQ[pbase:pbase + 7, q0:q0 + 512],
                                     start=False, stop=True),
                            inc=("tm", 1))
                if kt >= 1:
                    for half in (0, 1):
                        av_mm_last[half] = b.emit(
                            "tensor",
                            lambda t, half=half, kt=kt:
                            t.matmul(psB[0:65, 512 * half:512 * half + 512],
                                     vtm[:, kt - 1, VB * half:VB * half + 65],
                                     at_buf[:, half, (kt - 1) % 2, :],
                                     start=(kt - 1 == 0), stop=(kt - 1 == 31)),
                            waits=[exp_hist[(half, kt - 1)]] +
                                  ([av_guard] if kt == 1 and av_guard else []),
                            inc=("tm", 1))
                        av_hist[(half, kt - 1)] = av_mm_last[half]
                if kt < 32:
                    for half, pbase, sc0 in ((0, 0, 0), (1, 64, 1024)):
                        scol = sc0 + 512 * jj
                        exp_hist[(half, kt)] = exp_ticks[half] = b.emit(
                            "scalar",
                            lambda s, half=half, scol=scol, kt=kt:
                            s.activation(at_buf[:, half, kt % 2, :],
                                         psA[:, scol:scol + 512], AF.Exp),
                            waits=[tick_h[half]] +
                                  ([av_hist.get((half, kt - 2))] if kt >= 2 else []),
                            inc=("aa", 1))
            first_qq = False
            av_done = av_mm_last[1]
            rc = b.emit("vector",
                        lambda v: v.reciprocal(lrow[64:65, 0:1024], psB[64:65, 0:1024]),
                        waits=[av_done, av_mm_last[0]], inc=("vv", 1))
            s1_ = b.dma("sd", scr_l[None, 0:1024], lrow[64:65, 0:1024], waits=[rc])
            s2_ = b.dma("sd", lrow[0:64, None, 0:1024],
                        scr_l[None, 0:1024].partition_broadcast(64),
                        waits=[s1_])
            e1 = b.emit("vector",
                        lambda v:
                        v.tensor_mul(avstage[0:64, 0:512],
                                     psB[0:64, 0:512], lrow[0:64, 0:512]),
                        waits=[s2_] + ([vst_guard] if vst_guard else []), inc=("vv", 1))
            e2 = b.emit("vector",
                        lambda v:
                        v.tensor_mul(avstage[0:64, 512:1024],
                                     psB[0:64, 512:1024], lrow[0:64, 512:1024]),
                        inc=("vv", 1))
            av_guard = e2
            b.dma("pk", attn_park[hA, :, q0:q0 + 512], avstage[0:64, 0:512],
                  waits=[e1])
            vst_guard = b.dma("pk", attn_park[hB, :, q0:q0 + 512],
                              avstage[0:64, 512:1024], waits=[e2])
        vtm_guard = av_done
        pair_sb_guard = av_done
        for i in range(4):
            gA.set(i, exp_ticks[1])
            gB.set(i, e2)

    emit_precast(99)
    ld_wo = b.dma("gd", wq_s, woT.rearrange("c p o -> p c o"), waits=[pair_sb_guard])
    ld_attn = b.dma("sd", stream_b[:, :, :],
                    attn_park.rearrange("(c w) p t -> (w p) c t", w=2),
                    waits=[vst_guard])
    ld_xf = b.dma("sd", res_f[:, :, :],
                  xkv.rearrange("c p t -> p c t")[:, :, 0:TOK],
                  waits=[last_proj_tick])
    y1_done = None
    for oc in range(NIC):
        for tc in range(NT_Q):
            qi = (oc * NT_Q + tc) % 4
            tick = None
            for ic in range(NIC):
                tick = b.emit(
                    "tensor",
                    lambda t, oc=oc, tc=tc, ic=ic, qi=qi:
                    t.matmul(psA[:, 512 * qi:512 * qi + 512],
                             wq_s[:, ic, 128 * oc:128 * oc + 128],
                             stream_b[:, ic, 512 * tc:512 * tc + 512],
                             start=(ic == 0), stop=(ic == NIC - 1)),
                    waits=(([ld_wo, ld_attn] if oc == 0 and tc == 0 else []) +
                           [gA.wait(qi)]) if ic == 0 else [],
                    inc=("tm", 1) if ic == NIC - 1 else None)
            y1_done = b.emit(
                "vector",
                lambda v, oc=oc, tc=tc, qi=qi:
                v.scalar_tensor_tensor(res_f[:, oc, 512 * tc:512 * tc + 512],
                                       psA[:, 512 * qi:512 * qi + 512],
                                       smalls[:, c_bo + oc:c_bo + oc + 1],
                                       res_f[:, oc, 512 * tc:512 * tc + 512],
                                       ALU.add, ALU.add),
                waits=[tick, ld_xf, ld_b], inc=("vv", 1))
            gA.set(qi, y1_done)

    def layernorm(src_f32, wait_prev, park_idx):
        sb = stream_b
        sqv = AR2[:].rearrange("p (c t) -> p c t", c=NIC)
        cp = b.emit("vector", lambda v: v.tensor_copy(sb[:, :, :], src_f32[:, :, :]),
                    waits=[wait_prev], inc=("vv", 1))
        sq = b.emit("vector", lambda v: v.tensor_mul(sqv[:, :, :], sb[:, :, :], sb[:, :, :]),
                    waits=[cp], inc=("vv", 1))
        st_done = None
        for stat_i, (srcb, ps) in enumerate(((sb, psA), (sqv, psB))):
            for nt in range(4):
                for ic in range(NIC):
                    st_done = b.emit(
                        "tensor",
                        lambda t, srcb=srcb, ps=ps, ic=ic, nt=nt:
                        t.matmul(ps[0:1, 512 * nt:512 * nt + 512], onescol[:],
                                 srcb[:, ic, 512 * nt:512 * nt + 512],
                                 start=(ic == 0), stop=(ic == NIC - 1)),
                        waits=[sq, ones_done] if ic == 0 and nt == 0 and stat_i == 0 else [],
                        inc=("tm", 1) if ic == NIC - 1 else None)
        Pv, Qv, Rv = statw[0:1, 0:2048], statw[0:1, 2048:4096], statw[0:1, 4096:6144]
        w = b.emit("vector", lambda v: v.tensor_scalar(Pv, psA[0:1, 0:2048],
                                                       1.0 / 1024, None, ALU.mult),
                   waits=[st_done], inc=("vv", 1))
        w = b.emit("vector", lambda v: v.tensor_scalar(Qv, psB[0:1, 0:2048],
                                                       1.0 / 1024, None, ALU.mult),
                   waits=[w], inc=("vv", 1))
        w = b.emit("vector", lambda v: v.tensor_mul(Rv, Pv, Pv), waits=[w], inc=("vv", 1))
        w = b.emit("vector", lambda v: v.tensor_sub(Qv, Qv, Rv), waits=[w], inc=("vv", 1))
        c4 = b.emit("vector", lambda v: v.tensor_scalar(Qv, Qv, 1e-5, None, ALU.add),
                    waits=[w], inc=("vv", 1))
        c5 = b.emit("scalar", lambda s: s.activation(Rv, Qv, AF.Sqrt),
                    waits=[c4], inc=("aa", 1))
        w = b.emit("vector", lambda v: v.reciprocal(Qv, Rv), waits=[c5], inc=("vv", 1))
        w = b.emit("vector", lambda v: v.tensor_mul(Pv, Pv, Qv), waits=[w], inc=("vv", 1))
        c8 = b.emit("vector", lambda v: v.tensor_scalar(Pv, Pv, -1.0, None, ALU.mult),
                    waits=[w], inc=("vv", 1))
        s1_ = b.dma("sd", scr_row[None, 0:2048], Qv, waits=[c8])
        s2_ = b.dma("sd", scr_row[None, 2048:4096], Pv)
        s3_ = b.dma("sd", statw[:, None, 0:4096],
                    scr_row[None, 0:4096].partition_broadcast(128), waits=[s1_, s2_])
        last = None
        for ic in range(NIC):
            t1 = b.emit("vector", lambda v, ic=ic: v.tensor_mul(
                src_f32[:, ic, :], src_f32[:, ic, :], statw[:, 0:2048]),
                waits=[s3_] if ic == 0 else [], inc=("vv", 1))
            t2 = b.emit("vector", lambda v, ic=ic: v.tensor_add(
                src_f32[:, ic, :], src_f32[:, ic, :], statw[:, 2048:4096]),
                waits=[t1], inc=("vv", 1))
            last = b.emit("vector", lambda v, ic=ic: v.tensor_copy(
                sb[:, ic, :], src_f32[:, ic, :]),
                waits=[t2], inc=("vv", 1))
        pk = b.dma("cd", y_park[park_idx].rearrange("c p t -> p c t"), src_f32[:, :, :],
                   waits=[last])
        return last, pk

    def ln_tile(ybuf, dst_b16, t0, w, wait_list, park_idx, tag):
        sb = dst_b16
        cpw = b.emit("vector", lambda v: v.tensor_copy(sb[:, :, t0:t0 + w], ybuf[:, :, :]),
                     waits=wait_list, inc=("vv", 1))
        sqw = b.emit("vector", lambda v: v.tensor_mul(
            Yv[:, 0:NIC, :], sb[:, :, t0:t0 + w], sb[:, :, t0:t0 + w]),
            waits=[cpw], inc=("vv", 1))
        st_done = None
        for stat_i, (srcb, ps) in enumerate(((sb[:, :, t0:t0 + w], psA), (Yv[:, 0:NIC, :], psB))):
            for ic in range(NIC):
                st_done = b.emit(
                    "tensor",
                    lambda t, srcb=srcb, ps=ps, ic=ic:
                    t.matmul(ps[0:1, 1536:1536 + w], onescol[:], srcb[:, ic, :],
                             start=(ic == 0), stop=(ic == NIC - 1)),
                    waits=[sqw, ones_done] + gA.all() if ic == 0 and stat_i == 0 else [],
                    inc=("tm", 1) if ic == NIC - 1 else None)
        Pv = ARF[0:1, 4096:4096 + w]
        Qv = ARF[0:1, 4608:4608 + w]
        Rv = ARF[0:1, 5120:5120 + w]
        wv_ = b.emit("vector", lambda v: v.tensor_scalar(Pv, psA[0:1, 1536:1536 + w],
                                                         1.0 / 1024, None, ALU.mult),
                     waits=[st_done], inc=("vv", 1))
        wv_ = b.emit("vector", lambda v: v.tensor_scalar(Qv, psB[0:1, 1536:1536 + w],
                                                         1.0 / 1024, None, ALU.mult),
                     waits=[wv_], inc=("vv", 1))
        wv_ = b.emit("vector", lambda v: v.tensor_mul(Rv, Pv, Pv), waits=[wv_], inc=("vv", 1))
        wv_ = b.emit("vector", lambda v: v.tensor_sub(Qv, Qv, Rv), waits=[wv_], inc=("vv", 1))
        wv_ = b.emit("vector", lambda v: v.tensor_scalar(Qv, Qv, 1e-5, None, ALU.add),
                     waits=[wv_], inc=("vv", 1))
        wv_ = b.emit("scalar", lambda s: s.activation(Rv, Qv, AF.Sqrt),
                     waits=[wv_], inc=("aa", 1))
        wv_ = b.emit("vector", lambda v: v.reciprocal(Qv, Rv), waits=[wv_], inc=("vv", 1))
        wv_ = b.emit("vector", lambda v: v.tensor_mul(Pv, Pv, Qv), waits=[wv_], inc=("vv", 1))
        wv_ = b.emit("vector", lambda v: v.tensor_scalar(Pv, Pv, -1.0, None, ALU.mult),
                     waits=[wv_], inc=("vv", 1))
        u1 = b.dma("sd", scr_row[None, 0:w], Qv, waits=[wv_])
        u2 = b.dma("sd", scr_row[None, w:2 * w], Pv, waits=[wv_])
        bcv = ARF[:, 5120:5120 + 2 * w]
        u3 = b.dma("sd", bcv[:, None, :], scr_row[None, 0:2 * w].partition_broadcast(128),
                   waits=[u1, u2])
        last = None
        for ic in range(NIC):
            a1 = b.emit("vector", lambda v, ic=ic: v.tensor_mul(
                ybuf[:, ic, :], ybuf[:, ic, :], bcv[:, 0:w]),
                waits=[u3] if ic == 0 else [], inc=("vv", 1))
            a2 = b.emit("vector", lambda v, ic=ic: v.tensor_add(
                ybuf[:, ic, :], ybuf[:, ic, :], bcv[:, w:2 * w]),
                waits=[a1], inc=("vv", 1))
            last = b.emit("vector", lambda v, ic=ic: v.tensor_copy(
                sb[:, ic, t0:t0 + w], ybuf[:, ic, :]),
                waits=[a2], inc=("vv", 1))
        pk = b.dma("cd", y_park[park_idx].rearrange("c p t -> p c t")[:, :, t0:t0 + w],
                   ybuf[:, :, :], waits=[last])
        for i in range(4):
            gA.set(i, last)
        return last, pk

    if DEBUG:
        dbg_ticks.append(b.dma("sd", dbg["d_attn"][:],
                               attn_park.rearrange("h p t -> h p t"),
                               waits=[vst_guard] + b.qall("pk")))
        dbg_ticks.append(b.dma("sd", dbg["d_y1"].rearrange("c p t -> p c t"),
                               res_f[:, :, :], waits=[y1_done]))
    ln1, y1_pk = layernorm(res_f, y1_done if not DEBUG else dbg_ticks[-1], 0)

    wraw = res_f[:, 4:6, :].rearrange("p c t -> p (c t)")
    mxs = wraw[:, 3072:3168]
    rss = wraw[:, 3168:3264]
    wqnat = AR0[:, 24576:27648]
    wqT_s = AR0[:, 27648:30720]
    qw_state = {"last": None, "tr": None}
    quant_jobs = []
    for wsl_, wi_, noc_, nic_ in ((wgate, 0, 24, NIC), (wup, 1, 24, NIC), (wdown, 2, NIC, 24)):
        for oc_ in range(noc_):
            quant_jobs.append((wsl_, wi_, oc_, nic_))

    def emit_quant(n):
        while n > 0 and quant_jobs:
            wsl, wi, oc, nic = quant_jobs.pop(0)
            qw_last = qw_state["last"]
            tr_prev2 = qw_state["tr"]
            I_ = nic * 128
            nb = I_ // 32
            first = (wi == 0 and oc == 0)
            lw = b.dma("gd", wraw[:, 0:I_], wsl[oc],
                       waits=([qw_last, y1_pk] if first else [qw_last]))
            q1 = b.emit("vector",
                        lambda v, I_=I_, nb=nb: v.tensor_reduce(
                            mxs[:, 0:nb].rearrange("p (b o) -> p b o", o=1),
                            wraw[:, 0:I_].rearrange("p (b j) -> p b j", j=32),
                            AX.X, ALU.max, apply_absolute_value=True),
                        waits=[lw], inc=("vv", 1))
            q2 = b.emit("vector",
                        lambda v, nb=nb: v.tensor_scalar(mxs[:, 0:nb], mxs[:, 0:nb],
                                                         1.0 / 7.0, 1e-12, ALU.mult, ALU.max),
                        waits=[q1], inc=("vv", 1))
            q3 = b.emit("vector",
                        lambda v, nb=nb: v.reciprocal(rss[:, 0:nb], mxs[:, 0:nb]),
                        waits=[q2], inc=("vv", 1))
            q4 = b.emit("vector",
                        lambda v, I_=I_, nb=nb: v.tensor_tensor(
                            wraw[:, 0:I_].rearrange("p (b j) -> p b j", j=32),
                            wraw[:, 0:I_].rearrange("p (b j) -> p b j", j=32),
                            rss[:, 0:nb][:, :, None].broadcast_to([128, nb, 32]),
                            ALU.mult),
                        waits=[q3], inc=("vv", 1))
            q5 = b.emit("vector",
                        lambda v, I_=I_: v.tensor_scalar(wraw[:, 0:I_], wraw[:, 0:I_],
                                                         MAGIC, MAGIC, ALU.add, ALU.subtract),
                        waits=[q4], inc=("vv", 1))
            q6 = b.emit("vector",
                        lambda v, I_=I_: v.tensor_scalar(wraw[:, 0:I_], wraw[:, 0:I_],
                                                         7.0, -8.0, ALU.min, ALU.max),
                        waits=[q5], inc=("vv", 1))
            q7 = b.emit("vector",
                        lambda v, I_=I_, nb=nb: v.tensor_tensor(
                            wqnat[:, 0:I_].rearrange("p (b j) -> p b j", j=32),
                            wraw[:, 0:I_].rearrange("p (b j) -> p b j", j=32),
                            mxs[:, 0:nb][:, :, None].broadcast_to([128, nb, 32]),
                            ALU.mult),
                        waits=[q6] + ([tr_prev2] if tr_prev2 else []), inc=("vv", 1))
            pk_n = b.dma("gd", wqn_park[wi, oc, :, 0:I_], wqnat[:, 0:I_],
                         waits=[q7])
            qw_state["tr"] = pk_n
            qw_state["last"] = pk_n
            n -= 1

    TT_F = 512
    NTT = TOK // TT_F
    Xv = AR0[:].rearrange("p (c t) -> p c t", t=TT_F)
    Yv = AR2[:].rearrange("p (c t) -> p c t", t=TT_F)
    wbuf = AR3[:].rearrange("p (j c o) -> p j c o", j=2, o=128)
    y2t = ARF[:, 0:4096].rearrange("p (c t) -> p c t", t=TT_F)

    def ffn_stage(tt, ws, n_oc, n_ic, rhs_fn, evac_fn, wait_extra, dma_q="gd"):
        nonlocal_ticks = []
        h = n_ic // 2
        for oc in range(n_oc):
            emit_quant(1)
            jb = oc % 2
            src = ws[oc].rearrange("c p o -> p c o")
            w_ex = (wait_extra + pc_ticks) if oc == 0 else []
            wta = b.dma("sd", wbuf[:, jb, 0:h, :], src[:, 0:h, :],
                        waits=[wgrd.wait(jb)] + w_ex)
            wtb = b.dma("cd", wbuf[:, jb, h:n_ic, :], src[:, h:n_ic, :],
                        waits=[wgrd.wait(jb)] + w_ex)
            qi = oc % 4
            tick = None
            for ic in range(n_ic):
                tick = b.emit(
                    "tensor",
                    lambda t, oc=oc, ic=ic, qi=qi, jb=jb, rhs=rhs_fn:
                    t.matmul(psA[:, 512 * qi:512 * qi + 512],
                             wbuf[:, jb, ic, :], rhs(ic),
                             start=(ic == 0), stop=(ic == n_ic - 1)),
                    waits=([wta, gA.wait(qi)] + (wait_extra if oc == 0 else []))
                    if ic == 0 else ([wtb] if ic == h else []),
                    inc=("tm", 1) if ic == n_ic - 1 else None)
            wgrd.set(jb, tick)
            ev = evac_fn(oc, qi, tick)
            gA.set(qi, ev)
            nonlocal_ticks.append(ev)
        return nonlocal_ticks

    wgrd = Guard(2)
    out_ticks = []
    ln2_prev = ln1
    pk_prev = None
    for tt in range(NTT):
        t0 = TT_F * tt
        y1t_ld = b.dma("cd", y2t[:, :, :],
                       y_park[0].rearrange("c p t -> p c t")[:, :, t0:t0 + TT_F],
                       waits=[y1_pk, ln2_prev, pk_prev])
        s1_ticks = []
        for oc in range(32):
            jb = oc % 2
            wt1 = b.dma("sd", wbuf[:, jb, 0:NIC, :], hc_w1[oc].rearrange("c p o -> p c o"),
                        waits=[wgrd.wait(jb)] +
                              (([ln1] + pc_ticks) if oc == 0 and tt == 0 else []))
            qi = (2 * oc) % 4
            tick = None
            for ic in range(NIC):
                tick = b.emit(
                    "tensor",
                    lambda t, oc=oc, ic=ic, qi=qi, jb=jb, t0=t0:
                    t.matmul(psA[:, 512 * qi:512 * qi + 512],
                             wbuf[:, jb, ic, :], stream_b[:, ic, t0:t0 + TT_F],
                             start=(ic == 0), stop=(ic == NIC - 1)),
                    waits=[wt1, gA.wait(qi)] if ic == 0 else [],
                    inc=("tm", 1) if ic == NIC - 1 else None)
            wgrd.set(jb, tick)
            e1 = b.emit("scalar",
                        lambda s, oc=oc, qi=qi:
                        s.activation(Xv[:, oc, :], psA[:, 512 * qi:512 * qi + 512],
                                     AF.Silu, bias=0.0,
                                     scale=smalls[:, c_s1 + oc:c_s1 + oc + 1]),
                        waits=[tick, wait_tiny], inc=("aa", 1))
            gA.set(qi, e1)
            jb2 = (oc + 1) % 2
            wt2 = b.dma("cd", wbuf[:, jb2, 0:NIC, :], hc_w2[oc].rearrange("c p o -> p c o"),
                        waits=[wgrd.wait(jb2)])
            qi2 = (2 * oc + 1) % 4
            tick = None
            for ic in range(NIC):
                tick = b.emit(
                    "tensor",
                    lambda t, oc=oc, ic=ic, qi2=qi2, jb2=jb2, t0=t0:
                    t.matmul(psA[:, 512 * qi2:512 * qi2 + 512],
                             wbuf[:, jb2, ic, :], stream_b[:, ic, t0:t0 + TT_F],
                             start=(ic == 0), stop=(ic == NIC - 1)),
                    waits=[wt2, gA.wait(qi2)] if ic == 0 else [],
                    inc=("tm", 1) if ic == NIC - 1 else None)
            wgrd.set(jb2, tick)
            mul = b.emit("vector",
                         lambda v, oc=oc, qi2=qi2:
                         v.scalar_tensor_tensor(Xv[:, oc, :],
                                                psA[:, 512 * qi2:512 * qi2 + 512],
                                                smalls[:, c_s2 + oc:c_s2 + oc + 1],
                                                Xv[:, oc, :],
                                                ALU.mult, ALU.mult),
                         waits=[tick, e1, wait_tiny], inc=("vv", 1))
            gA.set(qi2, mul)
            s1_ticks.append(mul)
        def rhs_X(ic):
            return Xv[:, ic, :]

        def rhs_Y(ic):
            return Yv[:, ic, :]

        def ev_gelu(dstv, scol):
            def f(oc, qi, tick):
                return b.emit("scalar",
                              lambda s, oc=oc, qi=qi:
                              s.activation(dstv[:, oc, :],
                                           psA[:, 512 * qi:512 * qi + 512],
                                           AF.Gelu, bias=0.0,
                                           scale=smalls[:, scol + oc:scol + oc + 1]),
                              waits=[tick, wait_tiny], inc=("aa", 1))
            return f
        s2_ticks = ffn_stage(tt, hc_h1, 32, 32, rhs_X, ev_gelu(Yv, c_h1s), s1_ticks)
        s3_ticks = ffn_stage(tt, hc_h2, 32, 32, rhs_Y, ev_gelu(Xv, c_h2s), s2_ticks)
        def ev_out(oc, qi, tick):
            return b.emit("vector",
                          lambda v, oc=oc, qi=qi:
                          v.scalar_tensor_tensor(y2t[:, oc, :],
                                                 psA[:, 512 * qi:512 * qi + 512],
                                                 smalls[:, c_s3 + oc:c_s3 + oc + 1],
                                                 y2t[:, oc, :],
                                                 ALU.mult, ALU.add),
                          waits=[tick, y1t_ld, wait_tiny], inc=("vv", 1))
        s4_ticks = ffn_stage(tt, hc_w3, NIC, 32, rhs_X, ev_out, s3_ticks)
        lnt, pk2 = ln_tile(y2t, stream_b, t0, TT_F, s4_ticks, 1, tt)
        outv = out.rearrange("(c p) t -> p c t", p=128)
        tap = b.dma("sd", outv[:, :, t0:t0 + TT_F], y2t[:, :, :], waits=[lnt])
        ln2_prev = tap
        pk_prev = pk2
        out_ticks.append(tap)
        out_ticks.append(pk2)

    emit_quant(999)
    qw_seed = qw_state["last"]
    tsw_prev = None
    for wsl_, wi_, noc_, nic_ in ((wgate, 0, 24, NIC), (wup, 1, 24, NIC), (wdown, 2, NIC, 24)):
        for oc_ in range(noc_):
            trs = []
            for ic_ in range(nic_):
                trs.append(b.dma("vt", wqT_s[:, 128 * ic_:128 * ic_ + 128],
                                 wqn_park[wi_, oc_, :, 128 * ic_:128 * ic_ + 128],
                                 waits=([qw_seed] if oc_ == 0 and wi_ == 0 else []) +
                                       ([tsw_prev] if tsw_prev else []),
                                 transpose=True))
            tsw_prev = b.dma("gd", wq_park[wi_, oc_, 0:nic_].rearrange("c p o -> p c o"),
                             wqT_s[:, 0:(128 * nic_)].rearrange("p (c o) -> p c o", o=128),
                             waits=trs)
    qw_state["last"] = tsw_prev

    l1w = AR3[:, 0:1536].rearrange("p (w c o) -> p w c o", w=3, o=64)
    l1wo = AR3[:, 1536:2560]
    kftm = AR3[:, 2560:3584].rearrange("p (k c) -> p k c", k=16)
    vtm2 = AR3[:, 3584:4864].rearrange("p (k c) -> p k c", k=16)
    srunb = AR3[:, 4864:4944]
    qf = AR0[:, 0:2048]
    kf = AR0[:, 2048:4096]
    vf = AR0[:, 4096:6144]
    at2 = AR0[:, 6144:6272]
    outn = AR0[:, 8192:10240]
    numbuf = ARF[:, 0:2048]
    stf = ARF[:, 2048:2128]
    prefx = ARF[:, 2128:2208]
    scr1 = ARF[:, 2304:2816]
    scr2 = ARF[:, 2816:3328]

    wl1_l = [b.dma("gd", l1w[:, 0, :, :], l1wqT.rearrange("c p o -> p c o"),
                   waits=[ln2_prev])]
    wl1_l.append(b.dma("gd", l1w[:, 1, :, :], l1wkT.rearrange("c p o -> p c o")))
    wl1_l.append(b.dma("gd", l1w[:, 2, :, :], l1wvT.rearrange("c p o -> p c o")))
    l1wov = l1wo[0:64, :].rearrange("p (c o) -> p c o", o=128)
    wlo = b.dma("gd", l1wov, l1woT.rearrange("p (c o) -> p c o", o=128), waits=[ln2_prev])

    proj_out = {}
    prev_ev = None
    for wi, dst, elu in ((0, qf, True), (1, kf, True), (2, vf, False)):
        for tc in range(4):
            qi = tc % 4
            tick = None
            for ic in range(NIC):
                tick = b.emit(
                    "tensor",
                    lambda t, wi=wi, ic=ic, tc=tc, qi=qi:
                    t.matmul(psA[0:64, 512 * qi:512 * qi + 512],
                             l1w[:, wi, ic, :], stream_b[:, ic, 512 * tc:512 * tc + 512],
                             start=(ic == 0), stop=(ic == NIC - 1)),
                    waits=(wl1_l + [gA.wait(qi), pk_prev]) if ic == 0 else [],
                    inc=("tm", 1) if ic == NIC - 1 else None)
            if elu:
                m1 = b.emit("vector",
                            lambda v, qi=qi: v.tensor_scalar(scr1[0:64, :],
                                                             psA[0:64, 512 * qi:512 * qi + 512],
                                                             0.0, None, ALU.min),
                            waits=[tick] + ([prev_ev] if prev_ev else []), inc=("vv", 1))
                m2 = b.emit("vector",
                            lambda v, qi=qi: v.tensor_scalar(scr2[0:64, :],
                                                             psA[0:64, 512 * qi:512 * qi + 512],
                                                             0.0, None, ALU.max),
                            waits=[m1], inc=("vv", 1))
                m3 = b.emit("scalar", lambda s: s.activation(scr1[0:64, :], scr1[0:64, :],
                                                             AF.Exp),
                            waits=[m2], inc=("aa", 1))
                ev = b.emit("vector",
                            lambda v, dst=dst, tc=tc: v.tensor_add(
                                dst[0:64, 512 * tc:512 * tc + 512],
                                scr1[0:64, :], scr2[0:64, :]),
                            waits=[m3], inc=("vv", 1))
            else:
                ev = b.emit("vector",
                            lambda v, dst=dst, tc=tc, qi=qi: v.tensor_copy(
                                dst[0:64, 512 * tc:512 * tc + 512],
                                psA[0:64, 512 * qi:512 * qi + 512]),
                            waits=[tick], inc=("vv", 1))
            gA.set(qi, ev)
            prev_ev = ev
            proj_out[(wi, tc)] = ev
    proj_done = [proj_out[(wi, tc)] for wi in range(3) for tc in range(4)]

    vo = b.emit("vector", lambda v: v.memset(vtm2[:], 0.0), waits=[ln2_prev], inc=("vv", 1))
    vo2 = b.emit("vector", lambda v: v.memset(vtm2[:, :, 64:65], 1.0), waits=[vo], inc=("vv", 1))
    ttk = None
    for ch in range(16):
        ttk = b.dma("vt", kftm[:, ch, :], kf[0:64, 128 * ch:128 * ch + 128],
                    waits=proj_done if ch == 0 else [], transpose=True)
        ttk = b.dma("vt", vtm2[:, ch, 0:64], vf[0:64, 128 * ch:128 * ch + 128],
                    waits=[vo2] if ch == 0 else [], transpose=True)
    tm_done = b.qall("vt")

    sttk = None
    for ch in range(16):
        sttk = b.emit("tensor",
                      lambda t, ch=ch:
                      t.matmul(psB[0:64, 1536:1601], kftm[:, ch, :], vtm2[:, ch, 0:65],
                               start=(ch == 0), stop=(ch == 15), skip_group_check=True),
                      waits=tm_done + gB.all() if ch == 0 else [],
                      inc=("tm", 1) if ch == 15 else None)
    exz = b.emit("vector", lambda v: v.memset(stf[0:64, 0:80], 0.0),
                 waits=[ln2_prev, pk_prev], inc=("vv", 1))
    exm = b.emit("vector", lambda v: v.tensor_scalar(stf[0:64, 0:65], psB[0:64, 1536:1601],
                                                     pmsk[0:64, 0:1], None, ALU.mult),
                 waits=[sttk, pm_ld, exz], inc=("vv", 1))
    exd = b.dma("pk", st_in.rearrange("(p c) -> p c", p=64),
                stf[0:64, 0:80], waits=[exm])
    cstate = b.emit("gpsimd",
                    lambda g: g.collective_compute(
                        "AllReduce", ALU.add, ins=[st_in[:]], outs=[st_out[:]],
                        replica_groups=[[0, 1], [2, 3], [4, 5], [6, 7]]),
                    waits=[exd], inc=("gg", 1))
    exr = b.dma("pk", prefx[0:64, 0:65], st_out.rearrange("(p c) -> p c", p=64)[:, 0:65],
                waits=[cstate])
    pfx = b.emit("vector", lambda v: v.tensor_scalar(prefx[0:64, 0:65], prefx[0:64, 0:65],
                                                     pmsk[0:64, 1:2], None, ALU.mult),
                 waits=[exr], inc=("vv", 1))
    srb0 = b.emit("vector", lambda v: v.tensor_copy(srunb[0:64, 0:65], prefx[0:64, 0:65]),
                  waits=[pfx], inc=("vv", 1))

    srb_tick = srb0
    num_ev = None
    for ch in range(16):
        atk = b.emit("tensor",
                     lambda t, ch=ch:
                     t.matmul(psB[:, 0:128], kf[0:64, 128 * ch:128 * ch + 128],
                              qf[0:64, 128 * ch:128 * ch + 128],
                              start=True, stop=True, skip_group_check=True),
                     waits=[num_ev] if num_ev else [],
                     inc=("tm", 1))
        msk = b.emit("vector",
                     lambda v: v.tensor_mul(at2[:, :], psB[:, 0:128], tri32[:]),
                     waits=[atk, tri_done], inc=("vv", 1))
        n1 = b.emit("tensor",
                    lambda t, ch=ch:
                    t.matmul(psB[0:65, 512:640], srunb[0:64, 0:65],
                             qf[0:64, 128 * ch:128 * ch + 128],
                             start=True, stop=False, skip_group_check=True),
                    waits=[srb_tick] + ([num_ev] if num_ev else []))
        n2 = b.emit("tensor",
                    lambda t, ch=ch:
                    t.matmul(psB[0:65, 512:640], vtm2[:, ch, 0:65], at2[:, :],
                             start=False, stop=True, skip_group_check=True),
                    waits=[msk], inc=("tm", 1))
        num_ev = b.emit("vector",
                        lambda v, ch=ch: v.tensor_copy(numbuf[0:65, 128 * ch:128 * ch + 128],
                                                       psB[0:65, 512:640]),
                        waits=[n2], inc=("vv", 1))
        if ch < 15:
            sup = b.emit("tensor",
                         lambda t, ch=ch:
                         t.matmul(psB[0:64, 1536:1601], kftm[:, ch, :], vtm2[:, ch, 0:65],
                                  start=(ch == 0), stop=True, skip_group_check=True),
                         waits=[exm] if ch == 0 else [], inc=("tm", 1))
            sev = b.emit("vector",
                         lambda v: v.scalar_tensor_tensor(stf[0:64, 0:65],
                                                          psB[0:64, 1536:1601], 1.0,
                                                          prefx[0:64, 0:65],
                                                          ALU.mult, ALU.add),
                         waits=[sup, pfx], inc=("vv", 1))
            srb_tick = b.emit("vector",
                              lambda v: v.tensor_copy(srunb[0:64, 0:65], stf[0:64, 0:65]),
                              waits=[sev], inc=("vv", 1))

    if DEBUG:
        dbg_ticks.append(b.dma("sd", dbg["d_qf"][:], AR0[:, 0:2048], waits=proj_done))
        dbg_ticks.append(b.dma("sd", dbg["d_kf"][:], AR0[:, 2048:4096], waits=proj_done))
        dbg_ticks.append(b.dma("sd", dbg["d_vf"][:], AR0[:, 4096:6144], waits=proj_done))
        dbg_ticks.append(b.dma("sd", dbg["d_num"][:], numbuf[:, :], waits=[num_ev]))
        dbg_ticks.append(b.dma("sd", dbg["d_st"][:], ARF[:, 2128:2208], waits=[pfx]))
        dbg_ticks.append(b.dma("sd", dbg["d_y2"].rearrange("c p t -> p c t"),
                               y_park[1].rearrange("c p t -> p c t"), waits=[pk_prev]))
    dn1 = b.emit("vector", lambda v: v.tensor_scalar(numbuf[64:65, :], numbuf[64:65, :],
                                                     1e-6, None, ALU.add),
                 waits=[num_ev], inc=("vv", 1))
    dn2 = b.emit("vector", lambda v: v.reciprocal(numbuf[64:65, :], numbuf[64:65, :]),
                 waits=[dn1], inc=("vv", 1))
    dn3 = b.dma("sd", scr_l[None, :], numbuf[64:65, :], waits=[dn2])
    dn4 = b.dma("sd", lrow[0:64, None, :], scr_l[None, :].partition_broadcast(64),
                waits=[dn3])
    dnm = b.emit("vector", lambda v: v.tensor_mul(outn[0:64, :], numbuf[0:64, :],
                                                  lrow[0:64, :]),
                 waits=[dn4], inc=("vv", 1))

    y3t = ARF[:, 0:4096].rearrange("p (c t) -> p c t", t=TT_F)
    ln3_prev = None
    pk3_prev = None
    for tt in range(NTT):
        t0 = TT_F * tt
        y2ld = b.dma("cd", y3t[:, :, :],
                     y_park[1].rearrange("c p t -> p c t")[:, :, t0:t0 + TT_F],
                     waits=[pk_prev, ln2_prev, dnm] +
                           ([ln3_prev, pk3_prev] if ln3_prev else []) + dbg_ticks)
        s4t = []
        for oc in range(NIC):
            qi = oc % 4
            tick = None
            tick = b.emit("tensor",
                          lambda t, oc=oc, qi=qi, t0=t0:
                          t.matmul(psA[:, 512 * qi:512 * qi + 512],
                                   l1wov[:, oc, :],
                                   outn[0:64, t0:t0 + TT_F],
                                   start=True, stop=True),
                          waits=[wlo, dnm, gA.wait(qi)],
                          inc=("tm", 1))
            ev = b.emit("vector",
                        lambda v, oc=oc, qi=qi:
                        v.scalar_tensor_tensor(y3t[:, oc, :],
                                               psA[:, 512 * qi:512 * qi + 512], 1.0,
                                               y3t[:, oc, :], ALU.mult, ALU.add),
                        waits=[tick, y2ld], inc=("vv", 1))
            gA.set(qi, ev)
            s4t.append(ev)
        if DEBUG and tt == 0:
            s4t = s4t + [b.dma("sd", dbg["d_y3p"].rearrange("c p t -> p c t"),
                               y3t[:, :, :], waits=s4t)]
            dbg_ticks.append(s4t[-1])
        ln3_prev, pk3_prev = ln_tile(y3t, stream_b, t0, TT_F, s4t, 2, 10 + tt)

    if DEBUG:
        dbg_ticks.append(b.dma("sd", dbg["d_outn"][:], AR0[:, 8192:10240], waits=[dnm]))
        dbg_ticks.append(b.dma("sd", dbg["d_y3"].rearrange("c p t -> p c t"),
                               y_park[2].rearrange("c p t -> p c t"), waits=[pk3_prev]))
    gm_prev = [None]
    def global_max(local_col, idx, n, wait_list):
        g1 = b.dma("sd", scr_pm[None, :], local_col[:, 0:1],
                   waits=wait_list + ([gm_prev[0]] if gm_prev[0] else []))
        g2 = b.dma("sd", ARF[0:1, 6016:6144], scr_pm[None, :], waits=[g1])
        g3 = b.emit("vector", lambda v: v.tensor_reduce(ARF[0:1, 6012:6013],
                                                        ARF[0:1, 6016:6144],
                                                        AX.X, ALU.max),
                    waits=[g2], inc=("vv", 1))
        g3b = b.emit("vector", lambda v: v.tensor_copy(
            ARF[0:1, 6008:6012], ARF[0:1, 6012:6013].broadcast_to([1, 4])),
            waits=[g3], inc=("vv", 1))
        g4 = b.dma("pk", cc_in[idx][None, 0:4], ARF[0:1, 6008:6012], waits=[g3b])
        g5 = b.emit("gpsimd",
                    lambda g: g.collective_compute(
                        "AllReduce", ALU.max, ins=[cc_in[idx][:]], outs=[cc_out[idx][:]],
                        replica_groups=[list(range(8))]),
                    waits=[g4], inc=("gg", 1))
        g6 = b.dma("pk", ARF[0:1, 6013:6014], cc_out[idx][None, 0:1], waits=[g5])
        g7 = b.emit("vector", lambda v: v.tensor_scalar(ARF[0:1, 6014:6015],
                                                        ARF[0:1, 6013:6014],
                                                        1.0 / 32767.0, 1e-12,
                                                        ALU.mult, ALU.max),
                    waits=[g6], inc=("vv", 1))
        g8 = b.emit("vector", lambda v: v.reciprocal(ARF[0:1, 6015:6016],
                                                     ARF[0:1, 6014:6015]),
                    waits=[g7], inc=("vv", 1))
        g9 = b.dma("sd", scr_pm[None, 0:2], ARF[0:1, 6014:6016], waits=[g8])
        col = 208 + 2 * idx
        g10 = b.dma("sd", smalls[:, col:col + 2][:, None, :],
                    scr_pm[None, 0:2].partition_broadcast(128), waits=[g9])
        gm_prev[0] = g10
        return g10, smalls[:, col + 1:col + 2], smalls[:, col:col + 1]

    def qdq3(dst_ap, src_ap, inv_ap, sc_ap, wait_list, scr=None, out_guard=None):
        sq_ = scr if scr is not None else scr_q
        a1 = b.emit("vector", lambda v: v.tensor_scalar(sq_[:], src_ap, inv_ap, MAGIC,
                                                        ALU.mult, ALU.add),
                    waits=wait_list, inc=("vv", 1))
        a2 = b.emit("vector", lambda v: v.tensor_scalar(sq_[:], sq_[:], MAGIC, 32767.0,
                                                        ALU.subtract, ALU.min),
                    waits=[a1], inc=("vv", 1))
        a3 = b.emit("vector", lambda v: v.tensor_scalar(dst_ap, sq_[:], -32768.0, sc_ap,
                                                        ALU.max, ALU.mult),
                    waits=[a2] + ([out_guard] if out_guard else []), inc=("vv", 1))
        return a3

    emit_quant(999)
    qw_last = qw_state["last"]
    lm = None
    xmax_ticks = []
    for tt in range(NTT):
        t0 = TT_F * tt
        yld = b.dma("cd", y3t[:, :, :],
                    y_park[2].rearrange("c p t -> p c t")[:, :, t0:t0 + TT_F],
                    waits=[pk3_prev, qw_last] if tt == 0 else [lm])
        r1 = b.emit("vector", lambda v: v.tensor_reduce(
            ARF[:, 4096 + 0:4096 + 1].rearrange("p a -> p a 1")[:, :, 0] if False
            else ARF[:, 4100:4101],
            y3t[:, :, :].rearrange("p c t -> p (c t)"), AX.X, ALU.max,
            apply_absolute_value=True),
            waits=[yld], inc=("vv", 1))
        lm = b.emit("vector", lambda v, tt=tt: v.tensor_tensor(
            ARF[:, 4101:4102], ARF[:, 4101:4102], ARF[:, 4100:4101], ALU.max)
            if tt > 0 else v.tensor_copy(ARF[:, 4101:4102], ARF[:, 4100:4101]),
            waits=[r1], inc=("vv", 1))
        xmax_ticks.append(lm)
    gx, inv_x, sc_x = global_max(ARF[:, 4101:4102], 0, 1, [lm])
    scr_q = ARF[:, 4608:5120]
    xq_ticks = []
    xq_prev = None
    for tt in range(NTT):
        t0 = TT_F * tt
        yld = b.dma("cd", y3t[:, :, :],
                    y_park[2].rearrange("c p t -> p c t")[:, :, t0:t0 + TT_F],
                    waits=[gx] if tt == 0 else [xq_prev])
        for oc in range(NIC):
            xq_prev = qdq3(stream_b[:, oc, t0:t0 + TT_F], y3t[:, oc, :], inv_x, sc_x,
                           [yld, gx] + ([xq_prev] if xq_prev else []))
        xq_ticks.append(xq_prev)

    wbuf2 = AR3[:].rearrange("p (j c o) -> p j c o", j=2, o=128)
    gstages = [AR0[:, 512 * j:512 * j + 512] for j in range(4)]
    gmx = [ARF[:, 4096 + j:4097 + j] for j in range(4)]
    gumax = ARF[:, 4102:4104]
    gmz = b.emit("vector", lambda v: v.memset(ARF[:, 4096:4104], 0.0),
                 waits=[dnm], inc=("vv", 1))
    gu_ticks = []
    wgrd2 = Guard(2)
    sgrds = Guard(4)
    it = 0
    for wi, wpk in ((0, 0), (1, 1)):
        for tt in range(NTT):
            t0 = TT_F * tt
            for oc in range(24):
                jb = it % 2
                sj = it % 4
                wt = b.dma("sd" if oc % 2 == 0 else "cd", wbuf2[:, jb, 0:NIC, :],
                           wq_park[wi, oc, 0:NIC].rearrange("c p o -> p c o"),
                           waits=[wgrd2.wait(jb)] + ([qw_last] + xq_ticks if oc == 0 and tt == 0 else []))
                qi = it % 4
                tick = None
                for ic in range(NIC):
                    tick = b.emit(
                        "tensor",
                        lambda t, oc=oc, ic=ic, qi=qi, jb=jb, t0=t0:
                        t.matmul(psA[:, 512 * qi:512 * qi + 512],
                                 wbuf2[:, jb, ic, :], stream_b[:, ic, t0:t0 + TT_F],
                                 start=(ic == 0), stop=(ic == NIC - 1)),
                        waits=([wt, gA.wait(qi)] + (xq_ticks if oc == 0 and tt == 0 else []))
                        if ic == 0 else [],
                        inc=("tm", 1) if ic == NIC - 1 else None)
                wgrd2.set(jb, tick)
                ev = b.emit("scalar",
                            lambda s, qi=qi, sj=sj:
                            s.activation(gstages[sj][:], psA[:, 512 * qi:512 * qi + 512],
                                         AF.Copy),
                            waits=[tick] + ([sgrds.wait(sj)] if sgrds.wait(sj) else []),
                            inc=("aa", 1))
                gA.set(qi, ev)
                mx1 = b.emit("vector",
                             lambda v, sj=sj: v.tensor_reduce(gmx[sj], gstages[sj][:],
                                                              AX.X, ALU.max,
                                                              apply_absolute_value=True),
                             waits=[ev, gmz], inc=("vv", 1))
                mx2 = b.emit("vector",
                             lambda v, wi=wi, sj=sj: v.tensor_tensor(gumax[:, wi:wi + 1],
                                                                     gumax[:, wi:wi + 1],
                                                                     gmx[sj], ALU.max),
                             waits=[mx1], inc=("vv", 1))
                pkq = "pk" if it % 2 == 0 else "gd"
                sg = b.dma(pkq, gu_park[wpk, oc, :, t0:t0 + TT_F], gstages[sj][:],
                           waits=[ev, mx1])
                sgrds.set(sj, sg)
                gu_ticks.append(sg)
                it += 1
    gg1, inv_up, sc_up = global_max(gumax[:, 1:2], 1, 1, [mx2] + gu_ticks[-1:])
    gg2, inv_gt, sc_gt = global_max(gumax[:, 0:1], 2, 1, [mx2])

    g2max = ARF[:, 4104:4105]
    gz2 = b.emit("vector", lambda v: v.memset(ARF[:, 4104:4112], 0.0), inc=("vv", 1))
    SETS = []
    SETS.append(dict(tq1=AR0[:, 1024:1536], tq2=ARF[:, 5120:5632],
                     tq3=ARF[:, 5632:6144], scr=ARF[:, 4608:5120],
                     mx=ARF[:, 4105:4106]))
    SETS.append(dict(tq1=AR0[:, 1536:2048], tq2=res_f[:, 4, 0:512],
                     tq3=res_f[:, 4, 512:1024], scr=res_f[:, 4, 1024:1536],
                     mx=ARF[:, 4106:4107]))
    g2_ticks = []
    sguard = Guard(2)
    it2 = 0
    mx2b = None
    for tt in range(NTT):
        t0 = TT_F * tt
        for oc in range(24):
            S_ = SETS[it2 % 2]
            tq1, tq2, tq3 = S_["tq1"], S_["tq2"], S_["tq3"]
            rld = b.dma("sd" if it2 % 2 == 0 else "cd", tq1[:],
                        gu_park[0, oc, :, t0:t0 + TT_F],
                        waits=([gg2] + gu_ticks if it2 < 2 else []) +
                              ([sguard.wait(it2)] if sguard.wait(it2) else []))
            gq = qdq3(tq2[:], tq1[:], inv_gt, sc_gt, [rld, gg2, gz2], scr=S_["scr"])
            sg = b.emit("scalar", lambda s, tq2=tq2, tq3=tq3:
                        s.activation(tq3[:], tq2[:], AF.Sigmoid),
                        waits=[gq], inc=("aa", 1))
            f1 = b.emit("vector", lambda v, tq3=tq3: v.tensor_scalar(tq3[:], tq3[:], 65536.0, MAGIC,
                                                                     ALU.mult, ALU.add),
                        waits=[sg], inc=("vv", 1))
            f2 = b.emit("vector", lambda v, tq3=tq3: v.tensor_scalar(tq3[:], tq3[:], MAGIC, 65535.0,
                                                                     ALU.subtract, ALU.min),
                        waits=[f1], inc=("vv", 1))
            f3 = b.emit("vector", lambda v, tq3=tq3: v.tensor_scalar(tq3[:], tq3[:], 0.0,
                                                                     1.0 / 65536.0,
                                                                     ALU.max, ALU.mult),
                        waits=[f2], inc=("vv", 1))
            g2r = b.emit("vector", lambda v, tq2=tq2, tq3=tq3: v.tensor_mul(tq2[:], tq2[:], tq3[:]),
                         waits=[f3], inc=("vv", 1))
            mx1 = b.emit("vector",
                         lambda v, tq2=tq2, S_=S_: v.tensor_reduce(S_["mx"], tq2[:],
                                                                   AX.X, ALU.max,
                                                                   apply_absolute_value=True),
                         waits=[g2r], inc=("vv", 1))
            mx2b = b.emit("vector",
                          lambda v, S_=S_: v.tensor_tensor(g2max[:], g2max[:],
                                                           S_["mx"], ALU.max),
                          waits=[mx1], inc=("vv", 1))
            cstg = b.emit("vector", lambda v, tq1=tq1, tq2=tq2: v.tensor_copy(tq1[:], tq2[:]),
                          waits=[g2r], inc=("vv", 1))
            pk_ = b.dma("pk" if it2 % 2 == 0 else "gd",
                        gu_park[0, oc, :, t0:t0 + TT_F], tq1[:], waits=[cstg])
            sguard.set(it2, pk_)
            g2_ticks.append(pk_)
            it2 += 1
    gg3, inv_g2, sc_g2 = global_max(g2max[:], 3, 1, [mx2b] + g2_ticks[-1:])

    omax = ARF[:, 4107:4108]
    oz = b.emit("vector", lambda v: v.memset(omax[:], 0.0), inc=("vv", 1))
    up_b = [AR0[:, 2048:2560], AR0[:, 2560:3072]]
    o_ticks = []
    oguard = Guard(2)
    it3 = 0
    mo2 = None
    for tt in range(NTT):
        t0 = TT_F * tt
        for oc in range(24):
            S_ = SETS[it3 % 2]
            tq1, tq2, tq3 = S_["tq1"], S_["tq2"], S_["tq3"]
            ub = up_b[it3 % 2]
            rl1 = b.dma("sd" if it3 % 2 == 0 else "cd", tq1[:],
                        gu_park[0, oc, :, t0:t0 + TT_F],
                        waits=([gg3] + g2_ticks if it3 < 2 else []) +
                              ([oguard.wait(it3)] if oguard.wait(it3) else []))
            rl2 = b.dma("sd" if it3 % 2 == 0 else "cd", ub[:],
                        gu_park[1, oc, :, t0:t0 + TT_F],
                        waits=([gg3] if it3 < 2 else []) +
                              ([oguard.wait(it3)] if oguard.wait(it3) else []))
            g2q = qdq3(tq2[:], tq1[:], inv_g2, sc_g2, [rl1, gg3, oz], scr=S_["scr"])
            upq = qdq3(tq3[:], ub[:], inv_up, sc_up, [rl2, gg1, g2q], scr=S_["scr"])
            orw = b.emit("vector", lambda v, tq2=tq2, tq3=tq3: v.tensor_mul(tq2[:], tq2[:], tq3[:]),
                         waits=[g2q, upq], inc=("vv", 1))
            mo1 = b.emit("vector",
                         lambda v, tq2=tq2, S_=S_: v.tensor_reduce(S_["mx"], tq2[:],
                                                                   AX.X, ALU.max,
                                                                   apply_absolute_value=True),
                         waits=[orw], inc=("vv", 1))
            mo2 = b.emit("vector",
                         lambda v, S_=S_: v.tensor_tensor(omax[:], omax[:],
                                                          S_["mx"], ALU.max),
                         waits=[mo1], inc=("vv", 1))
            cst2 = b.emit("vector", lambda v, tq1=tq1, tq2=tq2: v.tensor_copy(tq1[:], tq2[:]),
                          waits=[orw], inc=("vv", 1))
            pk_ = b.dma("pk" if it3 % 2 == 0 else "gd",
                        gu_park[0, oc, :, t0:t0 + TT_F], tq1[:], waits=[cst2])
            oguard.set(it3, pk_)
            o_ticks.append(pk_)
            it3 += 1
    gg4, inv_o, sc_o = global_max(omax[:], 0, 1, [mo2] + o_ticks[-1:])

    if DEBUG:
        dbg_ticks.append(b.dma("sd", dbg["d_sc"][:], smalls[:, 208:224], waits=[gg4]))
        dbg_ticks.append(b.dma("sd", dbg["d_xq"].rearrange("c p t -> p c t"),
                               stream_b[:, :, :], waits=xq_ticks))
        dbg_ticks.append(b.dma("sd", dbg["d_gu"][0], gu_park[0, 0, :, :], waits=[rp]))
        dbg_ticks.append(b.dma("sd", dbg["d_gu"][1], gu_park[1, 0, :, :], waits=[rp]))
        dbg_ticks.append(b.dma("sd", dbg["d_wq"].rearrange("c p o -> p c o"),
                               wq_park[2, 6, 2:2 + NIC].rearrange("c p o -> p c o"),
                               waits=[qw_last]))
    oq = AR0[:, 20480:32768].rearrange("p (c t) -> p c t", t=TT_F)
    y4t = ARF[:, 0:4096].rearrange("p (c t) -> p c t", t=TT_F)
    outv = out.rearrange("(c p) t -> p c t", p=128)
    fin_ticks = []
    ln4_prev = None
    pk4_prev = None
    rp2 = None
    for tt in range(NTT):
        t0 = TT_F * tt
        oq_ticks = []
        for oc in range(24):
            S_ = SETS[oc % 2]
            tq1 = S_["tq1"]
            rl = b.dma("sd" if oc % 2 == 0 else "cd", tq1[:],
                       gu_park[0, oc, :, t0:t0 + TT_F],
                       waits=([gg4] + o_ticks if oc < 2 and tt == 0 else [])
                       + ([oq_ticks[-2]] if len(oq_ticks) >= 2 else
                          ([rp2] if rp2 else [])))
            qt = qdq3(oq[:, oc, :], tq1[:], inv_o, sc_o,
                      [rl, gg4] + ([ln4_prev] if ln4_prev else []), scr=S_["scr"])
            oq_ticks.append(qt)
        rp2 = oq_ticks[-1]
        y3ld = b.dma("cd", y4t[:, :, :],
                     y_park[2].rearrange("c p t -> p c t")[:, :, t0:t0 + TT_F],
                     waits=([ln4_prev, pk4_prev] if ln4_prev else g2_ticks[-2:]))
        s4t = []
        for oc in range(NIC):
            jb = (oc + tt) % 2
            wt = b.dma("sd", wbuf2[:, jb, 0:24, :],
                       wq_park[2, oc, 0:24].rearrange("c p o -> p c o"),
                       waits=[wgrd2.wait(jb)])
            qi = oc % 4
            tick = None
            for ic in range(24):
                tick = b.emit(
                    "tensor",
                    lambda t, oc=oc, ic=ic, qi=qi, jb=jb:
                    t.matmul(psA[:, 512 * qi:512 * qi + 512],
                             wbuf2[:, jb, ic, :], oq[:, ic, :],
                             start=(ic == 0), stop=(ic == 23)),
                    waits=([wt, gA.wait(qi)] + oq_ticks) if ic == 0 else [],
                    inc=("tm", 1) if ic == 23 else None)
            wgrd2.set(jb, tick)
            ev = b.emit("vector",
                        lambda v, oc=oc, qi=qi:
                        v.scalar_tensor_tensor(y4t[:, oc, :],
                                               psA[:, 512 * qi:512 * qi + 512], 1.0,
                                               y4t[:, oc, :], ALU.mult, ALU.add),
                        waits=[tick, y3ld], inc=("vv", 1))
            gA.set(qi, ev)
            s4t.append(ev)
        if DEBUG and tt == 0:
            dbg_ticks.append(b.dma("sd", dbg["d_oq"].rearrange("c p t -> p c t"),
                                   oq[:, :, :], waits=oq_ticks))
            s4t = s4t + [b.dma("sd", dbg["d_y4p"].rearrange("c p t -> p c t"),
                               y4t[:, :, :], waits=s4t)]
            dbg_ticks.append(s4t[-1])
        ln4_prev, pk4_prev = ln_tile(y4t, stream_b, t0, TT_F, s4t, 0, 20 + tt)
        fin = b.dma("sd", outv[:, :, t0:t0 + TT_F], y4t[:, :, :], waits=[ln4_prev])
        fin_ticks.append(fin)
        ln4_prev = fin

    b.emit("sync", None, waits=fin_ticks + dbg_ticks + b.qall("cd") + b.qall("pk"))

    b.finalize()
    return nc


def _layout_inputs(inputs):
    f = lambda a: np.ascontiguousarray(np.asarray(a), np.float32)
    x = np.asarray(inputs["L0_attn_x"])
    TT = lambda w: np.asarray(w).T
    wq, wk, wv, wo = (np.asarray(inputs["L0_attn_W" + k]) for k in "qkvo")
    shared = {
        "wqT": f(wq.T.reshape(NIC, 128, D)),
        "wkT": f(wk.T.reshape(NIC, 128, D)),
        "wvT": f(wv.T.reshape(NIC, 128, D)),
        "woT": f(wo.T.reshape(NIC, 128, D)),
        "bq": f(np.asarray(inputs["L0_attn_bq"]).reshape(NIC, 128).T),
        "bk": f(np.asarray(inputs["L0_attn_bk"]).reshape(NIC, 128).T),
        "bv": f(np.asarray(inputs["L0_attn_bv"]).reshape(NIC, 128).T),
        "bo": f(np.asarray(inputs["L0_attn_bo"]).reshape(NIC, 128).T),
        "wqsT": f(TT(inputs["L0_attn_Wqs"])),
        "wksT": f(TT(inputs["L0_attn_Wks"])),
        "wvsT": f(TT(inputs["L0_attn_Wvs"])),
        "bqs": f(inputs["L0_attn_bqs"]),
        "bks": f(inputs["L0_attn_bks"]),
        "bvs": f(inputs["L0_attn_bvs"]),
        "temp": f(np.asarray(inputs["L0_attn_temp"]).reshape(1)),
        "w1T": f(TT(inputs["L0_ffn_w1_w"]).reshape(NIC, 128, 32, 128).transpose(2, 0, 1, 3)),
        "w2T": f(TT(inputs["L0_ffn_w2_w"]).reshape(NIC, 128, 32, 128).transpose(2, 0, 1, 3)),
        "h1T": f(TT(inputs["L0_ffn_h1_w"]).reshape(32, 128, 32, 128).transpose(2, 0, 1, 3)),
        "h2T": f(TT(inputs["L0_ffn_h2_w"]).reshape(32, 128, 32, 128).transpose(2, 0, 1, 3)),
        "w3T": f(TT(inputs["L0_ffn_w3_w"]).reshape(32, 128, NIC, 128).transpose(2, 0, 1, 3)),
        "s1": f(np.asarray(inputs["L0_ffn_w1_s"]).reshape(32, 128).T),
        "s2": f(np.asarray(inputs["L0_ffn_w2_s"]).reshape(32, 128).T),
        "h1s": f(np.asarray(inputs["L0_ffn_h1_s"]).reshape(32, 128).T),
        "h2s": f(np.asarray(inputs["L0_ffn_h2_s"]).reshape(32, 128).T),
        "s3": f(np.asarray(inputs["L0_ffn_w3_s"]).reshape(NIC, 128).T),
        "l1wqT": f(TT(inputs["L1_attn_Wq"]).reshape(NIC, 128, 64)),
        "l1wkT": f(TT(inputs["L1_attn_Wk"]).reshape(NIC, 128, 64)),
        "l1wvT": f(TT(inputs["L1_attn_Wv"]).reshape(NIC, 128, 64)),
        "l1woT": f(TT(inputs["L1_attn_Wo"])),
        "wgate": f(np.asarray(inputs["L1_ffn_w_gate"]).reshape(24, 128, D)),
        "wup": f(np.asarray(inputs["L1_ffn_w_up"]).reshape(24, 128, D)),
        "wdown": f(np.asarray(inputs["L1_ffn_w_down"]).reshape(NIC, 128, 3072)),
    }
    per_core = []
    for c in range(NCORES):
        bi, hf = c // 2, c % 2
        own = x[bi, hf * TOK:(hf + 1) * TOK]
        oth = x[bi, (1 - hf) * TOK:(2 - hf) * TOK]
        xc = np.concatenate([own, oth], 0)
        d = dict(shared)
        d["xkv"] = f(xc.T.reshape(NIC, 128, S))
        d["pmask"] = f(np.tile(np.array([[1.0 - hf, float(hf)]], np.float32), (128, 1)))
        per_core.append(d)
    return per_core


_NC_CACHE = {}


def kernel(**inputs):
    per_core = _layout_inputs(inputs)
    if "nc" not in _NC_CACHE:
        _NC_CACHE["nc"] = build_program()
    nc = _NC_CACHE["nc"]
    res = run_bass_kernel_spmd(nc, per_core, core_ids=list(range(NCORES)))
    out = np.empty((B, S, D), np.float32)
    for c in range(NCORES):
        bi, hf = c // 2, c % 2
        o = res.results[c]["out"]
        out[bi, hf * TOK:(hf + 1) * TOK] = o.T
    return out
